# version 11
# baseline (speedup 1.0000x reference)
import os as _os
import contextlib
import numpy as np
import concourse.bass as bass
import concourse.mybir as mybir
from concourse.bass_utils import run_bass_kernel_spmd

F32 = mybir.dt.float32
F16 = mybir.dt.float16
AF = mybir.ActivationFunctionType
ALU = mybir.AluOpType
AX = mybir.AxisListType

T = 4096
D = 1024
NP = 8
QS = 512
NQS = T // QS
NKC = 32
NTT = T // 128
LN64 = float(np.log(64.0))


class Ctr:
    def __init__(self, sem):
        self.sem = sem
        self.v = 0

    def inc(self, inst, n=1):
        inst.then_inc(self.sem, n)
        self.v += n
        return self.v


def _final(nc, waits):
    with nc.Block() as blk:
        @blk.sync
        def _(s):
            for sem, v in waits:
                if v > 0:
                    s.wait_ge(sem, v)


def build_nc(phases=99):
    nc = bass.Bass("TRN2", target_bir_lowering=False)

    x_d = nc.dram_tensor("x", [T, D], F32, kind="ExternalInput").ap()
    wqT_d = nc.dram_tensor("wqT", [D, D], F32, kind="ExternalInput").ap()
    wkT_d = nc.dram_tensor("wkT", [D, D], F32, kind="ExternalInput").ap()
    wvT_d = nc.dram_tensor("wvT", [D, D], F32, kind="ExternalInput").ap()
    woT_d = nc.dram_tensor("woT", [D, D], F32, kind="ExternalInput").ap()
    bqk_d = nc.dram_tensor("bqkT", [128, 16], F32, kind="ExternalInput").ap()
    bo_d = nc.dram_tensor("bo_row", [1, D], F32, kind="ExternalInput").ap()
    fb_d = nc.dram_tensor("fb_row", [1, D], F32, kind="ExternalInput").ap()
    bv_d = nc.dram_tensor("bv_row", [1, D], F32, kind="ExternalInput").ap()
    wr_d = nc.dram_tensor("wr", [128, 8, 8, 128], F32, kind="ExternalInput").ap()
    wi_d = nc.dram_tensor("wi", [128, 8, 8, 128], F32, kind="ExternalInput").ap()
    c_d = nc.dram_tensor("cmat", [D, D], F16, kind="ExternalInput").ap()
    s_d = nc.dram_tensor("smat", [D, D], F16, kind="ExternalInput").ap()
    ident_d = nc.dram_tensor("ident", [128, 128], F16, kind="ExternalInput").ap()
    e2_d = nc.dram_tensor("e2", [128, 128], F16, kind="ExternalInput").ap()
    eb_d = nc.dram_tensor("eb", [128, 128], F16, kind="ExternalInput").ap()
    zsel_d = nc.dram_tensor("zsel", [128, 4], F16, kind="ExternalInput").ap()

    out_d = nc.dram_tensor("out", [T, D], F32, kind="ExternalOutput").ap()

    q_d = nc.dram_tensor("q_scr", [NP, 128, T], F16, kind="Internal").ap()
    v_d = nc.dram_tensor("v_scr", [T, D], F16, kind="Internal").ap()
    mt_d = nc.dram_tensor("mt_scr", [D, D], F16, kind="Internal").ap()
    gr_d = nc.dram_tensor("gr_scr", [D, D], F32, kind="Internal").ap()
    gip_d = nc.dram_tensor("gip_scr", [D, D], F32, kind="Internal").ap()
    part_d = nc.dram_tensor("part_scr", [T, D], F32, kind="Internal").ap()

    bigA_h = nc.alloc_sbuf_tensor("bigA", [128, NP * T], F16)
    bigB_h = nc.alloc_sbuf_tensor("bigB", [128, NP * T], F16)
    a16_h = nc.alloc_sbuf_tensor("a16", [128, 12288], F16)
    a32_h = nc.alloc_sbuf_tensor("a32", [128, 5120], F32)

    XNT = bigA_h.ap()
    VQ = XNT
    KT = bigB_h.ap()
    CSR = KT
    GrS = KT[:, 16384:24576]
    GipS = KT[:, 24576:32768]
    A16 = a16_h.ap()
    A32 = a32_h.ap()

    ident = nc.alloc_sbuf_tensor("identt", [128, 128], F16).ap()
    e2 = nc.alloc_sbuf_tensor("e2t", [128, 128], F16).ap()
    eb = nc.alloc_sbuf_tensor("ebt", [128, 128], F16).ap()
    zsel = nc.alloc_sbuf_tensor("zselt", [128, 4], F16).ap()
    bqk = nc.alloc_sbuf_tensor("bqkt", [128, 16], F32).ap()
    bvz = nc.alloc_sbuf_tensor("bvzt", [128, D], F16).ap()
    bz = nc.alloc_sbuf_tensor("bzt", [128, D], F16).ap()
    zpad = nc.alloc_sbuf_tensor("zpadt", [128, QS], F16).ap()
    zf32 = nc.alloc_sbuf_tensor("zf32t", [2, QS], F32).ap()
    rcpt = nc.alloc_sbuf_tensor("rcptt", [2, QS], F32).ap()
    sclt = nc.alloc_sbuf_tensor("scltt", [128, QS], F32).ap()
    wo_sb = nc.alloc_sbuf_tensor("wo_sbt", [128, 2048], F16).ap()
    qstage = nc.alloc_sbuf_tensor("qstaget", [128, 1024], F16).ap()
    vstage = nc.alloc_sbuf_tensor("vstaget", [128, 2048], F16).ap()
    xstage = nc.alloc_sbuf_tensor("xstaget", [128, 1024], F16).ap()
    brow0 = nc.alloc_sbuf_tensor("brow0t", [1, D], F32).ap()
    brow1 = nc.alloc_sbuf_tensor("brow1t", [1, D], F32).ap()
    lnbias = nc.alloc_sbuf_tensor("lnbiast", [128, 1], F32).ap()
    epsb = nc.alloc_sbuf_tensor("epsbt", [128, 1], F32).ap()
    st_s1 = nc.alloc_sbuf_tensor("st_s1t", [128, 1], F32).ap()
    st_s2 = nc.alloc_sbuf_tensor("st_s2t", [128, 1], F32).ap()
    st_mean = nc.alloc_sbuf_tensor("st_meant", [128, 1], F32).ap()
    st_var = nc.alloc_sbuf_tensor("st_vart", [128, 1], F32).ap()
    st_msq = nc.alloc_sbuf_tensor("st_msqt", [128, 1], F32).ap()
    st_lnv = nc.alloc_sbuf_tensor("st_lnvt", [128, 1], F32).ap()
    st_rstd = nc.alloc_sbuf_tensor("st_rstdt", [128, 1], F32).ap()

    x0 = A32[:, 0:1024]
    x1 = A32[:, 1024:2048]
    xsq = A32[:, 2048:3072]
    ev32 = [A32[:, i * 512:(i + 1) * 512] for i in range(4)]
    pstg = [A32[:, 3072 + i * 1024:3072 + (i + 1) * 1024] for i in range(2)]

    QTs = A16[:, 0:4096]
    pt2_h = nc.alloc_sbuf_tensor("pt2t", [128, 1024], F16).ap()
    PT = [A16[:, 4096:5120], A16[:, 5120:6144], pt2_h, xstage]
    PS = [A16[:, 6144:7168], A16[:, 7168:8192]]
    OTr = A16[:, 8192:12288]

    with contextlib.ExitStack() as _stack:
        semTE = _stack.enter_context(nc.semaphore("sTE"))
        semAC = _stack.enter_context(nc.semaphore("sAC"))
        semDV = _stack.enter_context(nc.semaphore("sDV"))
        semIN = _stack.enter_context(nc.semaphore("sIN"))
        semWT = _stack.enter_context(nc.semaphore("sWT"))
        semST = _stack.enter_context(nc.semaphore("sST"))
        semL0 = _stack.enter_context(nc.semaphore("sL0"))
        semL1 = _stack.enter_context(nc.semaphore("sL1"))
        semL2 = _stack.enter_context(nc.semaphore("sL2"))
        semL3 = _stack.enter_context(nc.semaphore("sL3"))
        semX0 = _stack.enter_context(nc.semaphore("sX0"))
        semX1 = _stack.enter_context(nc.semaphore("sX1"))
        semQS0 = _stack.enter_context(nc.semaphore("sQS0"))
        semQS1 = _stack.enter_context(nc.semaphore("sQS1"))
        semVS0 = _stack.enter_context(nc.semaphore("sVS0"))
        semVS1 = _stack.enter_context(nc.semaphore("sVS1"))
        semVC0 = _stack.enter_context(nc.semaphore("sVC0"))
        semVC1 = _stack.enter_context(nc.semaphore("sVC1"))
        semVC2 = _stack.enter_context(nc.semaphore("sVC2"))
        semVC3 = _stack.enter_context(nc.semaphore("sVC3"))
        semQT = _stack.enter_context(nc.semaphore("sQT"))
        semWO0 = _stack.enter_context(nc.semaphore("sWO0"))
        semWO1 = _stack.enter_context(nc.semaphore("sWO1"))
        semPA0 = _stack.enter_context(nc.semaphore("sPA0"))
        semPA1 = _stack.enter_context(nc.semaphore("sPA1"))
        semOU0 = _stack.enter_context(nc.semaphore("sOU0"))
        semOU1 = _stack.enter_context(nc.semaphore("sOU1"))
        TE, AC, DV = Ctr(semTE), Ctr(semAC), Ctr(semDV)
        IN = Ctr(semIN)
        WT = Ctr(semWT)
        ST = Ctr(semST)
        L = [Ctr(semL0), Ctr(semL1), Ctr(semL2), Ctr(semL3)]
        semL = [semL0, semL1, semL2, semL3]
        sX = [Ctr(semX0), Ctr(semX1)]
        semX = [semX0, semX1]
        QSm = [Ctr(semQS0), Ctr(semQS1)]
        semQS = [semQS0, semQS1]
        VS = [Ctr(semVS0), Ctr(semVS1)]
        semVS = [semVS0, semVS1]
        VC = [Ctr(semVC0), Ctr(semVC1), Ctr(semVC2), Ctr(semVC3)]
        semVC = [semVC0, semVC1, semVC2, semVC3]
        QT = Ctr(semQT)
        WO = [Ctr(semWO0), Ctr(semWO1)]
        semWO = [semWO0, semWO1]
        PA = [Ctr(semPA0), Ctr(semPA1)]
        semPA = [semPA0, semPA1]
        OU = [Ctr(semOU0), Ctr(semOU1)]
        semOU = [semOU0, semOU1]

        with nc.Block() as blk:
            @blk.vector
            def _(v):
                DV.inc(v.memset(bvz, 0.0))
                DV.inc(v.memset(bz, 0.0))
                DV.inc(v.memset(zpad, 0.0))
                DV.inc(v.memset(lnbias, -LN64))
                DV.inc(v.memset(epsb, 1e-5))

            @blk.gpsimd
            def _(g):
                for dst, src in [
                    (ident, ident_d), (e2, e2_d), (eb, eb_d), (zsel, zsel_d),
                    (bqk, bqk_d), (brow0, bo_d), (brow1, fb_d),
                ]:
                    IN.inc(g.dma_start(out=dst, in_=src), 16)
                for k in range(8):
                    IN.inc(g.dma_start(out=CSR[:, k * 1024:(k + 1) * 1024],
                                       in_=c_d[k * 128:(k + 1) * 128, :]), 16)
                    IN.inc(g.dma_start(
                        out=CSR[:, 8192 + k * 1024:8192 + (k + 1) * 1024],
                        in_=s_d[k * 128:(k + 1) * 128, :]), 16)
                g.wait_ge(semDV, 3)
                IN.inc(g.dma_start(out=bvz[0:1, :], in_=bv_d), 16)

        in_done = IN.v
        assert in_done == 24 * 16

        with nc.Block() as blk:
            @blk.vector
            def _(v):
                v.wait_ge(semIN, in_done)
                DV.inc(v.tensor_add(brow0, brow0, brow1))
                v.wait_ge(semDV, DV.v)
                DV.inc(v.tensor_copy(bz[0:1, :], brow0))
        dv_bz = DV.v

        NDV, NAC, NT1 = 15, 3, 8
        mx1a = {}
        _sxv = [c.v for c in sX]
        for t in range(NTT):
            _sxv[t % 2] += 16
            mx1a[t] = _sxv[t % 2]
        mL1 = {}
        _lv = [c.v for c in L]
        for a in range(8):
            _lv[a % 2] += 32
            mL1[a] = _lv[a % 2]
        dvP = {}
        _dvv = DV.v
        for t in range(NTT):
            _dvv += NDV
            dvP[("tile", t)] = _dvv
            if t % 4 == 3:
                a = t // 4
                _dvv += 5
                dvP[("p0ev", a)] = _dvv
        teP = {}

        with (
            nc.psum_tensor("tp0", [128, 128], F16) as tp0,
            nc.psum_tensor("tp1", [128, 128], F16) as tp1,
            nc.psum_tensor("pgr", [128, 1024], F32) as pgr,
            nc.psum_tensor("pga", [128, 1024], F32) as pga,
            nc.psum_tensor("pgb", [128, 1024], F32) as pgb,
            nc.Block() as blk,
        ):
            tps = [tp0, tp1]
            dvB, acB, teB = DV.v, AC.v, TE.v

            def dv_tile_base(t):
                return dvP[("tile", t)] - NDV

            @blk.tensor
            def _(te):
                te.wait_ge(semIN, in_done)
                for t in range(NTT):
                    te.wait_ge(semDV, dv_tile_base(t) + 7)
                    for j in range(8):
                        tj = t * 8 + j
                        if tj >= 2:
                            tp_, jp = (tj - 2) // 8, (tj - 2) % 8
                            te.wait_ge(semDV, dv_tile_base(tp_) + 7 + jp + 1)
                        TE.inc(te.transpose(
                            tps[j % 2][:],
                            xstage[:, j * 128:(j + 1) * 128], ident))
                    teP[("tr", t)] = TE.v
                    if t % 4 == 3:
                        a = t // 4
                        if a >= 1:
                            te.wait_ge(semDV, dvP[("p0ev", a - 1)])
                        te.wait_ge(semL[a % 2], mL1[a])
                        for k in range(8):
                            base = (a % 2) * 2048
                            wrs = A16[:, base + k * 128:base + (k + 1) * 128]
                            wis = A16[:, base + 1024 + k * 128:
                                      base + 1024 + (k + 1) * 128]
                            C = [CSR[:, k * 1024 + h * 512:
                                     k * 1024 + (h + 1) * 512]
                                 for h in range(2)]
                            S = [CSR[:, 8192 + k * 1024 + h * 512:
                                     8192 + k * 1024 + (h + 1) * 512]
                                 for h in range(2)]
                            st, sp = (k == 0), (k == 7)
                            te.matmul(pgr[:, 0:512], wrs, C[0], start=st,
                                      stop=False)
                            te.matmul(pgr[:, 512:1024], wrs, C[1], start=st,
                                      stop=False)
                            te.matmul(pga[:, 0:512], wrs, S[0], start=st,
                                      stop=sp)
                            te.matmul(pga[:, 512:1024], wrs, S[1], start=st,
                                      stop=sp)
                            te.matmul(pgr[:, 0:512], wis, S[0], start=False,
                                      stop=sp)
                            te.matmul(pgr[:, 512:1024], wis, S[1], start=False,
                                      stop=sp)
                            te.matmul(pgb[:, 0:512], wis, C[0], start=st,
                                      stop=sp)
                            TE.inc(te.matmul(pgb[:, 512:1024], wis, C[1],
                                             start=st, stop=sp))
                        teP[("p0", a)] = TE.v

            @blk.scalar
            def _(sc):
                for t in range(NTT):
                    xs = x0 if t % 2 == 0 else x1
                    sc.wait_ge(semX[t % 2], mx1a[t])
                    if t >= 1:
                        sc.wait_ge(semDV, dv_tile_base(t - 1) + 2)
                    AC.inc(sc.activation(xsq, xs, AF.Square))
                    sc.wait_ge(semDV, dv_tile_base(t) + 6)
                    AC.inc(sc.activation(st_lnv, st_var, AF.Ln, bias=epsb))
                    sc.wait_ge(semAC, AC.v)
                    AC.inc(sc.activation(st_rstd, st_lnv, AF.Exp, scale=-0.5))

            @blk.vector
            def _(v):
                for t in range(NTT):
                    xs = x0 if t % 2 == 0 else x1
                    b = dv_tile_base(t)
                    v.wait_ge(semX[t % 2], mx1a[t])
                    DV.inc(v.reduce_sum(st_s1, xs, axis=AX.X))
                    v.wait_ge(semAC, acB + t * NAC + 1)
                    DV.inc(v.reduce_sum(st_s2, xsq, axis=AX.X))
                    v.wait_ge(semDV, DV.v)
                    DV.inc(v.tensor_scalar_mul(st_mean, st_s1, 1.0 / D))
                    v.wait_ge(semDV, DV.v)
                    DV.inc(v.tensor_scalar_mul(st_var, st_s2, 1.0 / D))
                    v.wait_ge(semDV, DV.v)
                    DV.inc(v.tensor_mul(st_msq, st_mean, st_mean))
                    v.wait_ge(semDV, DV.v)
                    DV.inc(v.tensor_sub(st_var, st_var, st_msq))
                    v.wait_ge(semAC, acB + (t + 1) * NAC)
                    if t >= 1:
                        v.wait_ge(semTE, teP[("tr", t - 1)])
                    DV.inc(v.tensor_scalar(xstage, xs, st_mean, st_rstd,
                                           ALU.subtract, ALU.mult))
                    for j in range(8):
                        v.wait_ge(semTE, teP[("tr", t)] - 8 + j + 1)
                        DV.inc(v.tensor_copy(
                            XNT[:, j * T + t * 128:j * T + (t + 1) * 128],
                            tps[j % 2][:]))
                    assert DV.v == dvP[("tile", t)]
                    if t % 4 == 3:
                        a = t // 4
                        v.wait_ge(semTE, teP[("p0", a)])
                        DV.inc(v.tensor_copy(GrS[:, a * 1024:a * 1024 + 512],
                                             pgr[:, 0:512]))
                        DV.inc(v.tensor_copy(
                            GrS[:, a * 1024 + 512:(a + 1) * 1024],
                            pgr[:, 512:1024]))
                        DV.inc(v.tensor_copy(pstg[0], pga[:]))
                        v.wait_ge(semDV, DV.v)
                        DV.inc(v.tensor_sub(GipS[:, a * 1024:a * 1024 + 512],
                                            pstg[0][:, 0:512], pgb[:, 0:512]))
                        DV.inc(v.tensor_sub(
                            GipS[:, a * 1024 + 512:(a + 1) * 1024],
                            pstg[0][:, 512:1024], pgb[:, 512:1024]))
                        assert DV.v == dvP[("p0ev", a)]

            @blk.sync
            def _(s):
                for t in range(NTT):
                    if t >= 2:
                        s.wait_ge(semDV, dv_tile_base(t - 2) + 7)
                    sX[t % 2].inc(s.dma_start(
                        out=(x0 if t % 2 == 0 else x1),
                        in_=x_d[t * 128:(t + 1) * 128, :]), 16)
                    assert sX[t % 2].v == mx1a[t]

            @blk.gpsimd
            def _(g):
                g.wait_ge(semIN, in_done)
                for a in range(8):
                    base = (a % 2) * 2048
                    if a >= 2:
                        g.wait_ge(semTE, teP[("p0", a - 2)])
                    L[a % 2].inc(g.dma_start(
                        out=A16[:, base:base + 1024],
                        in_=wr_d[:, a, :, :].rearrange("p k m -> p (k m)")),
                        16)
                    L[a % 2].inc(g.dma_start(
                        out=A16[:, base + 1024:base + 2048],
                        in_=wi_d[:, a, :, :].rearrange("p k m -> p (k m)")),
                        16)
                    assert L[a % 2].v == mL1[a]

        p1a = dict(dv=DV.v, te=TE.v, ac=AC.v)

        if phases < 2:
            _final(nc, [(semIN, IN.v), (semX0, sX[0].v), (semX1, sX[1].v)])
            return nc

        p0s1 = dict(dv=DV.v, te=TE.v)

        mtst = [qstage[:, 0:512], qstage[:, 512:1024]]
        mST2 = {}
        _stv = ST.v
        for a in range(8):
            _stv += 32
            mST2[a] = _stv
        with (
            nc.psum_tensor("pmt0", [128, 1024], F32) as pmt0,
            nc.psum_tensor("pmt1", [128, 1024], F32) as pmt1,
            nc.Block() as blk,
        ):
            teB, dvB = TE.v, DV.v

            @blk.tensor
            def _(te):
                te.wait_ge(semDV, p0s1["dv"])
                for a in range(8):
                    pm = pmt0 if a % 2 == 0 else pmt1
                    if a >= 2:
                        te.wait_ge(semDV, dvB + (a - 1) * 2)
                    for k in range(8):
                        grs = GrS[:, k * 1024 + a * 128:
                                  k * 1024 + (a + 1) * 128]
                        gis = GipS[:, k * 1024 + a * 128:
                                   k * 1024 + (a + 1) * 128]
                        C = [CSR[:, k * 1024 + h * 512:k * 1024 + (h + 1) * 512]
                             for h in range(2)]
                        S = [CSR[:, 8192 + k * 1024 + h * 512:
                                 8192 + k * 1024 + (h + 1) * 512]
                             for h in range(2)]
                        st, sp = (k == 0), (k == 7)
                        te.matmul(pm[:, 0:512], grs, C[0], start=st, stop=False)
                        te.matmul(pm[:, 512:1024], grs, C[1], start=st,
                                  stop=False)
                        te.matmul(pm[:, 0:512], gis, S[0], start=False, stop=sp)
                        TE.inc(te.matmul(pm[:, 512:1024], gis, S[1],
                                         start=False, stop=sp))

            @blk.vector
            def _(v):
                for a in range(8):
                    pm = pmt0 if a % 2 == 0 else pmt1
                    v.wait_ge(semTE, teB + (a + 1) * 8)
                    if a >= 1:
                        v.wait_ge(semST, mST2[a - 1])
                    DV.inc(v.tensor_scalar_mul(mtst[0], pm[:, 0:512], 1.0 / D))
                    v.wait_ge(semDV, DV.v)
                    DV.inc(v.tensor_scalar_mul(mtst[1], pm[:, 512:1024],
                                               1.0 / D))

            @blk.gpsimd
            def _(g):
                for a in range(8):
                    g.wait_ge(semDV, dvB + (a + 1) * 2)
                    ST.inc(g.dma_start(out=mt_d[a * 128:(a + 1) * 128, 0:512],
                                       in_=mtst[0]), 16)
                    ST.inc(g.dma_start(
                        out=mt_d[a * 128:(a + 1) * 128, 512:1024],
                        in_=mtst[1]), 16)
                    assert ST.v == mST2[a]

        p0 = dict(dv=DV.v, te=TE.v, st=ST.v)

        if phases < 3:
            _final(nc, [(semST, ST.v)])
            return nc

        mPA1b = {}
        _pav = [c.v for c in PA]
        for t in range(NTT):
            _pav[t % 2] += 16
            mPA1b[t] = _pav[t % 2]
        with (
            nc.psum_tensor("pf0", [128, 1024], F32) as pf0,
            nc.psum_tensor("pf1", [128, 1024], F32) as pf1,
            nc.Block() as blk,
        ):
            teB, dvB = TE.v, DV.v
            wt_mt = WT.v + 8 * 16

            @blk.tensor
            def _(te):
                te.wait_ge(semWT, wt_mt)
                te.wait_ge(semDV, dv_bz)
                for t in range(NTT):
                    pf = pf0 if t % 2 == 0 else pf1
                    if t >= 2:
                        te.wait_ge(semDV, dvB + (t - 1))
                    for k in range(8):
                        lh = XNT[:, k * T + t * 128:k * T + (t + 1) * 128]
                        st = (k == 0)
                        te.matmul(pf[:, 0:512], lh,
                                  A16[:, k * 1024:k * 1024 + 512],
                                  start=st, stop=False)
                        te.matmul(pf[:, 512:1024], lh,
                                  A16[:, k * 1024 + 512:(k + 1) * 1024],
                                  start=st, stop=False)
                    te.matmul(pf[:, 0:512], eb, bz[:, 0:512],
                              start=False, stop=True)
                    TE.inc(te.matmul(pf[:, 512:1024], eb, bz[:, 512:1024],
                                     start=False, stop=True))

            @blk.vector
            def _(v):
                for t in range(NTT):
                    pf = pf0 if t % 2 == 0 else pf1
                    v.wait_ge(semTE, teB + t + 1)
                    if t >= 2:
                        v.wait_ge(semPA[t % 2], mPA1b[t - 2])
                    DV.inc(v.tensor_copy(pstg[t % 2], pf[:]))

            @blk.gpsimd
            def _(g):
                g.wait_ge(semTE, p0["te"])
                g.wait_ge(semST, p0["st"])
                for k in range(8):
                    WT.inc(g.dma_start(
                        out=A16[:, k * 1024:(k + 1) * 1024],
                        in_=mt_d[k * 128:(k + 1) * 128, :]), 16)
                assert WT.v == wt_mt
                for t in range(NTT):
                    g.wait_ge(semDV, dvB + t + 1)
                    PA[t % 2].inc(g.dma_start(
                        out=part_d[t * 128:(t + 1) * 128, :],
                        in_=pstg[t % 2]), 16)
                    assert PA[t % 2].v == mPA1b[t]

        p1b = dict(dv=DV.v, te=TE.v)
        pa_1b_final = [PA[0].v, PA[1].v]

        if phases < 4:
            _final(nc, [(semPA0, PA[0].v), (semPA1, PA[1].v)])
            return nc

        mQS = {}
        _qv = [c.v for c in QSm]
        for n in range(64):
            _qv[n % 2] += 16
            mQS[n] = _qv[n % 2]
        mVS = {}
        _vv = [c.v for c in VS]
        for t in range(NTT):
            _vv[t % 2] += 16
            mVS[t] = _vv[t % 2]
        with (
            nc.psum_tensor("pq0", [128, 512], F32) as pq0,
            nc.psum_tensor("pq1", [128, 512], F32) as pq1,
            nc.psum_tensor("pv0", [128, 1024], F32) as pv0,
            nc.psum_tensor("pv1", [128, 1024], F32) as pv1,
            nc.Block() as blk,
        ):
            teB, dvB = TE.v, DV.v
            wt_wq = WT.v + 8 * 16
            wt_wk = wt_wq + 8 * 16
            wt_wv = wt_wk + 8 * 16

            @blk.tensor
            def _(te):
                te.wait_ge(semDV, p1b["dv"])
                te.wait_ge(semWT, wt_wq)
                for c in range(NP):
                    for t8 in range(8):
                        n = c * 8 + t8
                        pq = pq0 if n % 2 == 0 else pq1
                        if n >= 2:
                            te.wait_ge(semDV, dvB + n - 1)
                        last = None
                        for k in range(8):
                            last = te.matmul(
                                pq[:],
                                A16[:, k * 1024 + c * 128:
                                    k * 1024 + (c + 1) * 128],
                                XNT[:, k * T + t8 * 512:k * T + (t8 + 1) * 512],
                                start=(k == 0), stop=(k == 7))
                        TE.inc(last)
                te.wait_ge(semWT, wt_wk)
                for c in range(NP):
                    for t8 in range(8):
                        n = c * 8 + t8
                        pq = pq0 if n % 2 == 0 else pq1
                        if n >= 2:
                            te.wait_ge(semDV, dvB + 64 + n - 1)
                        last = None
                        for k in range(8):
                            last = te.matmul(
                                pq[:],
                                A16[:, k * 1024 + c * 128:
                                    k * 1024 + (c + 1) * 128],
                                XNT[:, k * T + t8 * 512:k * T + (t8 + 1) * 512],
                                start=(k == 0), stop=(k == 7))
                        TE.inc(last)
                te.wait_ge(semWT, wt_wv)
                te.wait_ge(semIN, in_done)
                for t in range(NTT):
                    pv = pv0 if t % 2 == 0 else pv1
                    if t >= 2:
                        te.wait_ge(semDV, dvB + 128 + t - 1)
                    for k in range(8):
                        lh = XNT[:, k * T + t * 128:k * T + (t + 1) * 128]
                        st = (k == 0)
                        te.matmul(pv[:, 0:512], lh,
                                  A16[:, k * 1024:k * 1024 + 512],
                                  start=st, stop=False)
                        te.matmul(pv[:, 512:1024], lh,
                                  A16[:, k * 1024 + 512:(k + 1) * 1024],
                                  start=st, stop=False)
                    te.matmul(pv[:, 0:512], eb, bvz[:, 0:512],
                              start=False, stop=True)
                    TE.inc(te.matmul(pv[:, 512:1024], eb, bvz[:, 512:1024],
                                     start=False, stop=True))

            @blk.vector
            def _(v):
                for c in range(NP):
                    for t8 in range(8):
                        n = c * 8 + t8
                        pq = pq0 if n % 2 == 0 else pq1
                        v.wait_ge(semTE, teB + n + 1)
                        if n >= 2:
                            v.wait_ge(semQS[n % 2], mQS[n - 2])
                        DV.inc(v.tensor_scalar(
                            qstage[:, (n % 2) * 512:(n % 2) * 512 + 512],
                            pq[:], bqk[:, c:c + 1], None, ALU.add))
                for c in range(NP):
                    for t8 in range(8):
                        n = c * 8 + t8
                        pq = pq0 if n % 2 == 0 else pq1
                        v.wait_ge(semTE, teB + 64 + n + 1)
                        DV.inc(v.tensor_scalar(
                            KT[:, c * T + t8 * 512:c * T + (t8 + 1) * 512],
                            pq[:], bqk[:, 8 + c:9 + c], None, ALU.add))
                for t in range(NTT):
                    pv = pv0 if t % 2 == 0 else pv1
                    v.wait_ge(semTE, teB + 128 + t + 1)
                    if t >= 2:
                        v.wait_ge(semVS[t % 2], mVS[t - 2])
                    DV.inc(v.tensor_copy(
                        vstage[:, (t % 2) * 1024:(t % 2) * 1024 + 1024],
                        pv[:]))

            @blk.gpsimd
            def _(g):
                g.wait_ge(semTE, p1b["te"])
                for k in range(8):
                    WT.inc(g.dma_start(
                        out=A16[:, k * 1024:(k + 1) * 1024],
                        in_=wqT_d[k * 128:(k + 1) * 128, :]), 16)
                assert WT.v == wt_wq
                for n in range(64):
                    c, t8 = n // 8, n % 8
                    g.wait_ge(semDV, dvB + n + 1)
                    QSm[n % 2].inc(g.dma_start(
                        out=q_d[c, :, t8 * 512:(t8 + 1) * 512],
                        in_=qstage[:, (n % 2) * 512:(n % 2) * 512 + 512]), 16)
                    assert QSm[n % 2].v == mQS[n]
                g.wait_ge(semTE, teB + 64)
                for k in range(8):
                    WT.inc(g.dma_start(
                        out=A16[:, k * 1024:(k + 1) * 1024],
                        in_=wkT_d[k * 128:(k + 1) * 128, :]), 16)
                assert WT.v == wt_wk
                g.wait_ge(semTE, teB + 128)
                for k in range(8):
                    WT.inc(g.dma_start(
                        out=A16[:, k * 1024:(k + 1) * 1024],
                        in_=wvT_d[k * 128:(k + 1) * 128, :]), 16)
                assert WT.v == wt_wv
                for t in range(NTT):
                    g.wait_ge(semDV, dvB + 128 + t + 1)
                    VS[t % 2].inc(g.dma_start(
                        out=v_d[t * 128:(t + 1) * 128, :],
                        in_=vstage[:, (t % 2) * 1024:(t % 2) * 1024 + 1024]),
                        16)
                    assert VS[t % 2].v == mVS[t]

        p2 = dict(dv=DV.v, te=TE.v)
        vs_final = [VS[0].v, VS[1].v]
        qs_final = [QSm[0].v, QSm[1].v]

        if phases < 5:
            _final(nc, [(semVS0, VS[0].v), (semVS1, VS[1].v),
                        (semQS0, QSm[0].v), (semQS1, QSm[1].v)])
            return nc

        with nc.Block() as blk:
            @blk.gpsimd
            def _(g):
                g.wait_ge(semQS0, qs_final[0])
                g.wait_ge(semQS1, qs_final[1])
                g.wait_ge(semTE, p2["te"])
                for pr in range(NP):
                    QT.inc(g.dma_start(
                        out=QTs[:, pr * 512:(pr + 1) * 512],
                        in_=q_d[pr, :, 0:512]), 16)
        qt_marks = {0: QT.v}

        def plan_dv_p3(base):
            m = {}
            v = base
            for pr in range(NP):
                for kc in range(NKC):
                    v += 1
                    m[("ps", pr, kc)] = v
                if pr >= 1:
                    v += 1
                    m[("scl", pr - 1)] = v
                    v += 1
                    m[("ot", pr - 1)] = v
                v += 3
                m[("zch", pr)] = v
            v += 1
            m[("scl", 7)] = v
            v += 1
            m[("ot", 7)] = v
            for tt in range(4):
                v += 2
                m[("ev", tt)] = v
            m["end"] = v
            return m

        P3 = {}
        _dvc = DV.v
        _acc = AC.v
        _vcc = [c.v for c in VC]
        _sxc = [c.v for c in sX]
        _woc = [c.v for c in WO]
        _pac = [c.v for c in PA]
        _ouc = [c.v for c in OU]
        _qtc = QT.v
        for q in range(NQS):
            pl = {}
            pl["dvM"] = plan_dv_p3(_dvc)
            _dvc = pl["dvM"]["end"]
            pl["acB"] = _acc
            _acc += NP * NKC
            pl["mVC"] = {}
            for kc in range(NKC):
                _vcc[kc % 4] += 16
                pl["mVC"][kc] = _vcc[kc % 4]
            pl["mXq"] = {}
            for tt in range(4):
                _sxc[tt % 2] += 16
                pl["mXq"][tt] = _sxc[tt % 2]
            pl["mWOq"] = {}
            for j in range(2 * NP):
                _woc[j % 2] += 16
                pl["mWOq"][j] = _woc[j % 2]
            pl["mPAq"] = {}
            for tt in range(4):
                _pac[tt % 2] += 16
                pl["mPAq"][tt] = _pac[tt % 2]
            pl["mOUq"] = {}
            for tt in range(4):
                _ouc[tt % 2] += 16
                pl["mOUq"][tt] = _ouc[tt % 2]
            if q + 1 < NQS:
                _qtc += NP * 16
                pl["qt_next"] = _qtc
            P3[q] = pl
        P3[0]["qt_ready"] = qt_marks[0]
        for q in range(1, NQS):
            P3[q]["qt_ready"] = P3[q - 1]["qt_next"]

        teM = {}

        with (
            nc.psum_tensor("pSA", [128, 1024], F32) as pSA,
            nc.psum_tensor("pSB", [128, 1024], F32) as pSB,
            nc.psum_tensor("pOu0", [128, 512], F32) as pOu0,
            nc.psum_tensor("pOu1", [128, 512], F32) as pOu1,
            nc.psum_tensor("pOu2", [128, 512], F32) as pOu2,
            nc.psum_tensor("pzb", [128, 512], F32) as pzb,
            nc.Block() as blk,
        ):
            SA = [pSA, pSB]
            pOu = [pOu0, pOu1, pOu2]
            pz = pzb[0:2, :]
            pbc = pzb
            pes = [pSA, pSB]

            @blk.tensor
            def _(te):
                for q in range(NQS):
                    pl = P3[q]
                    dvM = pl["dvM"]
                    acB = pl["acB"]
                    dvM_prev = P3[q - 1]["dvM"] if q > 0 else None

                    if q == 0:
                        te.wait_ge(semDV, p2["dv"])
                    else:
                        te.wait_ge(semDV, dvM_prev[("ev", 3)])
                    te.wait_ge(semQT, pl["qt_ready"])

                    def emit_sc(u):
                        pr, kc = u // NKC, u % NKC
                        if u >= 2:
                            te.wait_ge(semAC, acB + u - 1)
                        sset = SA[u % 2]
                        te.matmul(
                            sset[:, 0:512],
                            KT[0:64, pr * T + kc * 128:pr * T + (kc + 1) * 128],
                            QTs[0:64, pr * 512:(pr + 1) * 512],
                            start=True, stop=True)
                        last = te.matmul(
                            sset[:, 512:1024],
                            KT[64:128, pr * T + kc * 128:
                               pr * T + (kc + 1) * 128],
                            QTs[64:128, pr * 512:(pr + 1) * 512],
                            start=True, stop=True, tile_position=(64, 0))
                        teM[(q, "sc", u)] = TE.inc(last)

                    def emit_pv(u):
                        pr, kc = u // NKC, u % NKC
                        te.wait_ge(semAC, acB + u + 1)
                        te.wait_ge(semVC[kc % 4], pl["mVC"][kc])
                        if kc == 0:
                            if pr >= 3:
                                te.wait_ge(semDV, dvM[("ot", pr - 3)])
                            elif q > 0:
                                te.wait_ge(semDV,
                                           dvM_prev[("ot", {0: 6, 1: 7,
                                                            2: 5}[pr])])
                        pt = PT[u % 4]
                        po = pOu[pr % 3]
                        st, sp = (kc == 0), (kc == NKC - 1)
                        te.matmul(po[0:64, :],
                                  VQ[:, kc * 1024 + pr * 128:
                                     kc * 1024 + pr * 128 + 64],
                                  pt[:, 0:512], start=st, stop=sp,
                                  skip_group_check=True)
                        last = te.matmul(
                            po[64:128, :],
                            VQ[:, kc * 1024 + pr * 128 + 64:
                               kc * 1024 + (pr + 1) * 128],
                            pt[:, 512:1024], start=st, stop=sp,
                            tile_position=(0, 64), skip_group_check=True)
                        teM[(q, "pv", u)] = TE.inc(last)

                    def emit_zmm(pr):
                        te.wait_ge(semDV, dvM[("ps", pr, NKC - 1)])
                        if pr >= 1:
                            te.wait_ge(semDV, dvM[("scl", pr - 1)])
                        elif q > 0:
                            te.wait_ge(semDV, dvM_prev[("scl", 7)])
                        te.matmul(pz, zsel[:, 0:2], PS[0][:, 0:512],
                                  start=True, stop=False)
                        te.matmul(pz, zsel[:, 1:3], PS[0][:, 512:1024],
                                  start=False, stop=False)
                        te.matmul(pz, zsel[:, 0:2], PS[1][:, 0:512],
                                  start=False, stop=False)
                        last = te.matmul(pz, zsel[:, 1:3],
                                         PS[1][:, 512:1024],
                                         start=False, stop=True)
                        teM[(q, "zmm", pr)] = TE.inc(last)

                    def emit_bc(pr):
                        te.wait_ge(semDV, dvM[("zch", pr)])
                        if pr >= 1:
                            te.wait_ge(semDV, dvM[("scl", pr - 1)])
                        elif q > 0:
                            te.wait_ge(semDV, dvM_prev[("scl", 7)])
                        teM[(q, "bc", pr)] = TE.inc(
                            te.matmul(pbc[:], e2, zpad, start=True, stop=True,
                                      skip_group_check=True))

                    for u in range(NP * NKC):
                        emit_sc(u)
                        if u >= 2:
                            emit_pv(u - 2)
                            if (u - 2) % NKC == NKC - 1:
                                prd = (u - 2) // NKC
                                if prd >= 1:
                                    emit_bc(prd - 1)
                                emit_zmm(prd)
                    emit_pv(NP * NKC - 2)
                    emit_pv(NP * NKC - 1)
                    emit_bc(NP - 2)
                    emit_zmm(NP - 1)
                    emit_bc(NP - 1)

                    for i in range(2 * NP):
                        wave, pc = i // NP, i % NP
                        if i == 0:
                            te.wait_ge(semAC, acB + 256)
                            te.wait_ge(semDV, dvM[("ot", 7)])
                        if i == NP:
                            te.wait_ge(semDV, dvM[("ev", 1)])
                        te.wait_ge(semWO[i % 2], pl["mWOq"][i])
                        ws = wo_sb[:, (i % 2) * 1024:(i % 2) * 1024 + 1024]
                        last = None
                        for w2 in range(2):
                            tt = wave * 2 + w2
                            lh = OTr[:, pc * 512 + tt * 128:
                                     pc * 512 + (tt + 1) * 128]
                            st, sp = (pc == 0), (pc == NP - 1)
                            te.matmul(pes[w2][:, 0:512], lh, ws[:, 0:512],
                                      start=st, stop=sp)
                            last = te.matmul(pes[w2][:, 512:1024], lh,
                                             ws[:, 512:1024], start=st, stop=sp)
                        teM[(q, "p4g", i)] = TE.inc(last)

            @blk.scalar
            def _(sc):
                for q in range(NQS):
                    pl = P3[q]
                    acB = pl["acB"]
                    dvM = pl["dvM"]
                    dvM_prev = P3[q - 1]["dvM"] if q > 0 else None
                    for u in range(NP * NKC):
                        sc.wait_ge(semTE, teM[(q, "sc", u)])
                        if u >= 4:
                            sc.wait_ge(semTE, teM[(q, "pv", u - 4)])
                        elif q > 0:
                            sc.wait_ge(semTE, teM[(q - 1, "pv", 252 + u)])
                        if u >= 2:
                            sc.wait_ge(semDV,
                                       dvM[("ps", (u - 2) // NKC,
                                            (u - 2) % NKC)])
                        elif q > 0:
                            sc.wait_ge(semDV, dvM_prev[("ps", 7, 30 + u)])
                        AC.inc(sc.activation(PT[u % 4], SA[u % 2][:],
                                             AF.Exp, bias=lnbias, scale=0.125))
                    assert AC.v == acB + 256

            @blk.vector
            def _(v):
                for q in range(NQS):
                    pl = P3[q]
                    acB = pl["acB"]
                    dvM = pl["dvM"]
                    for pr in range(NP):
                        for kc in range(NKC):
                            u = pr * NKC + kc
                            v.wait_ge(semAC, acB + u + 1)
                            if kc <= 1:
                                if pr >= 1:
                                    v.wait_ge(semTE, teM[(q, "zmm", pr - 1)])
                                elif q > 0:
                                    v.wait_ge(semTE, teM[(q - 1, "zmm", 7)])
                            if kc <= 1:
                                DV.inc(v.tensor_copy(PS[kc % 2], PT[u % 4]))
                            else:
                                DV.inc(v.tensor_add(PS[kc % 2], PS[kc % 2],
                                                    PT[u % 4]))
                        if pr >= 1:
                            v.wait_ge(semTE, teM[(q, "bc", pr - 1)])
                            v.wait_ge(semDV, DV.v)
                            DV.inc(v.tensor_copy(sclt, pbc[:]))
                            v.wait_ge(semDV, DV.v)
                            v.wait_ge(semTE, teM[(q, "pv", (pr - 1) * NKC + 31)])
                            DV.inc(v.tensor_mul(
                                OTr[:, (pr - 1) * 512:pr * 512],
                                pOu[(pr - 1) % 3][:], sclt))
                        v.wait_ge(semTE, teM[(q, "zmm", pr)])
                        v.wait_ge(semDV, DV.v)
                        DV.inc(v.tensor_copy(zf32, pz))
                        v.wait_ge(semDV, DV.v)
                        DV.inc(v.reciprocal(rcpt, zf32))
                        v.wait_ge(semDV, DV.v)
                        if pr >= 1:
                            v.wait_ge(semTE, teM[(q, "bc", pr - 1)])
                        elif q > 0:
                            v.wait_ge(semTE, teM[(q - 1, "bc", 7)])
                        DV.inc(v.tensor_copy(zpad[0:2, :], rcpt))
                        assert DV.v == dvM[("zch", pr)]
                    v.wait_ge(semTE, teM[(q, "bc", 7)])
                    v.wait_ge(semDV, DV.v)
                    DV.inc(v.tensor_copy(sclt, pbc[:]))
                    v.wait_ge(semDV, DV.v)
                    v.wait_ge(semTE, teM[(q, "pv", 7 * NKC + 31)])
                    DV.inc(v.tensor_mul(OTr[:, 7 * 512:8 * 512],
                                        pOu[7 % 3][:], sclt))
                    for tt in range(4):
                        v.wait_ge(semTE, teM[(q, "p4g", 7 if tt < 2 else 15)])
                        v.wait_ge(semPA[tt % 2], pl["mPAq"][tt])
                        v.wait_ge(semDV, DV.v)
                        DV.inc(v.tensor_add(xsq, pes[tt % 2][:], pstg[tt % 2]))
                        v.wait_ge(semX[tt % 2], pl["mXq"][tt])
                        v.wait_ge(semDV, DV.v)
                        DV.inc(v.tensor_add(pstg[tt % 2], xsq,
                                            x0 if tt % 2 == 0 else x1))
                    assert DV.v == dvM[("ev", 3)]

            @blk.sync
            def _(s):
                for q in range(NQS):
                    pl = P3[q]
                    dvM = pl["dvM"]
                    dvM_prev = P3[q - 1]["dvM"] if q > 0 else None
                    for kc in range(NKC):
                        if kc == 0:
                            if q == 0:
                                s.wait_ge(semTE, p2["te"])
                                s.wait_ge(semVS0, vs_final[0])
                                s.wait_ge(semVS1, vs_final[1])
                            else:
                                s.wait_ge(semTE, teM[(q - 1, "pv", 255)])
                        if kc >= 4:
                            s.wait_ge(semTE, teM[(q, "pv", kc - 4)])
                        VC[kc % 4].inc(s.dma_start(
                            out=VQ[:, kc * 1024:(kc + 1) * 1024],
                            in_=v_d[kc * 128:(kc + 1) * 128, :]), 16)
                        assert VC[kc % 4].v == pl["mVC"][kc]
                    for tt in range(4):
                        if tt >= 2:
                            s.wait_ge(semDV, dvM[("ev", tt - 2)])
                        elif q > 0:
                            s.wait_ge(semDV, dvM_prev[("ev", tt + 2)])
                        sX[tt % 2].inc(s.dma_start(
                            out=(x0 if tt % 2 == 0 else x1),
                            in_=x_d[q * QS + tt * 128:
                                    q * QS + (tt + 1) * 128, :]), 16)
                        assert sX[tt % 2].v == pl["mXq"][tt]
                    for tt in range(4):
                        s.wait_ge(semDV, dvM[("ev", tt)])
                        OU[tt % 2].inc(s.dma_start(
                            out=out_d[q * QS + tt * 128:
                                      q * QS + (tt + 1) * 128, :],
                            in_=pstg[tt % 2]), 16)
                        assert OU[tt % 2].v == pl["mOUq"][tt]

            @blk.gpsimd
            def _(g):
                for q in range(NQS):
                    pl = P3[q]
                    for tt in range(2):
                        if q == 0:
                            g.wait_ge(semPA[tt % 2], pa_1b_final[tt % 2])
                        else:
                            g.wait_ge(semOU[tt % 2], P3[q - 1]["mOUq"][tt + 2])
                        PA[tt % 2].inc(g.dma_start(
                            out=pstg[tt % 2],
                            in_=part_d[q * QS + tt * 128:
                                       q * QS + (tt + 1) * 128, :]), 16)
                        assert PA[tt % 2].v == pl["mPAq"][tt]
                    if q + 1 < NQS:
                        g.wait_ge(semTE, teM[(q, "sc", 255)])
                        for pr in range(NP):
                            QT.inc(g.dma_start(
                                out=QTs[:, pr * 512:(pr + 1) * 512],
                                in_=q_d[pr, :, (q + 1) * 512:(q + 2) * 512]),
                                16)
                        assert QT.v == pl["qt_next"]
                    for i in range(2 * NP):
                        if i >= 2:
                            g.wait_ge(semTE, teM[(q, "p4g", i - 2)])
                        elif q > 0:
                            g.wait_ge(semTE, teM[(q - 1, "p4g", 14 + i)])
                        WO[i % 2].inc(g.dma_start(
                            out=wo_sb[:, (i % 2) * 1024:(i % 2) * 1024 + 1024],
                            in_=woT_d[(i % NP) * 128:(i % NP + 1) * 128, :]),
                            16)
                        assert WO[i % 2].v == pl["mWOq"][i]
                    for tt in range(2, 4):
                        g.wait_ge(semOU[tt % 2], pl["mOUq"][tt - 2])
                        PA[tt % 2].inc(g.dma_start(
                            out=pstg[tt % 2],
                            in_=part_d[q * QS + tt * 128:
                                       q * QS + (tt + 1) * 128, :]), 16)
                        assert PA[tt % 2].v == pl["mPAq"][tt]

        assert DV.v == P3[NQS - 1]["dvM"]["end"]

        _final(nc, [(semOU0, OU[0].v), (semOU1, OU[1].v),
                    (semVC0, VC[0].v), (semVC1, VC[1].v),
                    (semVC2, VC[2].v), (semVC3, VC[3].v),
                    (semPA0, PA[0].v), (semPA1, PA[1].v),
                    (semQT, QT.v)])
    return nc


def _host_constants():
    j = np.arange(D, dtype=np.float64)
    ang = 2.0 * np.pi * np.outer(j, j) / D
    cmat = np.cos(ang).astype(np.float32).astype(np.float16)
    smat = np.sin(ang).astype(np.float32).astype(np.float16)
    ident = np.eye(128, dtype=np.float16)
    e2 = np.zeros((128, 128), np.float16)
    e2[0, 0:64] = 1.0
    e2[1, 64:128] = 1.0
    eb = np.zeros((128, 128), np.float16)
    eb[0, :] = 1.0
    zsel = np.zeros((128, 4), np.float16)
    zsel[:, 0] = 1.0
    zsel[:, 2] = 1.0
    return cmat, smat, ident, e2, eb, zsel


def kernel(L0_attn_hidden_states, L0_attn_Wq, L0_attn_bq, L0_attn_Wk,
           L0_attn_bk, L0_attn_Wv, L0_attn_bv, L0_attn_Wo, L0_attn_bo,
           L0_ffn_w_freq_real, L0_ffn_w_freq_imag, L0_ffn_bias):
    x = np.ascontiguousarray(np.asarray(L0_attn_hidden_states, np.float32))
    cmat, smat, ident, e2, eb, zsel = _host_constants()
    bqkT = np.ascontiguousarray(np.concatenate(
        [np.asarray(L0_attn_bq, np.float32).reshape(8, 128).T,
         np.asarray(L0_attn_bk, np.float32).reshape(8, 128).T], axis=1))
    shared = {
        "wqT": np.ascontiguousarray(np.asarray(L0_attn_Wq, np.float32).T),
        "wkT": np.ascontiguousarray(np.asarray(L0_attn_Wk, np.float32).T),
        "wvT": np.ascontiguousarray(np.asarray(L0_attn_Wv, np.float32).T),
        "woT": np.ascontiguousarray(np.asarray(L0_attn_Wo, np.float32).T),
        "bqkT": bqkT,
        "bo_row": np.asarray(L0_attn_bo, np.float32).reshape(1, D),
        "fb_row": np.asarray(L0_ffn_bias, np.float32).reshape(1, D),
        "bv_row": np.asarray(L0_attn_bv, np.float32).reshape(1, D),
        "wr": np.ascontiguousarray(
            np.asarray(L0_ffn_w_freq_real, np.float32)
            .reshape(8, 128, 8, 128).transpose(1, 2, 0, 3)),
        "wi": np.ascontiguousarray(
            np.asarray(L0_ffn_w_freq_imag, np.float32)
            .reshape(8, 128, 8, 128).transpose(1, 2, 0, 3)),
        "cmat": cmat, "smat": smat,
        "ident": ident, "e2": e2, "eb": eb, "zsel": zsel,
    }
    in_maps = []
    for b in range(8):
        m = dict(shared)
        m["x"] = np.ascontiguousarray(x[b])
        in_maps.append(m)
    nc = build_nc()
    res = run_bass_kernel_spmd(nc, in_maps, core_ids=list(range(8)))
    out = np.stack([res.results[b]["out"] for b in range(8)], axis=0)
    return out.astype(np.float32)


# version 13
# speedup vs baseline: 1.0036x; 1.0036x over previous
import os as _os
import contextlib
import numpy as np
import concourse.bass as bass
import concourse.mybir as mybir
from concourse.bass_utils import run_bass_kernel_spmd

F32 = mybir.dt.float32
F16 = mybir.dt.float16
AF = mybir.ActivationFunctionType
ALU = mybir.AluOpType
AX = mybir.AxisListType

T = 4096
D = 1024
NP = 8
QS = 512
NQS = T // QS
NKC = 32
NTT = T // 128
LN64 = float(np.log(64.0))


class Ctr:
    def __init__(self, sem):
        self.sem = sem
        self.v = 0

    def inc(self, inst, n=1):
        inst.then_inc(self.sem, n)
        self.v += n
        return self.v


def _final(nc, waits):
    with nc.Block() as blk:
        @blk.sync
        def _(s):
            for sem, v in waits:
                if v > 0:
                    s.wait_ge(sem, v)


def build_nc(phases=99):
    nc = bass.Bass("TRN2", target_bir_lowering=False)

    x_d = nc.dram_tensor("x", [T, D], F32, kind="ExternalInput").ap()
    wqT_d = nc.dram_tensor("wqT", [D, D], F32, kind="ExternalInput").ap()
    wkT_d = nc.dram_tensor("wkT", [D, D], F32, kind="ExternalInput").ap()
    wvT_d = nc.dram_tensor("wvT", [D, D], F32, kind="ExternalInput").ap()
    woT_d = nc.dram_tensor("woT", [D, D], F32, kind="ExternalInput").ap()
    bqk_d = nc.dram_tensor("bqkT", [128, 16], F32, kind="ExternalInput").ap()
    bo_d = nc.dram_tensor("bo_row", [1, D], F32, kind="ExternalInput").ap()
    fb_d = nc.dram_tensor("fb_row", [1, D], F32, kind="ExternalInput").ap()
    bv_d = nc.dram_tensor("bv_row", [1, D], F32, kind="ExternalInput").ap()
    wr_d = nc.dram_tensor("wr", [128, 8, 8, 128], F32, kind="ExternalInput").ap()
    wi_d = nc.dram_tensor("wi", [128, 8, 8, 128], F32, kind="ExternalInput").ap()
    c_d = nc.dram_tensor("cmat", [D, D], F16, kind="ExternalInput").ap()
    s_d = nc.dram_tensor("smat", [D, D], F16, kind="ExternalInput").ap()
    ident_d = nc.dram_tensor("ident", [128, 128], F16, kind="ExternalInput").ap()
    e2_d = nc.dram_tensor("e2", [128, 128], F16, kind="ExternalInput").ap()
    eb_d = nc.dram_tensor("eb", [128, 128], F16, kind="ExternalInput").ap()
    zsel_d = nc.dram_tensor("zsel", [128, 4], F16, kind="ExternalInput").ap()

    out_d = nc.dram_tensor("out", [T, D], F32, kind="ExternalOutput").ap()

    q_d = nc.dram_tensor("q_scr", [NP, 128, T], F16, kind="Internal").ap()
    v_d = nc.dram_tensor("v_scr", [T, D], F16, kind="Internal").ap()
    mt_d = nc.dram_tensor("mt_scr", [D, D], F16, kind="Internal").ap()
    gr_d = nc.dram_tensor("gr_scr", [D, D], F32, kind="Internal").ap()
    gip_d = nc.dram_tensor("gip_scr", [D, D], F32, kind="Internal").ap()
    part_d = nc.dram_tensor("part_scr", [T, D], F32, kind="Internal").ap()

    bigA_h = nc.alloc_sbuf_tensor("bigA", [128, NP * T], F16)
    bigB_h = nc.alloc_sbuf_tensor("bigB", [128, NP * T], F16)
    a16_h = nc.alloc_sbuf_tensor("a16", [128, 12288], F16)
    a32_h = nc.alloc_sbuf_tensor("a32", [128, 5120], F32)

    XNT = bigA_h.ap()
    VQ = XNT
    KT = bigB_h.ap()
    CSR = KT
    GrS = KT[:, 16384:24576]
    GipS = KT[:, 24576:32768]
    A16 = a16_h.ap()
    A32 = a32_h.ap()

    ident = nc.alloc_sbuf_tensor("identt", [128, 128], F16).ap()
    e2 = nc.alloc_sbuf_tensor("e2t", [128, 128], F16).ap()
    eb = nc.alloc_sbuf_tensor("ebt", [128, 128], F16).ap()
    zsel = nc.alloc_sbuf_tensor("zselt", [128, 4], F16).ap()
    bqk = nc.alloc_sbuf_tensor("bqkt", [128, 16], F32).ap()
    bvz = nc.alloc_sbuf_tensor("bvzt", [128, D], F16).ap()
    bz = nc.alloc_sbuf_tensor("bzt", [128, D], F16).ap()
    zpad = nc.alloc_sbuf_tensor("zpadt", [128, QS], F16).ap()
    zf32 = nc.alloc_sbuf_tensor("zf32t", [2, QS], F32).ap()
    rcpt = nc.alloc_sbuf_tensor("rcptt", [2, QS], F32).ap()
    sclt = nc.alloc_sbuf_tensor("scltt", [128, QS], F32).ap()
    wo_sb = nc.alloc_sbuf_tensor("wo_sbt", [128, 2048], F16).ap()
    qstage = nc.alloc_sbuf_tensor("qstaget", [128, 1024], F16).ap()
    vstage = nc.alloc_sbuf_tensor("vstaget", [128, 2048], F16).ap()
    xstage = nc.alloc_sbuf_tensor("xstaget", [128, 1024], F16).ap()
    brow0 = nc.alloc_sbuf_tensor("brow0t", [1, D], F32).ap()
    brow1 = nc.alloc_sbuf_tensor("brow1t", [1, D], F32).ap()
    lnbias = nc.alloc_sbuf_tensor("lnbiast", [128, 1], F32).ap()
    epsb = nc.alloc_sbuf_tensor("epsbt", [128, 1], F32).ap()
    st_s1 = nc.alloc_sbuf_tensor("st_s1t", [128, 1], F32).ap()
    st_s2 = nc.alloc_sbuf_tensor("st_s2t", [128, 1], F32).ap()
    st_mean = nc.alloc_sbuf_tensor("st_meant", [128, 1], F32).ap()
    st_var = nc.alloc_sbuf_tensor("st_vart", [128, 1], F32).ap()
    st_msq = nc.alloc_sbuf_tensor("st_msqt", [128, 1], F32).ap()
    st_lnv = nc.alloc_sbuf_tensor("st_lnvt", [128, 1], F32).ap()
    st_rstd = nc.alloc_sbuf_tensor("st_rstdt", [128, 1], F32).ap()

    x0 = A32[:, 0:1024]
    x1 = A32[:, 1024:2048]
    xsq = A32[:, 2048:3072]
    ev32 = [A32[:, i * 512:(i + 1) * 512] for i in range(4)]
    pstg = [A32[:, 3072 + i * 1024:3072 + (i + 1) * 1024] for i in range(2)]

    QTs = A16[:, 0:4096]
    pt2_h = nc.alloc_sbuf_tensor("pt2t", [128, 1024], F16).ap()
    PT = [A16[:, 4096:5120], A16[:, 5120:6144], pt2_h, xstage]
    PS = [A16[:, 6144:7168], A16[:, 7168:8192]]
    OTr = A16[:, 8192:12288]

    with contextlib.ExitStack() as _stack:
        semTE = _stack.enter_context(nc.semaphore("sTE"))
        semAC = _stack.enter_context(nc.semaphore("sAC"))
        semDV = _stack.enter_context(nc.semaphore("sDV"))
        semIN = _stack.enter_context(nc.semaphore("sIN"))
        semWT = _stack.enter_context(nc.semaphore("sWT"))
        semST = _stack.enter_context(nc.semaphore("sST"))
        semL0 = _stack.enter_context(nc.semaphore("sL0"))
        semL1 = _stack.enter_context(nc.semaphore("sL1"))
        semL2 = _stack.enter_context(nc.semaphore("sL2"))
        semL3 = _stack.enter_context(nc.semaphore("sL3"))
        semX0 = _stack.enter_context(nc.semaphore("sX0"))
        semX1 = _stack.enter_context(nc.semaphore("sX1"))
        semQS0 = _stack.enter_context(nc.semaphore("sQS0"))
        semQS1 = _stack.enter_context(nc.semaphore("sQS1"))
        semVS0 = _stack.enter_context(nc.semaphore("sVS0"))
        semVS1 = _stack.enter_context(nc.semaphore("sVS1"))
        semVC0 = _stack.enter_context(nc.semaphore("sVC0"))
        semVC1 = _stack.enter_context(nc.semaphore("sVC1"))
        semVC2 = _stack.enter_context(nc.semaphore("sVC2"))
        semVC3 = _stack.enter_context(nc.semaphore("sVC3"))
        semQT = _stack.enter_context(nc.semaphore("sQT"))
        semWO0 = _stack.enter_context(nc.semaphore("sWO0"))
        semWO1 = _stack.enter_context(nc.semaphore("sWO1"))
        semPA0 = _stack.enter_context(nc.semaphore("sPA0"))
        semPA1 = _stack.enter_context(nc.semaphore("sPA1"))
        semOU0 = _stack.enter_context(nc.semaphore("sOU0"))
        semOU1 = _stack.enter_context(nc.semaphore("sOU1"))
        TE, AC, DV = Ctr(semTE), Ctr(semAC), Ctr(semDV)
        IN = Ctr(semIN)
        WT = Ctr(semWT)
        ST = Ctr(semST)
        L = [Ctr(semL0), Ctr(semL1), Ctr(semL2), Ctr(semL3)]
        semL = [semL0, semL1, semL2, semL3]
        sX = [Ctr(semX0), Ctr(semX1)]
        semX = [semX0, semX1]
        QSm = [Ctr(semQS0), Ctr(semQS1)]
        semQS = [semQS0, semQS1]
        VS = [Ctr(semVS0), Ctr(semVS1)]
        semVS = [semVS0, semVS1]
        VC = [Ctr(semVC0), Ctr(semVC1), Ctr(semVC2), Ctr(semVC3)]
        semVC = [semVC0, semVC1, semVC2, semVC3]
        QT = Ctr(semQT)
        WO = [Ctr(semWO0), Ctr(semWO1)]
        semWO = [semWO0, semWO1]
        PA = [Ctr(semPA0), Ctr(semPA1)]
        semPA = [semPA0, semPA1]
        OU = [Ctr(semOU0), Ctr(semOU1)]
        semOU = [semOU0, semOU1]

        with nc.Block() as blk:
            @blk.vector
            def _(v):
                DV.inc(v.memset(bvz, 0.0))
                DV.inc(v.memset(bz, 0.0))
                DV.inc(v.memset(zpad, 0.0))
                DV.inc(v.memset(lnbias, -LN64))
                DV.inc(v.memset(epsb, 1e-5))

            @blk.gpsimd
            def _(g):
                for dst, src in [
                    (ident, ident_d), (e2, e2_d), (eb, eb_d), (zsel, zsel_d),
                    (bqk, bqk_d), (brow0, bo_d), (brow1, fb_d),
                ]:
                    IN.inc(g.dma_start(out=dst, in_=src), 16)
                for k in range(8):
                    IN.inc(g.dma_start(out=CSR[:, k * 1024:(k + 1) * 1024],
                                       in_=c_d[k * 128:(k + 1) * 128, :]), 16)
                    IN.inc(g.dma_start(
                        out=CSR[:, 8192 + k * 1024:8192 + (k + 1) * 1024],
                        in_=s_d[k * 128:(k + 1) * 128, :]), 16)
                g.wait_ge(semDV, 3)
                IN.inc(g.dma_start(out=bvz[0:1, :], in_=bv_d), 16)

        in_done = IN.v
        assert in_done == 24 * 16

        with nc.Block() as blk:
            @blk.vector
            def _(v):
                v.wait_ge(semIN, in_done)
                DV.inc(v.tensor_add(brow0, brow0, brow1))
                v.wait_ge(semDV, DV.v)
                DV.inc(v.tensor_copy(bz[0:1, :], brow0))
        dv_bz = DV.v

        NDV, NAC, NT1 = 15, 3, 8
        mx1a = {}
        _sxv = [c.v for c in sX]
        for t in range(NTT):
            _sxv[t % 2] += 16
            mx1a[t] = _sxv[t % 2]
        mL1 = {}
        _lv = [c.v for c in L]
        for a in range(8):
            _lv[a % 2] += 32
            mL1[a] = _lv[a % 2]
        dvP = {}
        _dvv = DV.v
        for t in range(NTT):
            _dvv += NDV
            dvP[("tile", t)] = _dvv
            if t % 4 == 3:
                a = t // 4
                _dvv += 5
                dvP[("p0ev", a)] = _dvv
        teP = {}

        with (
            nc.psum_tensor("tp0", [128, 128], F16) as tp0,
            nc.psum_tensor("tp1", [128, 128], F16) as tp1,
            nc.psum_tensor("pgr", [128, 1024], F32) as pgr,
            nc.psum_tensor("pga", [128, 1024], F32) as pga,
            nc.psum_tensor("pgb", [128, 1024], F32) as pgb,
            nc.Block() as blk,
        ):
            tps = [tp0, tp1]
            dvB, acB, teB = DV.v, AC.v, TE.v

            def dv_tile_base(t):
                return dvP[("tile", t)] - NDV

            @blk.tensor
            def _(te):
                te.wait_ge(semIN, in_done)
                for t in range(NTT):
                    te.wait_ge(semDV, dv_tile_base(t) + 7)
                    for j in range(8):
                        tj = t * 8 + j
                        if tj >= 2:
                            tp_, jp = (tj - 2) // 8, (tj - 2) % 8
                            te.wait_ge(semDV, dv_tile_base(tp_) + 7 + jp + 1)
                        TE.inc(te.transpose(
                            tps[j % 2][:],
                            xstage[:, j * 128:(j + 1) * 128], ident))
                    teP[("tr", t)] = TE.v
                    if t % 4 == 3:
                        a = t // 4
                        if a >= 1:
                            te.wait_ge(semDV, dvP[("p0ev", a - 1)])
                        te.wait_ge(semL[a % 2], mL1[a])
                        for k in range(8):
                            base = (a % 2) * 2048
                            wrs = A16[:, base + k * 128:base + (k + 1) * 128]
                            wis = A16[:, base + 1024 + k * 128:
                                      base + 1024 + (k + 1) * 128]
                            C = [CSR[:, k * 1024 + h * 512:
                                     k * 1024 + (h + 1) * 512]
                                 for h in range(2)]
                            S = [CSR[:, 8192 + k * 1024 + h * 512:
                                     8192 + k * 1024 + (h + 1) * 512]
                                 for h in range(2)]
                            st, sp = (k == 0), (k == 7)
                            te.matmul(pgr[:, 0:512], wrs, C[0], start=st,
                                      stop=False)
                            te.matmul(pgr[:, 512:1024], wrs, C[1], start=st,
                                      stop=False)
                            te.matmul(pga[:, 0:512], wrs, S[0], start=st,
                                      stop=sp)
                            te.matmul(pga[:, 512:1024], wrs, S[1], start=st,
                                      stop=sp)
                            te.matmul(pgr[:, 0:512], wis, S[0], start=False,
                                      stop=sp)
                            te.matmul(pgr[:, 512:1024], wis, S[1], start=False,
                                      stop=sp)
                            te.matmul(pgb[:, 0:512], wis, C[0], start=st,
                                      stop=sp)
                            TE.inc(te.matmul(pgb[:, 512:1024], wis, C[1],
                                             start=st, stop=sp))
                        teP[("p0", a)] = TE.v

            @blk.scalar
            def _(sc):
                for t in range(NTT):
                    xs = x0 if t % 2 == 0 else x1
                    sc.wait_ge(semX[t % 2], mx1a[t])
                    if t >= 1:
                        sc.wait_ge(semDV, dv_tile_base(t - 1) + 2)
                    AC.inc(sc.activation(xsq, xs, AF.Square))
                    sc.wait_ge(semDV, dv_tile_base(t) + 6)
                    AC.inc(sc.activation(st_lnv, st_var, AF.Ln, bias=epsb))
                    sc.wait_ge(semAC, AC.v)
                    AC.inc(sc.activation(st_rstd, st_lnv, AF.Exp, scale=-0.5))

            @blk.vector
            def _(v):
                for t in range(NTT):
                    xs = x0 if t % 2 == 0 else x1
                    b = dv_tile_base(t)
                    v.wait_ge(semX[t % 2], mx1a[t])
                    DV.inc(v.reduce_sum(st_s1, xs, axis=AX.X))
                    v.wait_ge(semAC, acB + t * NAC + 1)
                    DV.inc(v.reduce_sum(st_s2, xsq, axis=AX.X))
                    v.wait_ge(semDV, DV.v)
                    DV.inc(v.tensor_scalar_mul(st_mean, st_s1, 1.0 / D))
                    v.wait_ge(semDV, DV.v)
                    DV.inc(v.tensor_scalar_mul(st_var, st_s2, 1.0 / D))
                    v.wait_ge(semDV, DV.v)
                    DV.inc(v.tensor_mul(st_msq, st_mean, st_mean))
                    v.wait_ge(semDV, DV.v)
                    DV.inc(v.tensor_sub(st_var, st_var, st_msq))
                    v.wait_ge(semAC, acB + (t + 1) * NAC)
                    if t >= 1:
                        v.wait_ge(semTE, teP[("tr", t - 1)])
                    DV.inc(v.tensor_scalar(xstage, xs, st_mean, st_rstd,
                                           ALU.subtract, ALU.mult))
                    for j in range(8):
                        v.wait_ge(semTE, teP[("tr", t)] - 8 + j + 1)
                        DV.inc(v.tensor_copy(
                            XNT[:, j * T + t * 128:j * T + (t + 1) * 128],
                            tps[j % 2][:]))
                    assert DV.v == dvP[("tile", t)]
                    if t % 4 == 3:
                        a = t // 4
                        v.wait_ge(semTE, teP[("p0", a)])
                        DV.inc(v.tensor_copy(GrS[:, a * 1024:a * 1024 + 512],
                                             pgr[:, 0:512]))
                        DV.inc(v.tensor_copy(
                            GrS[:, a * 1024 + 512:(a + 1) * 1024],
                            pgr[:, 512:1024]))
                        DV.inc(v.tensor_copy(pstg[0], pga[:]))
                        v.wait_ge(semDV, DV.v)
                        DV.inc(v.tensor_sub(GipS[:, a * 1024:a * 1024 + 512],
                                            pstg[0][:, 0:512], pgb[:, 0:512]))
                        DV.inc(v.tensor_sub(
                            GipS[:, a * 1024 + 512:(a + 1) * 1024],
                            pstg[0][:, 512:1024], pgb[:, 512:1024]))
                        assert DV.v == dvP[("p0ev", a)]

            @blk.sync
            def _(s):
                for t in range(NTT):
                    if t >= 2:
                        s.wait_ge(semDV, dv_tile_base(t - 2) + 7)
                    sX[t % 2].inc(s.dma_start(
                        out=(x0 if t % 2 == 0 else x1),
                        in_=x_d[t * 128:(t + 1) * 128, :]), 16)
                    assert sX[t % 2].v == mx1a[t]

            @blk.gpsimd
            def _(g):
                g.wait_ge(semIN, in_done)
                for a in range(8):
                    base = (a % 2) * 2048
                    if a >= 2:
                        g.wait_ge(semTE, teP[("p0", a - 2)])
                    L[a % 2].inc(g.dma_start(
                        out=A16[:, base:base + 1024],
                        in_=wr_d[:, a, :, :].rearrange("p k m -> p (k m)")),
                        16)
                    L[a % 2].inc(g.dma_start(
                        out=A16[:, base + 1024:base + 2048],
                        in_=wi_d[:, a, :, :].rearrange("p k m -> p (k m)")),
                        16)
                    assert L[a % 2].v == mL1[a]

        p1a = dict(dv=DV.v, te=TE.v, ac=AC.v)

        if phases < 2:
            _final(nc, [(semIN, IN.v), (semX0, sX[0].v), (semX1, sX[1].v)])
            return nc

        p0s1 = dict(dv=DV.v, te=TE.v)

        mtst = [qstage[:, 0:512], qstage[:, 512:1024]]
        mST2 = {}
        _stv = ST.v
        for a in range(8):
            _stv += 32
            mST2[a] = _stv
        with (
            nc.psum_tensor("pmt0", [128, 1024], F32) as pmt0,
            nc.psum_tensor("pmt1", [128, 1024], F32) as pmt1,
            nc.Block() as blk,
        ):
            teB, dvB = TE.v, DV.v

            @blk.tensor
            def _(te):
                te.wait_ge(semDV, p0s1["dv"])
                for a in range(8):
                    pm = pmt0 if a % 2 == 0 else pmt1
                    if a >= 2:
                        te.wait_ge(semDV, dvB + (a - 1) * 2)
                    for k in range(8):
                        grs = GrS[:, k * 1024 + a * 128:
                                  k * 1024 + (a + 1) * 128]
                        gis = GipS[:, k * 1024 + a * 128:
                                   k * 1024 + (a + 1) * 128]
                        C = [CSR[:, k * 1024 + h * 512:k * 1024 + (h + 1) * 512]
                             for h in range(2)]
                        S = [CSR[:, 8192 + k * 1024 + h * 512:
                                 8192 + k * 1024 + (h + 1) * 512]
                             for h in range(2)]
                        st, sp = (k == 0), (k == 7)
                        te.matmul(pm[:, 0:512], grs, C[0], start=st, stop=False)
                        te.matmul(pm[:, 512:1024], grs, C[1], start=st,
                                  stop=False)
                        te.matmul(pm[:, 0:512], gis, S[0], start=False, stop=sp)
                        TE.inc(te.matmul(pm[:, 512:1024], gis, S[1],
                                         start=False, stop=sp))

            @blk.vector
            def _(v):
                for a in range(8):
                    pm = pmt0 if a % 2 == 0 else pmt1
                    v.wait_ge(semTE, teB + (a + 1) * 8)
                    if a >= 1:
                        v.wait_ge(semST, mST2[a - 1])
                    DV.inc(v.tensor_scalar_mul(mtst[0], pm[:, 0:512], 1.0 / D))
                    v.wait_ge(semDV, DV.v)
                    DV.inc(v.tensor_scalar_mul(mtst[1], pm[:, 512:1024],
                                               1.0 / D))

            @blk.gpsimd
            def _(g):
                for a in range(8):
                    g.wait_ge(semDV, dvB + (a + 1) * 2)
                    ST.inc(g.dma_start(out=mt_d[a * 128:(a + 1) * 128, 0:512],
                                       in_=mtst[0]), 16)
                    ST.inc(g.dma_start(
                        out=mt_d[a * 128:(a + 1) * 128, 512:1024],
                        in_=mtst[1]), 16)
                    assert ST.v == mST2[a]

        p0 = dict(dv=DV.v, te=TE.v, st=ST.v)

        if phases < 3:
            _final(nc, [(semST, ST.v)])
            return nc

        mPA1b = {}
        _pav = [c.v for c in PA]
        for t in range(NTT):
            _pav[t % 2] += 16
            mPA1b[t] = _pav[t % 2]
        with (
            nc.psum_tensor("pf0", [128, 1024], F32) as pf0,
            nc.psum_tensor("pf1", [128, 1024], F32) as pf1,
            nc.Block() as blk,
        ):
            teB, dvB = TE.v, DV.v
            wt_mt = WT.v + 8 * 16

            @blk.tensor
            def _(te):
                te.wait_ge(semWT, wt_mt)
                te.wait_ge(semDV, dv_bz)
                for t in range(NTT):
                    pf = pf0 if t % 2 == 0 else pf1
                    if t >= 2:
                        te.wait_ge(semDV, dvB + (t - 1))
                    for k in range(8):
                        lh = XNT[:, k * T + t * 128:k * T + (t + 1) * 128]
                        st = (k == 0)
                        te.matmul(pf[:, 0:512], lh,
                                  A16[:, k * 1024:k * 1024 + 512],
                                  start=st, stop=False)
                        te.matmul(pf[:, 512:1024], lh,
                                  A16[:, k * 1024 + 512:(k + 1) * 1024],
                                  start=st, stop=False)
                    te.matmul(pf[:, 0:512], eb, bz[:, 0:512],
                              start=False, stop=True)
                    TE.inc(te.matmul(pf[:, 512:1024], eb, bz[:, 512:1024],
                                     start=False, stop=True))

            @blk.vector
            def _(v):
                for t in range(NTT):
                    pf = pf0 if t % 2 == 0 else pf1
                    v.wait_ge(semTE, teB + t + 1)
                    if t >= 2:
                        v.wait_ge(semPA[t % 2], mPA1b[t - 2])
                    DV.inc(v.tensor_copy(pstg[t % 2], pf[:]))

            @blk.gpsimd
            def _(g):
                g.wait_ge(semTE, p0["te"])
                g.wait_ge(semST, p0["st"])
                for k in range(8):
                    WT.inc(g.dma_start(
                        out=A16[:, k * 1024:(k + 1) * 1024],
                        in_=mt_d[k * 128:(k + 1) * 128, :]), 16)
                assert WT.v == wt_mt
                for t in range(NTT):
                    g.wait_ge(semDV, dvB + t + 1)
                    PA[t % 2].inc(g.dma_start(
                        out=part_d[t * 128:(t + 1) * 128, :],
                        in_=pstg[t % 2]), 16)
                    assert PA[t % 2].v == mPA1b[t]

        p1b = dict(dv=DV.v, te=TE.v)
        pa_1b_final = [PA[0].v, PA[1].v]

        if phases < 4:
            _final(nc, [(semPA0, PA[0].v), (semPA1, PA[1].v)])
            return nc

        mQS = {}
        _qv = [c.v for c in QSm]
        for n in range(64):
            _qv[n % 2] += 16
            mQS[n] = _qv[n % 2]
        mVS = {}
        _vv = [c.v for c in VS]
        for t in range(NTT):
            _vv[t % 2] += 16
            mVS[t] = _vv[t % 2]
        with (
            nc.psum_tensor("pq0", [128, 512], F32) as pq0,
            nc.psum_tensor("pq1", [128, 512], F32) as pq1,
            nc.psum_tensor("pv0", [128, 1024], F32) as pv0,
            nc.psum_tensor("pv1", [128, 1024], F32) as pv1,
            nc.Block() as blk,
        ):
            teB, dvB = TE.v, DV.v
            wt_wq = WT.v + 8 * 16
            wt_wk = wt_wq + 8 * 16
            wt_wv = wt_wk + 8 * 16

            @blk.tensor
            def _(te):
                te.wait_ge(semDV, p1b["dv"])
                te.wait_ge(semWT, wt_wq)
                for c in range(NP):
                    for t8 in range(8):
                        n = c * 8 + t8
                        pq = pq0 if n % 2 == 0 else pq1
                        if n >= 2:
                            te.wait_ge(semDV, dvB + n - 1)
                        last = None
                        for k in range(8):
                            last = te.matmul(
                                pq[:],
                                A16[:, k * 1024 + c * 128:
                                    k * 1024 + (c + 1) * 128],
                                XNT[:, k * T + t8 * 512:k * T + (t8 + 1) * 512],
                                start=(k == 0), stop=(k == 7))
                        TE.inc(last)
                te.wait_ge(semWT, wt_wk)
                for c in range(NP):
                    for t8 in range(8):
                        n = c * 8 + t8
                        pq = pq0 if n % 2 == 0 else pq1
                        if n >= 2:
                            te.wait_ge(semDV, dvB + 64 + n - 1)
                        last = None
                        for k in range(8):
                            last = te.matmul(
                                pq[:],
                                A16[:, k * 1024 + c * 128:
                                    k * 1024 + (c + 1) * 128],
                                XNT[:, k * T + t8 * 512:k * T + (t8 + 1) * 512],
                                start=(k == 0), stop=(k == 7))
                        TE.inc(last)
                te.wait_ge(semWT, wt_wv)
                te.wait_ge(semIN, in_done)
                for t in range(NTT):
                    pv = pv0 if t % 2 == 0 else pv1
                    if t >= 2:
                        te.wait_ge(semDV, dvB + 128 + t - 1)
                    for k in range(8):
                        lh = XNT[:, k * T + t * 128:k * T + (t + 1) * 128]
                        st = (k == 0)
                        te.matmul(pv[:, 0:512], lh,
                                  A16[:, k * 1024:k * 1024 + 512],
                                  start=st, stop=False)
                        te.matmul(pv[:, 512:1024], lh,
                                  A16[:, k * 1024 + 512:(k + 1) * 1024],
                                  start=st, stop=False)
                    te.matmul(pv[:, 0:512], eb, bvz[:, 0:512],
                              start=False, stop=True)
                    TE.inc(te.matmul(pv[:, 512:1024], eb, bvz[:, 512:1024],
                                     start=False, stop=True))

            @blk.vector
            def _(v):
                for c in range(NP):
                    for t8 in range(8):
                        n = c * 8 + t8
                        pq = pq0 if n % 2 == 0 else pq1
                        v.wait_ge(semTE, teB + n + 1)
                        if n >= 2:
                            v.wait_ge(semQS[n % 2], mQS[n - 2])
                        DV.inc(v.tensor_scalar(
                            qstage[:, (n % 2) * 512:(n % 2) * 512 + 512],
                            pq[:], bqk[:, c:c + 1], None, ALU.add))
                for c in range(NP):
                    for t8 in range(8):
                        n = c * 8 + t8
                        pq = pq0 if n % 2 == 0 else pq1
                        v.wait_ge(semTE, teB + 64 + n + 1)
                        DV.inc(v.tensor_scalar(
                            KT[:, c * T + t8 * 512:c * T + (t8 + 1) * 512],
                            pq[:], bqk[:, 8 + c:9 + c], None, ALU.add))
                for t in range(NTT):
                    pv = pv0 if t % 2 == 0 else pv1
                    v.wait_ge(semTE, teB + 128 + t + 1)
                    if t >= 2:
                        v.wait_ge(semVS[t % 2], mVS[t - 2])
                    DV.inc(v.tensor_copy(
                        vstage[:, (t % 2) * 1024:(t % 2) * 1024 + 1024],
                        pv[:]))

            @blk.gpsimd
            def _(g):
                g.wait_ge(semTE, p1b["te"])
                for k in range(8):
                    WT.inc(g.dma_start(
                        out=A16[:, k * 1024:(k + 1) * 1024],
                        in_=wqT_d[k * 128:(k + 1) * 128, :]), 16)
                assert WT.v == wt_wq
                for n in range(64):
                    c, t8 = n // 8, n % 8
                    g.wait_ge(semDV, dvB + n + 1)
                    QSm[n % 2].inc(g.dma_start(
                        out=q_d[c, :, t8 * 512:(t8 + 1) * 512],
                        in_=qstage[:, (n % 2) * 512:(n % 2) * 512 + 512]), 16)
                    assert QSm[n % 2].v == mQS[n]
                g.wait_ge(semTE, teB + 64)
                for k in range(8):
                    WT.inc(g.dma_start(
                        out=A16[:, k * 1024:(k + 1) * 1024],
                        in_=wkT_d[k * 128:(k + 1) * 128, :]), 16)
                assert WT.v == wt_wk
                g.wait_ge(semTE, teB + 128)
                for k in range(8):
                    WT.inc(g.dma_start(
                        out=A16[:, k * 1024:(k + 1) * 1024],
                        in_=wvT_d[k * 128:(k + 1) * 128, :]), 16)
                assert WT.v == wt_wv
                for t in range(NTT):
                    g.wait_ge(semDV, dvB + 128 + t + 1)
                    VS[t % 2].inc(g.dma_start(
                        out=v_d[t * 128:(t + 1) * 128, :],
                        in_=vstage[:, (t % 2) * 1024:(t % 2) * 1024 + 1024]),
                        16)
                    assert VS[t % 2].v == mVS[t]

        p2 = dict(dv=DV.v, te=TE.v)
        vs_final = [VS[0].v, VS[1].v]
        qs_final = [QSm[0].v, QSm[1].v]

        if phases < 5:
            _final(nc, [(semVS0, VS[0].v), (semVS1, VS[1].v),
                        (semQS0, QSm[0].v), (semQS1, QSm[1].v)])
            return nc

        with nc.Block() as blk:
            @blk.gpsimd
            def _(g):
                g.wait_ge(semQS0, qs_final[0])
                g.wait_ge(semQS1, qs_final[1])
                g.wait_ge(semTE, p2["te"])
                for pr in range(NP):
                    QT.inc(g.dma_start(
                        out=QTs[:, pr * 512:(pr + 1) * 512],
                        in_=q_d[pr, :, 0:512]), 16)
        qt_marks = {0: QT.v}

        def plan_dv_p3(base):
            m = {}
            v = base
            for pr in range(NP):
                for kc in range(NKC):
                    v += 1
                    m[("ps", pr, kc)] = v
                if pr >= 1:
                    v += 1
                    m[("scl", pr - 1)] = v
                    v += 1
                    m[("ot", pr - 1)] = v
                v += 3
                m[("zch", pr)] = v
            v += 1
            m[("scl", 7)] = v
            v += 1
            m[("ot", 7)] = v
            for tt in range(4):
                v += 2
                m[("ev", tt)] = v
            m["end"] = v
            return m

        P3 = {}
        _dvc = DV.v
        _acc = AC.v
        _vcc = [c.v for c in VC]
        _sxc = [c.v for c in sX]
        _woc = [c.v for c in WO]
        _pac = [c.v for c in PA]
        _ouc = [c.v for c in OU]
        _qtc = QT.v
        for q in range(NQS):
            pl = {}
            pl["dvM"] = plan_dv_p3(_dvc)
            _dvc = pl["dvM"]["end"]
            pl["acB"] = _acc
            _acc += NP * NKC
            pl["mVC"] = {}
            for kc in range(NKC):
                _vcc[kc % 4] += 16
                pl["mVC"][kc] = _vcc[kc % 4]
            pl["mXq"] = {}
            for tt in range(4):
                _sxc[tt % 2] += 16
                pl["mXq"][tt] = _sxc[tt % 2]
            pl["mWOq"] = {}
            for j in range(2 * NP):
                _woc[j % 2] += 16
                pl["mWOq"][j] = _woc[j % 2]
            pl["mPAq"] = {}
            for tt in range(4):
                _pac[tt % 2] += 16
                pl["mPAq"][tt] = _pac[tt % 2]
            pl["mOUq"] = {}
            for tt in range(4):
                _ouc[tt % 2] += 16
                pl["mOUq"][tt] = _ouc[tt % 2]
            if q + 1 < NQS:
                _qtc += NP * 16
                pl["qt_next"] = _qtc
            P3[q] = pl
        P3[0]["qt_ready"] = qt_marks[0]
        for q in range(1, NQS):
            P3[q]["qt_ready"] = P3[q - 1]["qt_next"]

        teM = {}

        with (
            nc.psum_tensor("pSA", [128, 1024], F32) as pSA,
            nc.psum_tensor("pSB", [128, 1024], F32) as pSB,
            nc.psum_tensor("pOu0", [128, 512], F32) as pOu0,
            nc.psum_tensor("pOu1", [128, 512], F32) as pOu1,
            nc.psum_tensor("pOu2", [128, 512], F32) as pOu2,
            nc.psum_tensor("pzb", [128, 512], F32) as pzb,
            nc.Block() as blk,
        ):
            SA = [pSA, pSB]
            pOu = [pOu0, pOu1, pOu2]
            pz = pzb[0:2, :]
            pbc = pzb
            pes = [pSA, pSB]

            @blk.tensor
            def _(te):
                for q in range(NQS):
                    pl = P3[q]
                    dvM = pl["dvM"]
                    acB = pl["acB"]
                    dvM_prev = P3[q - 1]["dvM"] if q > 0 else None

                    if q == 0:
                        te.wait_ge(semDV, p2["dv"])
                    else:
                        te.wait_ge(semDV, dvM_prev[("ev", 3)])
                    te.wait_ge(semQT, pl["qt_ready"])

                    def emit_sc(u):
                        pr, kc = u // NKC, u % NKC
                        if u >= 2:
                            te.wait_ge(semAC, acB + u - 1)
                        sset = SA[u % 2]
                        te.matmul(
                            sset[:, 0:512],
                            KT[0:64, pr * T + kc * 128:pr * T + (kc + 1) * 128],
                            QTs[0:64, pr * 512:(pr + 1) * 512],
                            start=True, stop=True)
                        last = te.matmul(
                            sset[:, 512:1024],
                            KT[64:128, pr * T + kc * 128:
                               pr * T + (kc + 1) * 128],
                            QTs[64:128, pr * 512:(pr + 1) * 512],
                            start=True, stop=True, tile_position=(64, 0))
                        teM[(q, "sc", u)] = TE.inc(last)

                    def emit_pv(u):
                        pr, kc = u // NKC, u % NKC
                        te.wait_ge(semAC, acB + u + 1)
                        te.wait_ge(semVC[kc % 4], pl["mVC"][kc])
                        if kc == 0:
                            if pr >= 3:
                                te.wait_ge(semDV, dvM[("ot", pr - 3)])
                            elif q > 0:
                                te.wait_ge(semDV,
                                           dvM_prev[("ot", {0: 6, 1: 7,
                                                            2: 5}[pr])])
                        pt = PT[u % 4]
                        po = pOu[pr % 3]
                        st, sp = (kc == 0), (kc == NKC - 1)
                        te.matmul(po[0:64, :],
                                  VQ[:, kc * 1024 + pr * 128:
                                     kc * 1024 + pr * 128 + 64],
                                  pt[:, 0:512], start=st, stop=sp,
                                  skip_group_check=True)
                        last = te.matmul(
                            po[64:128, :],
                            VQ[:, kc * 1024 + pr * 128 + 64:
                               kc * 1024 + (pr + 1) * 128],
                            pt[:, 512:1024], start=st, stop=sp,
                            tile_position=(0, 64), skip_group_check=True)
                        teM[(q, "pv", u)] = TE.inc(last)

                    def emit_zmm(pr):
                        te.wait_ge(semDV, dvM[("ps", pr, NKC - 1)])
                        if pr >= 1:
                            te.wait_ge(semDV, dvM[("scl", pr - 1)])
                        elif q > 0:
                            te.wait_ge(semDV, dvM_prev[("scl", 7)])
                        te.matmul(pz, zsel[:, 0:2], PS[0][:, 0:512],
                                  start=True, stop=False)
                        te.matmul(pz, zsel[:, 1:3], PS[0][:, 512:1024],
                                  start=False, stop=False)
                        te.matmul(pz, zsel[:, 0:2], PS[1][:, 0:512],
                                  start=False, stop=False)
                        last = te.matmul(pz, zsel[:, 1:3],
                                         PS[1][:, 512:1024],
                                         start=False, stop=True)
                        teM[(q, "zmm", pr)] = TE.inc(last)

                    def emit_bc(pr):
                        te.wait_ge(semDV, dvM[("zch", pr)])
                        if pr >= 1:
                            te.wait_ge(semDV, dvM[("scl", pr - 1)])
                        elif q > 0:
                            te.wait_ge(semDV, dvM_prev[("scl", 7)])
                        teM[(q, "bc", pr)] = TE.inc(
                            te.matmul(pbc[:], e2, zpad, start=True, stop=True,
                                      skip_group_check=True))

                    for u in range(NP * NKC):
                        emit_sc(u)
                        if u >= 2:
                            emit_pv(u - 2)
                            if (u - 2) % NKC == NKC - 1:
                                prd = (u - 2) // NKC
                                if prd >= 1:
                                    emit_bc(prd - 1)
                                emit_zmm(prd)
                    emit_pv(NP * NKC - 2)
                    emit_pv(NP * NKC - 1)
                    emit_bc(NP - 2)
                    emit_zmm(NP - 1)
                    emit_bc(NP - 1)

                    for i in range(2 * NP):
                        wave, pc = i // NP, i % NP
                        if i == 0:
                            te.wait_ge(semAC, acB + 256)
                            te.wait_ge(semDV, dvM[("ot", 7)])
                        if i == NP:
                            te.wait_ge(semDV, dvM[("ev", 1)])
                        te.wait_ge(semWO[i % 2], pl["mWOq"][i])
                        ws = wo_sb[:, (i % 2) * 1024:(i % 2) * 1024 + 1024]
                        last = None
                        for w2 in range(2):
                            tt = wave * 2 + w2
                            lh = OTr[:, pc * 512 + tt * 128:
                                     pc * 512 + (tt + 1) * 128]
                            st, sp = (pc == 0), (pc == NP - 1)
                            te.matmul(pes[w2][:, 0:512], lh, ws[:, 0:512],
                                      start=st, stop=sp)
                            last = te.matmul(pes[w2][:, 512:1024], lh,
                                             ws[:, 512:1024], start=st, stop=sp)
                        teM[(q, "p4g", i)] = TE.inc(last)

            @blk.scalar
            def _(sc):
                for q in range(NQS):
                    pl = P3[q]
                    acB = pl["acB"]
                    dvM = pl["dvM"]
                    dvM_prev = P3[q - 1]["dvM"] if q > 0 else None
                    for u in range(NP * NKC):
                        sc.wait_ge(semTE, teM[(q, "sc", u)])
                        if u >= 4:
                            if u % 2 == 0:
                                sc.wait_ge(semTE, teM[(q, "pv", u - 3)])
                        elif q > 0:
                            sc.wait_ge(semTE, teM[(q - 1, "pv", 252 + u)])
                        if u >= 2:
                            sc.wait_ge(semDV,
                                       dvM[("ps", (u - 2) // NKC,
                                            (u - 2) % NKC)])
                        elif q > 0:
                            sc.wait_ge(semDV, dvM_prev[("ps", 7, 30 + u)])
                        AC.inc(sc.activation(PT[u % 4], SA[u % 2][:],
                                             AF.Exp, bias=lnbias, scale=0.125))
                    assert AC.v == acB + 256

            @blk.vector
            def _(v):
                for q in range(NQS):
                    pl = P3[q]
                    acB = pl["acB"]
                    dvM = pl["dvM"]
                    for pr in range(NP):
                        for kc in range(NKC):
                            u = pr * NKC + kc
                            v.wait_ge(semAC, acB + u + 1)
                            if kc <= 1:
                                if pr >= 1:
                                    v.wait_ge(semTE, teM[(q, "zmm", pr - 1)])
                                elif q > 0:
                                    v.wait_ge(semTE, teM[(q - 1, "zmm", 7)])
                            if kc <= 1:
                                DV.inc(v.tensor_copy(PS[kc % 2], PT[u % 4]))
                            else:
                                DV.inc(v.tensor_add(PS[kc % 2], PS[kc % 2],
                                                    PT[u % 4]))
                        if pr >= 1:
                            v.wait_ge(semTE, teM[(q, "bc", pr - 1)])
                            v.wait_ge(semDV, DV.v)
                            DV.inc(v.tensor_copy(sclt, pbc[:]))
                            v.wait_ge(semDV, DV.v)
                            v.wait_ge(semTE, teM[(q, "pv", (pr - 1) * NKC + 31)])
                            DV.inc(v.tensor_mul(
                                OTr[:, (pr - 1) * 512:pr * 512],
                                pOu[(pr - 1) % 3][:], sclt))
                        v.wait_ge(semTE, teM[(q, "zmm", pr)])
                        v.wait_ge(semDV, DV.v)
                        DV.inc(v.tensor_copy(zf32, pz))
                        v.wait_ge(semDV, DV.v)
                        DV.inc(v.reciprocal(rcpt, zf32))
                        v.wait_ge(semDV, DV.v)
                        if pr >= 1:
                            v.wait_ge(semTE, teM[(q, "bc", pr - 1)])
                        elif q > 0:
                            v.wait_ge(semTE, teM[(q - 1, "bc", 7)])
                        DV.inc(v.tensor_copy(zpad[0:2, :], rcpt))
                        assert DV.v == dvM[("zch", pr)]
                    v.wait_ge(semTE, teM[(q, "bc", 7)])
                    v.wait_ge(semDV, DV.v)
                    DV.inc(v.tensor_copy(sclt, pbc[:]))
                    v.wait_ge(semDV, DV.v)
                    v.wait_ge(semTE, teM[(q, "pv", 7 * NKC + 31)])
                    DV.inc(v.tensor_mul(OTr[:, 7 * 512:8 * 512],
                                        pOu[7 % 3][:], sclt))
                    for tt in range(4):
                        v.wait_ge(semTE, teM[(q, "p4g", 7 if tt < 2 else 15)])
                        v.wait_ge(semPA[tt % 2], pl["mPAq"][tt])
                        v.wait_ge(semDV, DV.v)
                        DV.inc(v.tensor_add(xsq, pes[tt % 2][:], pstg[tt % 2]))
                        v.wait_ge(semX[tt % 2], pl["mXq"][tt])
                        v.wait_ge(semDV, DV.v)
                        DV.inc(v.tensor_add(pstg[tt % 2], xsq,
                                            x0 if tt % 2 == 0 else x1))
                    assert DV.v == dvM[("ev", 3)]

            @blk.sync
            def _(s):
                for q in range(NQS):
                    pl = P3[q]
                    dvM = pl["dvM"]
                    dvM_prev = P3[q - 1]["dvM"] if q > 0 else None
                    for kc in range(NKC):
                        if kc == 0:
                            if q == 0:
                                s.wait_ge(semTE, p2["te"])
                                s.wait_ge(semVS0, vs_final[0])
                                s.wait_ge(semVS1, vs_final[1])
                            else:
                                s.wait_ge(semTE, teM[(q - 1, "pv", 255)])
                        if kc >= 4:
                            s.wait_ge(semTE, teM[(q, "pv", kc - 4)])
                        VC[kc % 4].inc(s.dma_start(
                            out=VQ[:, kc * 1024:(kc + 1) * 1024],
                            in_=v_d[kc * 128:(kc + 1) * 128, :]), 16)
                        assert VC[kc % 4].v == pl["mVC"][kc]
                    for tt in range(4):
                        if tt >= 2:
                            s.wait_ge(semDV, dvM[("ev", tt - 2)])
                        elif q > 0:
                            s.wait_ge(semDV, dvM_prev[("ev", tt + 2)])
                        sX[tt % 2].inc(s.dma_start(
                            out=(x0 if tt % 2 == 0 else x1),
                            in_=x_d[q * QS + tt * 128:
                                    q * QS + (tt + 1) * 128, :]), 16)
                        assert sX[tt % 2].v == pl["mXq"][tt]
                    for tt in range(4):
                        s.wait_ge(semDV, dvM[("ev", tt)])
                        OU[tt % 2].inc(s.dma_start(
                            out=out_d[q * QS + tt * 128:
                                      q * QS + (tt + 1) * 128, :],
                            in_=pstg[tt % 2]), 16)
                        assert OU[tt % 2].v == pl["mOUq"][tt]

            @blk.gpsimd
            def _(g):
                for q in range(NQS):
                    pl = P3[q]
                    for tt in range(2):
                        if q == 0:
                            g.wait_ge(semPA[tt % 2], pa_1b_final[tt % 2])
                        else:
                            g.wait_ge(semOU[tt % 2], P3[q - 1]["mOUq"][tt + 2])
                        PA[tt % 2].inc(g.dma_start(
                            out=pstg[tt % 2],
                            in_=part_d[q * QS + tt * 128:
                                       q * QS + (tt + 1) * 128, :]), 16)
                        assert PA[tt % 2].v == pl["mPAq"][tt]
                    if q + 1 < NQS:
                        g.wait_ge(semTE, teM[(q, "sc", 255)])
                        for pr in range(NP):
                            QT.inc(g.dma_start(
                                out=QTs[:, pr * 512:(pr + 1) * 512],
                                in_=q_d[pr, :, (q + 1) * 512:(q + 2) * 512]),
                                16)
                        assert QT.v == pl["qt_next"]
                    for i in range(2 * NP):
                        if i >= 2:
                            g.wait_ge(semTE, teM[(q, "p4g", i - 2)])
                        elif q > 0:
                            g.wait_ge(semTE, teM[(q - 1, "p4g", 14 + i)])
                        WO[i % 2].inc(g.dma_start(
                            out=wo_sb[:, (i % 2) * 1024:(i % 2) * 1024 + 1024],
                            in_=woT_d[(i % NP) * 128:(i % NP + 1) * 128, :]),
                            16)
                        assert WO[i % 2].v == pl["mWOq"][i]
                    for tt in range(2, 4):
                        g.wait_ge(semOU[tt % 2], pl["mOUq"][tt - 2])
                        PA[tt % 2].inc(g.dma_start(
                            out=pstg[tt % 2],
                            in_=part_d[q * QS + tt * 128:
                                       q * QS + (tt + 1) * 128, :]), 16)
                        assert PA[tt % 2].v == pl["mPAq"][tt]

        assert DV.v == P3[NQS - 1]["dvM"]["end"]

        _final(nc, [(semOU0, OU[0].v), (semOU1, OU[1].v),
                    (semVC0, VC[0].v), (semVC1, VC[1].v),
                    (semVC2, VC[2].v), (semVC3, VC[3].v),
                    (semPA0, PA[0].v), (semPA1, PA[1].v),
                    (semQT, QT.v)])
    return nc


def _host_constants():
    j = np.arange(D, dtype=np.float64)
    ang = 2.0 * np.pi * np.outer(j, j) / D
    cmat = np.cos(ang).astype(np.float32).astype(np.float16)
    smat = np.sin(ang).astype(np.float32).astype(np.float16)
    ident = np.eye(128, dtype=np.float16)
    e2 = np.zeros((128, 128), np.float16)
    e2[0, 0:64] = 1.0
    e2[1, 64:128] = 1.0
    eb = np.zeros((128, 128), np.float16)
    eb[0, :] = 1.0
    zsel = np.zeros((128, 4), np.float16)
    zsel[:, 0] = 1.0
    zsel[:, 2] = 1.0
    return cmat, smat, ident, e2, eb, zsel


def kernel(L0_attn_hidden_states, L0_attn_Wq, L0_attn_bq, L0_attn_Wk,
           L0_attn_bk, L0_attn_Wv, L0_attn_bv, L0_attn_Wo, L0_attn_bo,
           L0_ffn_w_freq_real, L0_ffn_w_freq_imag, L0_ffn_bias):
    x = np.ascontiguousarray(np.asarray(L0_attn_hidden_states, np.float32))
    cmat, smat, ident, e2, eb, zsel = _host_constants()
    bqkT = np.ascontiguousarray(np.concatenate(
        [np.asarray(L0_attn_bq, np.float32).reshape(8, 128).T,
         np.asarray(L0_attn_bk, np.float32).reshape(8, 128).T], axis=1))
    shared = {
        "wqT": np.ascontiguousarray(np.asarray(L0_attn_Wq, np.float32).T),
        "wkT": np.ascontiguousarray(np.asarray(L0_attn_Wk, np.float32).T),
        "wvT": np.ascontiguousarray(np.asarray(L0_attn_Wv, np.float32).T),
        "woT": np.ascontiguousarray(np.asarray(L0_attn_Wo, np.float32).T),
        "bqkT": bqkT,
        "bo_row": np.asarray(L0_attn_bo, np.float32).reshape(1, D),
        "fb_row": np.asarray(L0_ffn_bias, np.float32).reshape(1, D),
        "bv_row": np.asarray(L0_attn_bv, np.float32).reshape(1, D),
        "wr": np.ascontiguousarray(
            np.asarray(L0_ffn_w_freq_real, np.float32)
            .reshape(8, 128, 8, 128).transpose(1, 2, 0, 3)),
        "wi": np.ascontiguousarray(
            np.asarray(L0_ffn_w_freq_imag, np.float32)
            .reshape(8, 128, 8, 128).transpose(1, 2, 0, 3)),
        "cmat": cmat, "smat": smat,
        "ident": ident, "e2": e2, "eb": eb, "zsel": zsel,
    }
    in_maps = []
    for b in range(8):
        m = dict(shared)
        m["x"] = np.ascontiguousarray(x[b])
        in_maps.append(m)
    nc = build_nc()
    res = run_bass_kernel_spmd(nc, in_maps, core_ids=list(range(8)))
    out = np.stack([res.results[b]["out"] for b in range(8)], axis=0)
    return out.astype(np.float32)


# version 14
# speedup vs baseline: 1.0573x; 1.0535x over previous
import os as _os
import contextlib
import numpy as np
import concourse.bass as bass
import concourse.mybir as mybir
from concourse.bass_utils import run_bass_kernel_spmd

F32 = mybir.dt.float32
F16 = mybir.dt.float16
AF = mybir.ActivationFunctionType
ALU = mybir.AluOpType
AX = mybir.AxisListType

T = 4096
D = 1024
NP = 8
QS = 512
NQS = T // QS
NKC = 32
NTT = T // 128
LN64 = float(np.log(64.0))


class Ctr:
    def __init__(self, sem):
        self.sem = sem
        self.v = 0

    def inc(self, inst, n=1):
        inst.then_inc(self.sem, n)
        self.v += n
        return self.v


def _final(nc, waits):
    with nc.Block() as blk:
        @blk.sync
        def _(s):
            for sem, v in waits:
                if v > 0:
                    s.wait_ge(sem, v)


def build_nc(phases=99):
    nc = bass.Bass("TRN2", target_bir_lowering=False)

    x_d = nc.dram_tensor("x", [T, D], F32, kind="ExternalInput").ap()
    wqT_d = nc.dram_tensor("wqT", [D, D], F32, kind="ExternalInput").ap()
    wkT_d = nc.dram_tensor("wkT", [D, D], F32, kind="ExternalInput").ap()
    wvT_d = nc.dram_tensor("wvT", [D, D], F32, kind="ExternalInput").ap()
    woT_d = nc.dram_tensor("woT", [D, D], F32, kind="ExternalInput").ap()
    bqk_d = nc.dram_tensor("bqkT", [128, 16], F32, kind="ExternalInput").ap()
    bo_d = nc.dram_tensor("bo_row", [1, D], F32, kind="ExternalInput").ap()
    fb_d = nc.dram_tensor("fb_row", [1, D], F32, kind="ExternalInput").ap()
    bv_d = nc.dram_tensor("bv_row", [1, D], F32, kind="ExternalInput").ap()
    wr_d = nc.dram_tensor("wr", [128, 8, 8, 128], F32, kind="ExternalInput").ap()
    wi_d = nc.dram_tensor("wi", [128, 8, 8, 128], F32, kind="ExternalInput").ap()
    c_d = nc.dram_tensor("cmat", [D, D], F16, kind="ExternalInput").ap()
    s_d = nc.dram_tensor("smat", [D, D], F16, kind="ExternalInput").ap()
    ident_d = nc.dram_tensor("ident", [128, 128], F16, kind="ExternalInput").ap()
    e2_d = nc.dram_tensor("e2", [128, 128], F16, kind="ExternalInput").ap()
    eb_d = nc.dram_tensor("eb", [128, 128], F16, kind="ExternalInput").ap()
    zsel_d = nc.dram_tensor("zsel", [128, 4], F16, kind="ExternalInput").ap()

    out_d = nc.dram_tensor("out", [T, D], F32, kind="ExternalOutput").ap()

    q_d = nc.dram_tensor("q_scr", [NP, 128, T], F16, kind="Internal").ap()
    v_d = nc.dram_tensor("v_scr", [T, D], F16, kind="Internal").ap()
    mt_d = nc.dram_tensor("mt_scr", [D, D], F16, kind="Internal").ap()
    gr_d = nc.dram_tensor("gr_scr", [D, D], F32, kind="Internal").ap()
    gip_d = nc.dram_tensor("gip_scr", [D, D], F32, kind="Internal").ap()
    part_d = nc.dram_tensor("part_scr", [T, D], F32, kind="Internal").ap()

    bigA_h = nc.alloc_sbuf_tensor("bigA", [128, NP * T], F16)
    bigB_h = nc.alloc_sbuf_tensor("bigB", [128, NP * T], F16)
    a16_h = nc.alloc_sbuf_tensor("a16", [128, 12288], F16)
    a32_h = nc.alloc_sbuf_tensor("a32", [128, 5120], F32)

    XNT = bigA_h.ap()
    VQ = XNT
    KT = bigB_h.ap()
    CSR = KT
    GrS = KT[:, 16384:24576]
    GipS = KT[:, 24576:32768]
    A16 = a16_h.ap()
    A32 = a32_h.ap()

    ident = nc.alloc_sbuf_tensor("identt", [128, 128], F16).ap()
    e2 = nc.alloc_sbuf_tensor("e2t", [128, 128], F16).ap()
    eb = nc.alloc_sbuf_tensor("ebt", [128, 128], F16).ap()
    zsel = nc.alloc_sbuf_tensor("zselt", [128, 4], F16).ap()
    bqk = nc.alloc_sbuf_tensor("bqkt", [128, 16], F32).ap()
    bvz = nc.alloc_sbuf_tensor("bvzt", [128, D], F16).ap()
    bz = nc.alloc_sbuf_tensor("bzt", [128, D], F16).ap()
    zpad = nc.alloc_sbuf_tensor("zpadt", [128, QS], F16).ap()
    zf32 = nc.alloc_sbuf_tensor("zf32t", [2, QS], F32).ap()
    rcpt = nc.alloc_sbuf_tensor("rcptt", [2, QS], F32).ap()
    sclt = nc.alloc_sbuf_tensor("scltt", [128, QS], F32).ap()
    wo_sb = nc.alloc_sbuf_tensor("wo_sbt", [128, 2048], F16).ap()
    qstage = nc.alloc_sbuf_tensor("qstaget", [128, 1024], F16).ap()
    vstage = nc.alloc_sbuf_tensor("vstaget", [128, 2048], F16).ap()
    xstage = nc.alloc_sbuf_tensor("xstaget", [128, 1024], F16).ap()
    brow0 = nc.alloc_sbuf_tensor("brow0t", [1, D], F32).ap()
    brow1 = nc.alloc_sbuf_tensor("brow1t", [1, D], F32).ap()
    lnbias = nc.alloc_sbuf_tensor("lnbiast", [128, 1], F32).ap()
    epsb = nc.alloc_sbuf_tensor("epsbt", [128, 1], F32).ap()
    st_s1 = nc.alloc_sbuf_tensor("st_s1t", [128, 1], F32).ap()
    st_s2 = nc.alloc_sbuf_tensor("st_s2t", [128, 1], F32).ap()
    st_mean = nc.alloc_sbuf_tensor("st_meant", [128, 1], F32).ap()
    st_var = nc.alloc_sbuf_tensor("st_vart", [128, 1], F32).ap()
    st_msq = nc.alloc_sbuf_tensor("st_msqt", [128, 1], F32).ap()
    st_lnv = nc.alloc_sbuf_tensor("st_lnvt", [128, 1], F32).ap()
    st_rstd = nc.alloc_sbuf_tensor("st_rstdt", [128, 1], F32).ap()

    x0 = A32[:, 0:1024]
    x1 = A32[:, 1024:2048]
    xsq = A32[:, 2048:3072]
    ev32 = [A32[:, i * 512:(i + 1) * 512] for i in range(4)]
    pstg = [A32[:, 3072 + i * 1024:3072 + (i + 1) * 1024] for i in range(2)]

    QTs = A16[:, 0:4096]
    pt2_h = nc.alloc_sbuf_tensor("pt2t", [128, 1024], F16).ap()
    PT = [A16[:, 4096:5120], A16[:, 5120:6144], pt2_h, xstage]
    PS = [A16[:, 6144:7168], A16[:, 7168:8192]]
    OTr = A16[:, 8192:12288]

    with contextlib.ExitStack() as _stack:
        semTE = _stack.enter_context(nc.semaphore("sTE"))
        semAC = _stack.enter_context(nc.semaphore("sAC"))
        semDV = _stack.enter_context(nc.semaphore("sDV"))
        semIN = _stack.enter_context(nc.semaphore("sIN"))
        semWT = _stack.enter_context(nc.semaphore("sWT"))
        semST = _stack.enter_context(nc.semaphore("sST"))
        semL0 = _stack.enter_context(nc.semaphore("sL0"))
        semL1 = _stack.enter_context(nc.semaphore("sL1"))
        semL2 = _stack.enter_context(nc.semaphore("sL2"))
        semL3 = _stack.enter_context(nc.semaphore("sL3"))
        semX0 = _stack.enter_context(nc.semaphore("sX0"))
        semX1 = _stack.enter_context(nc.semaphore("sX1"))
        semQS0 = _stack.enter_context(nc.semaphore("sQS0"))
        semQS1 = _stack.enter_context(nc.semaphore("sQS1"))
        semVS0 = _stack.enter_context(nc.semaphore("sVS0"))
        semVS1 = _stack.enter_context(nc.semaphore("sVS1"))
        semVC0 = _stack.enter_context(nc.semaphore("sVC0"))
        semVC1 = _stack.enter_context(nc.semaphore("sVC1"))
        semVC2 = _stack.enter_context(nc.semaphore("sVC2"))
        semVC3 = _stack.enter_context(nc.semaphore("sVC3"))
        semQT = _stack.enter_context(nc.semaphore("sQT"))
        semWO0 = _stack.enter_context(nc.semaphore("sWO0"))
        semWO1 = _stack.enter_context(nc.semaphore("sWO1"))
        semPA0 = _stack.enter_context(nc.semaphore("sPA0"))
        semPA1 = _stack.enter_context(nc.semaphore("sPA1"))
        semOU0 = _stack.enter_context(nc.semaphore("sOU0"))
        semOU1 = _stack.enter_context(nc.semaphore("sOU1"))
        TE, AC, DV = Ctr(semTE), Ctr(semAC), Ctr(semDV)
        IN = Ctr(semIN)
        WT = Ctr(semWT)
        ST = Ctr(semST)
        L = [Ctr(semL0), Ctr(semL1), Ctr(semL2), Ctr(semL3)]
        semL = [semL0, semL1, semL2, semL3]
        sX = [Ctr(semX0), Ctr(semX1)]
        semX = [semX0, semX1]
        QSm = [Ctr(semQS0), Ctr(semQS1)]
        semQS = [semQS0, semQS1]
        VS = [Ctr(semVS0), Ctr(semVS1)]
        semVS = [semVS0, semVS1]
        VC = [Ctr(semVC0), Ctr(semVC1), Ctr(semVC2), Ctr(semVC3)]
        semVC = [semVC0, semVC1, semVC2, semVC3]
        QT = Ctr(semQT)
        WO = [Ctr(semWO0), Ctr(semWO1)]
        semWO = [semWO0, semWO1]
        PA = [Ctr(semPA0), Ctr(semPA1)]
        semPA = [semPA0, semPA1]
        OU = [Ctr(semOU0), Ctr(semOU1)]
        semOU = [semOU0, semOU1]

        with nc.Block() as blk:
            @blk.vector
            def _(v):
                DV.inc(v.memset(bvz, 0.0))
                DV.inc(v.memset(bz, 0.0))
                DV.inc(v.memset(zpad, 0.0))
                DV.inc(v.memset(lnbias, -LN64))
                DV.inc(v.memset(epsb, 1e-5))

            @blk.gpsimd
            def _(g):
                for dst, src in [
                    (ident, ident_d), (e2, e2_d), (eb, eb_d), (zsel, zsel_d),
                    (bqk, bqk_d), (brow0, bo_d), (brow1, fb_d),
                ]:
                    IN.inc(g.dma_start(out=dst, in_=src), 16)
                for k in range(8):
                    IN.inc(g.dma_start(out=CSR[:, k * 1024:(k + 1) * 1024],
                                       in_=c_d[k * 128:(k + 1) * 128, :]), 16)
                    IN.inc(g.dma_start(
                        out=CSR[:, 8192 + k * 1024:8192 + (k + 1) * 1024],
                        in_=s_d[k * 128:(k + 1) * 128, :]), 16)
                g.wait_ge(semDV, 3)
                IN.inc(g.dma_start(out=bvz[0:1, :], in_=bv_d), 16)

        in_done = IN.v
        assert in_done == 24 * 16

        with nc.Block() as blk:
            @blk.vector
            def _(v):
                v.wait_ge(semIN, in_done)
                DV.inc(v.tensor_add(brow0, brow0, brow1))
                v.wait_ge(semDV, DV.v)
                DV.inc(v.tensor_copy(bz[0:1, :], brow0))
        dv_bz = DV.v

        NDV, NAC, NT1 = 15, 3, 8
        mx1a = {}
        _sxv = [c.v for c in sX]
        for t in range(NTT):
            _sxv[t % 2] += 16
            mx1a[t] = _sxv[t % 2]
        with (
            nc.psum_tensor("tp0", [128, 128], F16) as tp0,
            nc.psum_tensor("tp1", [128, 128], F16) as tp1,
            nc.psum_tensor("tp2", [128, 128], F16) as tp2,
            nc.psum_tensor("tp3", [128, 128], F16) as tp3,
            nc.Block() as blk,
        ):
            tps = [tp0, tp1, tp2, tp3]
            dvB, acB, teB = DV.v, AC.v, TE.v

            @blk.tensor
            def _(te):
                te.wait_ge(semIN, in_done)
                for t in range(NTT):
                    te.wait_ge(semDV, dvB + t * NDV + 7)
                    for j in range(8):
                        tj = t * 8 + j
                        if tj >= 4:
                            tp_, jp = (tj - 4) // 8, (tj - 4) % 8
                            te.wait_ge(semDV, dvB + tp_ * NDV + 7 + jp + 1)
                        TE.inc(te.transpose(
                            tps[j % 4][:],
                            xstage[:, j * 128:(j + 1) * 128], ident))

            @blk.scalar
            def _(sc):
                for t in range(NTT):
                    xs = x0 if t % 2 == 0 else x1
                    sc.wait_ge(semX[t % 2], mx1a[t])
                    if t >= 1:
                        sc.wait_ge(semDV, dvB + (t - 1) * NDV + 2)
                    AC.inc(sc.activation(xsq, xs, AF.Square))
                    sc.wait_ge(semDV, dvB + t * NDV + 6)
                    AC.inc(sc.activation(st_lnv, st_var, AF.Ln, bias=epsb))
                    sc.wait_ge(semAC, AC.v)
                    AC.inc(sc.activation(st_rstd, st_lnv, AF.Exp, scale=-0.5))

            @blk.vector
            def _(v):
                for t in range(NTT):
                    xs = x0 if t % 2 == 0 else x1
                    v.wait_ge(semX[t % 2], mx1a[t])
                    DV.inc(v.reduce_sum(st_s1, xs, axis=AX.X))
                    v.wait_ge(semAC, acB + t * NAC + 1)
                    DV.inc(v.reduce_sum(st_s2, xsq, axis=AX.X))
                    v.wait_ge(semDV, DV.v)
                    DV.inc(v.tensor_scalar_mul(st_mean, st_s1, 1.0 / D))
                    v.wait_ge(semDV, DV.v)
                    DV.inc(v.tensor_scalar_mul(st_var, st_s2, 1.0 / D))
                    v.wait_ge(semDV, DV.v)
                    DV.inc(v.tensor_mul(st_msq, st_mean, st_mean))
                    v.wait_ge(semDV, DV.v)
                    DV.inc(v.tensor_sub(st_var, st_var, st_msq))
                    v.wait_ge(semAC, acB + (t + 1) * NAC)
                    if t >= 1:
                        v.wait_ge(semTE, teB + t * NT1)
                    DV.inc(v.tensor_scalar(xstage, xs, st_mean, st_rstd,
                                           ALU.subtract, ALU.mult))
                    for j in range(8):
                        v.wait_ge(semTE, teB + t * NT1 + j + 1)
                        DV.inc(v.tensor_copy(
                            XNT[:, j * T + t * 128:j * T + (t + 1) * 128],
                            tps[j % 4][:]))

            @blk.sync
            def _(s):
                for t in range(NTT):
                    if t >= 2:
                        s.wait_ge(semDV, dvB + (t - 2) * NDV + 7)
                    sX[t % 2].inc(s.dma_start(
                        out=(x0 if t % 2 == 0 else x1),
                        in_=x_d[t * 128:(t + 1) * 128, :]), 16)
                    assert sX[t % 2].v == mx1a[t]

        p1a = dict(dv=DV.v, te=TE.v, ac=AC.v)

        if phases < 2:
            _final(nc, [(semIN, IN.v), (semX0, sX[0].v), (semX1, sX[1].v)])
            return nc

        mL1 = {}
        _lv = [c.v for c in L]
        for a in range(8):
            _lv[a % 2] += 32
            mL1[a] = _lv[a % 2]

        with (
            nc.psum_tensor("pgr", [128, 1024], F32) as pgr,
            nc.psum_tensor("pga", [128, 1024], F32) as pga,
            nc.psum_tensor("pgb", [128, 1024], F32) as pgb,
            nc.Block() as blk,
        ):
            teB, dvB = TE.v, DV.v

            @blk.tensor
            def _(te):
                te.wait_ge(semDV, p1a["dv"])
                for a in range(8):
                    if a >= 1:
                        te.wait_ge(semDV, dvB + a * 5)
                    te.wait_ge(semL[a % 2], mL1[a])
                    for k in range(8):
                        base = (a % 2) * 2048
                        wrs = A16[:, base + k * 128:base + (k + 1) * 128]
                        wis = A16[:, base + 1024 + k * 128:
                                  base + 1024 + (k + 1) * 128]
                        C = [CSR[:, k * 1024 + h * 512:k * 1024 + (h + 1) * 512]
                             for h in range(2)]
                        S = [CSR[:, 8192 + k * 1024 + h * 512:
                                 8192 + k * 1024 + (h + 1) * 512]
                             for h in range(2)]
                        st, sp = (k == 0), (k == 7)
                        te.matmul(pgr[:, 0:512], wrs, C[0], start=st, stop=False)
                        te.matmul(pgr[:, 512:1024], wrs, C[1], start=st,
                                  stop=False)
                        te.matmul(pga[:, 0:512], wrs, S[0], start=st, stop=sp)
                        te.matmul(pga[:, 512:1024], wrs, S[1], start=st, stop=sp)
                        te.matmul(pgr[:, 0:512], wis, S[0], start=False, stop=sp)
                        te.matmul(pgr[:, 512:1024], wis, S[1], start=False,
                                  stop=sp)
                        te.matmul(pgb[:, 0:512], wis, C[0], start=st, stop=sp)
                        TE.inc(te.matmul(pgb[:, 512:1024], wis, C[1],
                                         start=st, stop=sp))

            @blk.vector
            def _(v):
                for a in range(8):
                    v.wait_ge(semTE, teB + (a + 1) * 8)
                    DV.inc(v.tensor_copy(GrS[:, a * 1024:a * 1024 + 512],
                                         pgr[:, 0:512]))
                    DV.inc(v.tensor_copy(GrS[:, a * 1024 + 512:(a + 1) * 1024],
                                         pgr[:, 512:1024]))
                    DV.inc(v.tensor_copy(pstg[0], pga[:]))
                    v.wait_ge(semDV, DV.v)
                    DV.inc(v.tensor_sub(GipS[:, a * 1024:a * 1024 + 512],
                                        pstg[0][:, 0:512], pgb[:, 0:512]))
                    DV.inc(v.tensor_sub(GipS[:, a * 1024 + 512:(a + 1) * 1024],
                                        pstg[0][:, 512:1024],
                                        pgb[:, 512:1024]))

            @blk.gpsimd
            def _(g):
                g.wait_ge(semIN, in_done)
                g.wait_ge(semDV, p1a["dv"])
                g.wait_ge(semTE, p1a["te"])
                for a in range(8):
                    base = (a % 2) * 2048
                    if a >= 2:
                        g.wait_ge(semTE, teB + (a - 1) * 8)
                    L[a % 2].inc(g.dma_start(
                        out=A16[:, base:base + 1024],
                        in_=wr_d[:, a, :, :].rearrange("p k m -> p (k m)")),
                        16)
                    L[a % 2].inc(g.dma_start(
                        out=A16[:, base + 1024:base + 2048],
                        in_=wi_d[:, a, :, :].rearrange("p k m -> p (k m)")),
                        16)
                    assert L[a % 2].v == mL1[a]

        p0s1 = dict(dv=DV.v, te=TE.v)

        mtst = [qstage[:, 0:512], qstage[:, 512:1024]]
        mST2 = {}
        _stv = ST.v
        for a in range(8):
            _stv += 32
            mST2[a] = _stv
        with (
            nc.psum_tensor("pmt0", [128, 1024], F32) as pmt0,
            nc.psum_tensor("pmt1", [128, 1024], F32) as pmt1,
            nc.Block() as blk,
        ):
            teB, dvB = TE.v, DV.v

            @blk.tensor
            def _(te):
                te.wait_ge(semDV, p0s1["dv"])
                for a in range(8):
                    pm = pmt0 if a % 2 == 0 else pmt1
                    if a >= 2:
                        te.wait_ge(semDV, dvB + (a - 1) * 2)
                    for k in range(8):
                        grs = GrS[:, k * 1024 + a * 128:
                                  k * 1024 + (a + 1) * 128]
                        gis = GipS[:, k * 1024 + a * 128:
                                   k * 1024 + (a + 1) * 128]
                        C = [CSR[:, k * 1024 + h * 512:k * 1024 + (h + 1) * 512]
                             for h in range(2)]
                        S = [CSR[:, 8192 + k * 1024 + h * 512:
                                 8192 + k * 1024 + (h + 1) * 512]
                             for h in range(2)]
                        st, sp = (k == 0), (k == 7)
                        te.matmul(pm[:, 0:512], grs, C[0], start=st, stop=False)
                        te.matmul(pm[:, 512:1024], grs, C[1], start=st,
                                  stop=False)
                        te.matmul(pm[:, 0:512], gis, S[0], start=False, stop=sp)
                        TE.inc(te.matmul(pm[:, 512:1024], gis, S[1],
                                         start=False, stop=sp))

            @blk.vector
            def _(v):
                for a in range(8):
                    pm = pmt0 if a % 2 == 0 else pmt1
                    v.wait_ge(semTE, teB + (a + 1) * 8)
                    if a >= 1:
                        v.wait_ge(semST, mST2[a - 1])
                    DV.inc(v.tensor_scalar_mul(mtst[0], pm[:, 0:512], 1.0 / D))
                    v.wait_ge(semDV, DV.v)
                    DV.inc(v.tensor_scalar_mul(mtst[1], pm[:, 512:1024],
                                               1.0 / D))

            @blk.gpsimd
            def _(g):
                for a in range(8):
                    g.wait_ge(semDV, dvB + (a + 1) * 2)
                    ST.inc(g.dma_start(out=mt_d[a * 128:(a + 1) * 128, 0:512],
                                       in_=mtst[0]), 16)
                    ST.inc(g.dma_start(
                        out=mt_d[a * 128:(a + 1) * 128, 512:1024],
                        in_=mtst[1]), 16)
                    assert ST.v == mST2[a]

        p0 = dict(dv=DV.v, te=TE.v, st=ST.v)

        if phases < 3:
            _final(nc, [(semST, ST.v)])
            return nc

        mPA1b = {}
        _pav = [c.v for c in PA]
        for t in range(NTT):
            _pav[t % 2] += 16
            mPA1b[t] = _pav[t % 2]
        with (
            nc.psum_tensor("pf0", [128, 1024], F32) as pf0,
            nc.psum_tensor("pf1", [128, 1024], F32) as pf1,
            nc.Block() as blk,
        ):
            teB, dvB = TE.v, DV.v
            wt_mt = WT.v + 8 * 16

            @blk.tensor
            def _(te):
                te.wait_ge(semWT, wt_mt)
                te.wait_ge(semDV, dv_bz)
                for t in range(NTT):
                    pf = pf0 if t % 2 == 0 else pf1
                    if t >= 2:
                        te.wait_ge(semDV, dvB + (t - 1))
                    for k in range(8):
                        lh = XNT[:, k * T + t * 128:k * T + (t + 1) * 128]
                        st = (k == 0)
                        te.matmul(pf[:, 0:512], lh,
                                  A16[:, k * 1024:k * 1024 + 512],
                                  start=st, stop=False)
                        te.matmul(pf[:, 512:1024], lh,
                                  A16[:, k * 1024 + 512:(k + 1) * 1024],
                                  start=st, stop=False)
                    te.matmul(pf[:, 0:512], eb, bz[:, 0:512],
                              start=False, stop=True)
                    TE.inc(te.matmul(pf[:, 512:1024], eb, bz[:, 512:1024],
                                     start=False, stop=True))

            @blk.vector
            def _(v):
                for t in range(NTT):
                    pf = pf0 if t % 2 == 0 else pf1
                    v.wait_ge(semTE, teB + t + 1)
                    if t >= 2:
                        v.wait_ge(semPA[t % 2], mPA1b[t - 2])
                    DV.inc(v.tensor_copy(pstg[t % 2], pf[:]))

            @blk.gpsimd
            def _(g):
                g.wait_ge(semTE, p0["te"])
                g.wait_ge(semST, p0["st"])
                for k in range(8):
                    WT.inc(g.dma_start(
                        out=A16[:, k * 1024:(k + 1) * 1024],
                        in_=mt_d[k * 128:(k + 1) * 128, :]), 16)
                assert WT.v == wt_mt
                for t in range(NTT):
                    g.wait_ge(semDV, dvB + t + 1)
                    PA[t % 2].inc(g.dma_start(
                        out=part_d[t * 128:(t + 1) * 128, :],
                        in_=pstg[t % 2]), 16)
                    assert PA[t % 2].v == mPA1b[t]

        p1b = dict(dv=DV.v, te=TE.v)
        pa_1b_final = [PA[0].v, PA[1].v]

        if phases < 4:
            _final(nc, [(semPA0, PA[0].v), (semPA1, PA[1].v)])
            return nc

        mQS = {}
        _qv = [c.v for c in QSm]
        for n in range(64):
            _qv[n % 2] += 16
            mQS[n] = _qv[n % 2]
        mVS = {}
        _vv = [c.v for c in VS]
        for t in range(NTT):
            _vv[t % 2] += 16
            mVS[t] = _vv[t % 2]
        with (
            nc.psum_tensor("pq0", [128, 512], F32) as pq0,
            nc.psum_tensor("pq1", [128, 512], F32) as pq1,
            nc.psum_tensor("pv0", [128, 1024], F32) as pv0,
            nc.psum_tensor("pv1", [128, 1024], F32) as pv1,
            nc.Block() as blk,
        ):
            teB, dvB = TE.v, DV.v
            wt_wq = WT.v + 8 * 16
            wt_wk = wt_wq + 8 * 16
            wt_wv = wt_wk + 8 * 16

            @blk.tensor
            def _(te):
                te.wait_ge(semDV, p1b["dv"])
                te.wait_ge(semWT, wt_wq)
                for c in range(NP):
                    for t8 in range(8):
                        n = c * 8 + t8
                        pq = pq0 if n % 2 == 0 else pq1
                        if n >= 2:
                            te.wait_ge(semDV, dvB + n - 1)
                        last = None
                        for k in range(8):
                            last = te.matmul(
                                pq[:],
                                A16[:, k * 1024 + c * 128:
                                    k * 1024 + (c + 1) * 128],
                                XNT[:, k * T + t8 * 512:k * T + (t8 + 1) * 512],
                                start=(k == 0), stop=(k == 7))
                        TE.inc(last)
                te.wait_ge(semWT, wt_wk)
                for c in range(NP):
                    for t8 in range(8):
                        n = c * 8 + t8
                        pq = pq0 if n % 2 == 0 else pq1
                        if n >= 2:
                            te.wait_ge(semDV, dvB + 64 + n - 1)
                        last = None
                        for k in range(8):
                            last = te.matmul(
                                pq[:],
                                A16[:, k * 1024 + c * 128:
                                    k * 1024 + (c + 1) * 128],
                                XNT[:, k * T + t8 * 512:k * T + (t8 + 1) * 512],
                                start=(k == 0), stop=(k == 7))
                        TE.inc(last)
                te.wait_ge(semWT, wt_wv)
                te.wait_ge(semIN, in_done)
                for t in range(NTT):
                    pv = pv0 if t % 2 == 0 else pv1
                    if t >= 2:
                        te.wait_ge(semDV, dvB + 128 + t - 1)
                    for k in range(8):
                        lh = XNT[:, k * T + t * 128:k * T + (t + 1) * 128]
                        st = (k == 0)
                        te.matmul(pv[:, 0:512], lh,
                                  A16[:, k * 1024:k * 1024 + 512],
                                  start=st, stop=False)
                        te.matmul(pv[:, 512:1024], lh,
                                  A16[:, k * 1024 + 512:(k + 1) * 1024],
                                  start=st, stop=False)
                    te.matmul(pv[:, 0:512], eb, bvz[:, 0:512],
                              start=False, stop=True)
                    TE.inc(te.matmul(pv[:, 512:1024], eb, bvz[:, 512:1024],
                                     start=False, stop=True))

            @blk.vector
            def _(v):
                for c in range(NP):
                    for t8 in range(8):
                        n = c * 8 + t8
                        pq = pq0 if n % 2 == 0 else pq1
                        v.wait_ge(semTE, teB + n + 1)
                        if n >= 2:
                            v.wait_ge(semQS[n % 2], mQS[n - 2])
                        DV.inc(v.tensor_scalar(
                            qstage[:, (n % 2) * 512:(n % 2) * 512 + 512],
                            pq[:], bqk[:, c:c + 1], None, ALU.add))
                for c in range(NP):
                    for t8 in range(8):
                        n = c * 8 + t8
                        pq = pq0 if n % 2 == 0 else pq1
                        v.wait_ge(semTE, teB + 64 + n + 1)
                        DV.inc(v.tensor_scalar(
                            KT[:, c * T + t8 * 512:c * T + (t8 + 1) * 512],
                            pq[:], bqk[:, 8 + c:9 + c], None, ALU.add))
                for t in range(NTT):
                    pv = pv0 if t % 2 == 0 else pv1
                    v.wait_ge(semTE, teB + 128 + t + 1)
                    if t >= 2:
                        v.wait_ge(semVS[t % 2], mVS[t - 2])
                    DV.inc(v.tensor_copy(
                        vstage[:, (t % 2) * 1024:(t % 2) * 1024 + 1024],
                        pv[:]))

            @blk.gpsimd
            def _(g):
                g.wait_ge(semTE, p1b["te"])
                for k in range(8):
                    WT.inc(g.dma_start(
                        out=A16[:, k * 1024:(k + 1) * 1024],
                        in_=wqT_d[k * 128:(k + 1) * 128, :]), 16)
                assert WT.v == wt_wq
                for n in range(64):
                    c, t8 = n // 8, n % 8
                    g.wait_ge(semDV, dvB + n + 1)
                    QSm[n % 2].inc(g.dma_start(
                        out=q_d[c, :, t8 * 512:(t8 + 1) * 512],
                        in_=qstage[:, (n % 2) * 512:(n % 2) * 512 + 512]), 16)
                    assert QSm[n % 2].v == mQS[n]
                g.wait_ge(semTE, teB + 64)
                for k in range(8):
                    WT.inc(g.dma_start(
                        out=A16[:, k * 1024:(k + 1) * 1024],
                        in_=wkT_d[k * 128:(k + 1) * 128, :]), 16)
                assert WT.v == wt_wk
                g.wait_ge(semTE, teB + 128)
                for k in range(8):
                    WT.inc(g.dma_start(
                        out=A16[:, k * 1024:(k + 1) * 1024],
                        in_=wvT_d[k * 128:(k + 1) * 128, :]), 16)
                assert WT.v == wt_wv
                for t in range(NTT):
                    g.wait_ge(semDV, dvB + 128 + t + 1)
                    VS[t % 2].inc(g.dma_start(
                        out=v_d[t * 128:(t + 1) * 128, :],
                        in_=vstage[:, (t % 2) * 1024:(t % 2) * 1024 + 1024]),
                        16)
                    assert VS[t % 2].v == mVS[t]

        p2 = dict(dv=DV.v, te=TE.v)
        vs_final = [VS[0].v, VS[1].v]
        qs_final = [QSm[0].v, QSm[1].v]

        if phases < 5:
            _final(nc, [(semVS0, VS[0].v), (semVS1, VS[1].v),
                        (semQS0, QSm[0].v), (semQS1, QSm[1].v)])
            return nc

        with nc.Block() as blk:
            @blk.gpsimd
            def _(g):
                g.wait_ge(semQS0, qs_final[0])
                g.wait_ge(semQS1, qs_final[1])
                g.wait_ge(semTE, p2["te"])
                for pr in range(NP):
                    QT.inc(g.dma_start(
                        out=QTs[:, pr * 512:(pr + 1) * 512],
                        in_=q_d[pr, :, 0:512]), 16)
        qt_marks = {0: QT.v}

        def plan_dv_p3(base):
            m = {}
            v = base
            for pr in range(NP):
                for kc in range(NKC):
                    v += 1
                    m[("ps", pr, kc)] = v
                if pr >= 1:
                    v += 1
                    m[("scl", pr - 1)] = v
                    v += 1
                    m[("ot", pr - 1)] = v
                v += 3
                m[("zch", pr)] = v
            v += 1
            m[("scl", 7)] = v
            v += 1
            m[("ot", 7)] = v
            for tt in range(4):
                v += 2
                m[("ev", tt)] = v
            m["end"] = v
            return m

        P3 = {}
        _dvc = DV.v
        _acc = AC.v
        _vcc = [c.v for c in VC]
        _sxc = [c.v for c in sX]
        _woc = [c.v for c in WO]
        _pac = [c.v for c in PA]
        _ouc = [c.v for c in OU]
        _qtc = QT.v
        for q in range(NQS):
            pl = {}
            pl["dvM"] = plan_dv_p3(_dvc)
            _dvc = pl["dvM"]["end"]
            pl["acB"] = _acc
            _acc += NP * NKC
            pl["mVC"] = {}
            for kc in range(NKC):
                _vcc[kc % 4] += 16
                pl["mVC"][kc] = _vcc[kc % 4]
            pl["mXq"] = {}
            for tt in range(4):
                _sxc[tt % 2] += 16
                pl["mXq"][tt] = _sxc[tt % 2]
            pl["mWOq"] = {}
            for j in range(2 * NP):
                _woc[j % 2] += 16
                pl["mWOq"][j] = _woc[j % 2]
            pl["mPAq"] = {}
            for tt in range(4):
                _pac[tt % 2] += 16
                pl["mPAq"][tt] = _pac[tt % 2]
            pl["mOUq"] = {}
            for tt in range(4):
                _ouc[tt % 2] += 16
                pl["mOUq"][tt] = _ouc[tt % 2]
            if q + 1 < NQS:
                _qtc += NP * 16
                pl["qt_next"] = _qtc
            P3[q] = pl
        P3[0]["qt_ready"] = qt_marks[0]
        for q in range(1, NQS):
            P3[q]["qt_ready"] = P3[q - 1]["qt_next"]

        teM = {}

        with (
            nc.psum_tensor("pSA", [128, 1024], F32) as pSA,
            nc.psum_tensor("pSB", [128, 1024], F32) as pSB,
            nc.psum_tensor("pOu0", [128, 512], F32) as pOu0,
            nc.psum_tensor("pOu1", [128, 512], F32) as pOu1,
            nc.psum_tensor("pOu2", [128, 512], F32) as pOu2,
            nc.psum_tensor("pzb", [128, 512], F32) as pzb,
            nc.Block() as blk,
        ):
            SA = [pSA, pSB]
            pOu = [pOu0, pOu1, pOu2]
            pz = pzb[0:2, :]
            pbc = pzb
            pes = [pSA, pSB]

            @blk.tensor
            def _(te):
                for q in range(NQS):
                    pl = P3[q]
                    dvM = pl["dvM"]
                    acB = pl["acB"]
                    dvM_prev = P3[q - 1]["dvM"] if q > 0 else None

                    if q == 0:
                        te.wait_ge(semDV, p2["dv"])
                    else:
                        te.wait_ge(semDV, dvM_prev[("ev", 3)])
                    te.wait_ge(semQT, pl["qt_ready"])

                    def emit_sc(u):
                        pr, kc = u // NKC, u % NKC
                        if u >= 2:
                            te.wait_ge(semAC, acB + u - 1)
                        sset = SA[u % 2]
                        te.matmul(
                            sset[:, 0:512],
                            KT[0:64, pr * T + kc * 128:pr * T + (kc + 1) * 128],
                            QTs[0:64, pr * 512:(pr + 1) * 512],
                            start=True, stop=True)
                        last = te.matmul(
                            sset[:, 512:1024],
                            KT[64:128, pr * T + kc * 128:
                               pr * T + (kc + 1) * 128],
                            QTs[64:128, pr * 512:(pr + 1) * 512],
                            start=True, stop=True, tile_position=(64, 0))
                        teM[(q, "sc", u)] = TE.inc(last)

                    def emit_pv(u):
                        pr, kc = u // NKC, u % NKC
                        te.wait_ge(semAC, acB + u + 1)
                        te.wait_ge(semVC[kc % 4], pl["mVC"][kc])
                        if kc == 0:
                            if pr >= 3:
                                te.wait_ge(semDV, dvM[("ot", pr - 3)])
                            elif q > 0:
                                te.wait_ge(semDV,
                                           dvM_prev[("ot", {0: 6, 1: 7,
                                                            2: 5}[pr])])
                        pt = PT[u % 4]
                        po = pOu[pr % 3]
                        st, sp = (kc == 0), (kc == NKC - 1)
                        te.matmul(po[0:64, :],
                                  VQ[:, kc * 1024 + pr * 128:
                                     kc * 1024 + pr * 128 + 64],
                                  pt[:, 0:512], start=st, stop=sp,
                                  skip_group_check=True)
                        last = te.matmul(
                            po[64:128, :],
                            VQ[:, kc * 1024 + pr * 128 + 64:
                               kc * 1024 + (pr + 1) * 128],
                            pt[:, 512:1024], start=st, stop=sp,
                            tile_position=(0, 64), skip_group_check=True)
                        teM[(q, "pv", u)] = TE.inc(last)

                    def emit_zmm(pr):
                        te.wait_ge(semDV, dvM[("ps", pr, NKC - 1)])
                        if pr >= 1:
                            te.wait_ge(semDV, dvM[("scl", pr - 1)])
                        elif q > 0:
                            te.wait_ge(semDV, dvM_prev[("scl", 7)])
                        te.matmul(pz, zsel[:, 0:2], PS[0][:, 0:512],
                                  start=True, stop=False)
                        te.matmul(pz, zsel[:, 1:3], PS[0][:, 512:1024],
                                  start=False, stop=False)
                        te.matmul(pz, zsel[:, 0:2], PS[1][:, 0:512],
                                  start=False, stop=False)
                        last = te.matmul(pz, zsel[:, 1:3],
                                         PS[1][:, 512:1024],
                                         start=False, stop=True)
                        teM[(q, "zmm", pr)] = TE.inc(last)

                    def emit_bc(pr):
                        te.wait_ge(semDV, dvM[("zch", pr)])
                        if pr >= 1:
                            te.wait_ge(semDV, dvM[("scl", pr - 1)])
                        elif q > 0:
                            te.wait_ge(semDV, dvM_prev[("scl", 7)])
                        teM[(q, "bc", pr)] = TE.inc(
                            te.matmul(pbc[:], e2, zpad, start=True, stop=True,
                                      skip_group_check=True))

                    for u in range(NP * NKC):
                        emit_sc(u)
                        if u >= 2:
                            emit_pv(u - 2)
                            if (u - 2) % NKC == NKC - 1:
                                prd = (u - 2) // NKC
                                if prd >= 1:
                                    emit_bc(prd - 1)
                                emit_zmm(prd)
                    emit_pv(NP * NKC - 2)
                    emit_pv(NP * NKC - 1)
                    emit_bc(NP - 2)
                    emit_zmm(NP - 1)
                    emit_bc(NP - 1)

                    for i in range(2 * NP):
                        wave, pc = i // NP, i % NP
                        if i == 0:
                            te.wait_ge(semAC, acB + 256)
                            te.wait_ge(semDV, dvM[("ot", 7)])
                        if i == NP:
                            te.wait_ge(semDV, dvM[("ev", 1)])
                        te.wait_ge(semWO[i % 2], pl["mWOq"][i])
                        ws = wo_sb[:, (i % 2) * 1024:(i % 2) * 1024 + 1024]
                        last = None
                        for w2 in range(2):
                            tt = wave * 2 + w2
                            lh = OTr[:, pc * 512 + tt * 128:
                                     pc * 512 + (tt + 1) * 128]
                            st, sp = (pc == 0), (pc == NP - 1)
                            te.matmul(pes[w2][:, 0:512], lh, ws[:, 0:512],
                                      start=st, stop=sp)
                            last = te.matmul(pes[w2][:, 512:1024], lh,
                                             ws[:, 512:1024], start=st, stop=sp)
                        teM[(q, "p4g", i)] = TE.inc(last)

            @blk.scalar
            def _(sc):
                for q in range(NQS):
                    pl = P3[q]
                    acB = pl["acB"]
                    dvM = pl["dvM"]
                    dvM_prev = P3[q - 1]["dvM"] if q > 0 else None
                    for u in range(NP * NKC):
                        sc.wait_ge(semTE, teM[(q, "sc", u)])
                        if u >= 4:
                            sc.wait_ge(semTE, teM[(q, "pv", u - 4)])
                            sc.wait_ge(semDV,
                                       dvM[("ps", (u - 4) // NKC,
                                            (u - 4) % NKC)])
                        elif q > 0:
                            sc.wait_ge(semTE, teM[(q - 1, "pv", 252 + u)])
                            sc.wait_ge(semDV, dvM_prev[("ps", 7, 28 + u)])
                        AC.inc(sc.activation(PT[u % 4], SA[u % 2][:],
                                             AF.Exp, bias=lnbias, scale=0.125))
                    assert AC.v == acB + 256

            @blk.vector
            def _(v):
                for q in range(NQS):
                    pl = P3[q]
                    acB = pl["acB"]
                    dvM = pl["dvM"]
                    for pr in range(NP):
                        for kc in range(NKC):
                            u = pr * NKC + kc
                            v.wait_ge(semAC, acB + u + 1)
                            if kc <= 1:
                                if pr >= 1:
                                    v.wait_ge(semTE, teM[(q, "zmm", pr - 1)])
                                elif q > 0:
                                    v.wait_ge(semTE, teM[(q - 1, "zmm", 7)])
                            v.wait_ge(semDV, DV.v)
                            if kc <= 1:
                                DV.inc(v.tensor_copy(PS[kc % 2], PT[u % 4]))
                            else:
                                DV.inc(v.tensor_add(PS[kc % 2], PS[kc % 2],
                                                    PT[u % 4]))
                        if pr >= 1:
                            v.wait_ge(semTE, teM[(q, "bc", pr - 1)])
                            v.wait_ge(semDV, DV.v)
                            DV.inc(v.tensor_copy(sclt, pbc[:]))
                            v.wait_ge(semDV, DV.v)
                            v.wait_ge(semTE, teM[(q, "pv", (pr - 1) * NKC + 31)])
                            DV.inc(v.tensor_mul(
                                OTr[:, (pr - 1) * 512:pr * 512],
                                pOu[(pr - 1) % 3][:], sclt))
                        v.wait_ge(semTE, teM[(q, "zmm", pr)])
                        v.wait_ge(semDV, DV.v)
                        DV.inc(v.tensor_copy(zf32, pz))
                        v.wait_ge(semDV, DV.v)
                        DV.inc(v.reciprocal(rcpt, zf32))
                        v.wait_ge(semDV, DV.v)
                        if pr >= 1:
                            v.wait_ge(semTE, teM[(q, "bc", pr - 1)])
                        elif q > 0:
                            v.wait_ge(semTE, teM[(q - 1, "bc", 7)])
                        DV.inc(v.tensor_copy(zpad[0:2, :], rcpt))
                        assert DV.v == dvM[("zch", pr)]
                    v.wait_ge(semTE, teM[(q, "bc", 7)])
                    v.wait_ge(semDV, DV.v)
                    DV.inc(v.tensor_copy(sclt, pbc[:]))
                    v.wait_ge(semDV, DV.v)
                    v.wait_ge(semTE, teM[(q, "pv", 7 * NKC + 31)])
                    DV.inc(v.tensor_mul(OTr[:, 7 * 512:8 * 512],
                                        pOu[7 % 3][:], sclt))
                    for tt in range(4):
                        v.wait_ge(semTE, teM[(q, "p4g", 7 if tt < 2 else 15)])
                        v.wait_ge(semPA[tt % 2], pl["mPAq"][tt])
                        v.wait_ge(semDV, DV.v)
                        DV.inc(v.tensor_add(xsq, pes[tt % 2][:], pstg[tt % 2]))
                        v.wait_ge(semX[tt % 2], pl["mXq"][tt])
                        v.wait_ge(semDV, DV.v)
                        DV.inc(v.tensor_add(pstg[tt % 2], xsq,
                                            x0 if tt % 2 == 0 else x1))
                    assert DV.v == dvM[("ev", 3)]

            @blk.sync
            def _(s):
                for q in range(NQS):
                    pl = P3[q]
                    dvM = pl["dvM"]
                    dvM_prev = P3[q - 1]["dvM"] if q > 0 else None
                    for kc in range(NKC):
                        if kc == 0:
                            if q == 0:
                                s.wait_ge(semTE, p2["te"])
                                s.wait_ge(semVS0, vs_final[0])
                                s.wait_ge(semVS1, vs_final[1])
                            else:
                                s.wait_ge(semTE, teM[(q - 1, "pv", 255)])
                        if kc >= 4:
                            s.wait_ge(semTE, teM[(q, "pv", kc - 4)])
                        VC[kc % 4].inc(s.dma_start(
                            out=VQ[:, kc * 1024:(kc + 1) * 1024],
                            in_=v_d[kc * 128:(kc + 1) * 128, :]), 16)
                        assert VC[kc % 4].v == pl["mVC"][kc]
                    for tt in range(4):
                        if tt >= 2:
                            s.wait_ge(semDV, dvM[("ev", tt - 2)])
                        elif q > 0:
                            s.wait_ge(semDV, dvM_prev[("ev", tt + 2)])
                        sX[tt % 2].inc(s.dma_start(
                            out=(x0 if tt % 2 == 0 else x1),
                            in_=x_d[q * QS + tt * 128:
                                    q * QS + (tt + 1) * 128, :]), 16)
                        assert sX[tt % 2].v == pl["mXq"][tt]
                    for tt in range(4):
                        s.wait_ge(semDV, dvM[("ev", tt)])
                        OU[tt % 2].inc(s.dma_start(
                            out=out_d[q * QS + tt * 128:
                                      q * QS + (tt + 1) * 128, :],
                            in_=pstg[tt % 2]), 16)
                        assert OU[tt % 2].v == pl["mOUq"][tt]

            @blk.gpsimd
            def _(g):
                for q in range(NQS):
                    pl = P3[q]
                    for tt in range(2):
                        if q == 0:
                            g.wait_ge(semPA[tt % 2], pa_1b_final[tt % 2])
                        else:
                            g.wait_ge(semOU[tt % 2], P3[q - 1]["mOUq"][tt + 2])
                        PA[tt % 2].inc(g.dma_start(
                            out=pstg[tt % 2],
                            in_=part_d[q * QS + tt * 128:
                                       q * QS + (tt + 1) * 128, :]), 16)
                        assert PA[tt % 2].v == pl["mPAq"][tt]
                    if q + 1 < NQS:
                        g.wait_ge(semTE, teM[(q, "sc", 255)])
                        for pr in range(NP):
                            QT.inc(g.dma_start(
                                out=QTs[:, pr * 512:(pr + 1) * 512],
                                in_=q_d[pr, :, (q + 1) * 512:(q + 2) * 512]),
                                16)
                        assert QT.v == pl["qt_next"]
                    for i in range(2 * NP):
                        if i >= 2:
                            g.wait_ge(semTE, teM[(q, "p4g", i - 2)])
                        elif q > 0:
                            g.wait_ge(semTE, teM[(q - 1, "p4g", 14 + i)])
                        WO[i % 2].inc(g.dma_start(
                            out=wo_sb[:, (i % 2) * 1024:(i % 2) * 1024 + 1024],
                            in_=woT_d[(i % NP) * 128:(i % NP + 1) * 128, :]),
                            16)
                        assert WO[i % 2].v == pl["mWOq"][i]
                    for tt in range(2, 4):
                        g.wait_ge(semOU[tt % 2], pl["mOUq"][tt - 2])
                        PA[tt % 2].inc(g.dma_start(
                            out=pstg[tt % 2],
                            in_=part_d[q * QS + tt * 128:
                                       q * QS + (tt + 1) * 128, :]), 16)
                        assert PA[tt % 2].v == pl["mPAq"][tt]

        assert DV.v == P3[NQS - 1]["dvM"]["end"]

        _final(nc, [(semOU0, OU[0].v), (semOU1, OU[1].v),
                    (semVC0, VC[0].v), (semVC1, VC[1].v),
                    (semVC2, VC[2].v), (semVC3, VC[3].v),
                    (semPA0, PA[0].v), (semPA1, PA[1].v),
                    (semQT, QT.v)])
    return nc


def _host_constants():
    j = np.arange(D, dtype=np.float64)
    ang = 2.0 * np.pi * np.outer(j, j) / D
    cmat = np.cos(ang).astype(np.float32).astype(np.float16)
    smat = np.sin(ang).astype(np.float32).astype(np.float16)
    ident = np.eye(128, dtype=np.float16)
    e2 = np.zeros((128, 128), np.float16)
    e2[0, 0:64] = 1.0
    e2[1, 64:128] = 1.0
    eb = np.zeros((128, 128), np.float16)
    eb[0, :] = 1.0
    zsel = np.zeros((128, 4), np.float16)
    zsel[:, 0] = 1.0
    zsel[:, 2] = 1.0
    return cmat, smat, ident, e2, eb, zsel


def kernel(L0_attn_hidden_states, L0_attn_Wq, L0_attn_bq, L0_attn_Wk,
           L0_attn_bk, L0_attn_Wv, L0_attn_bv, L0_attn_Wo, L0_attn_bo,
           L0_ffn_w_freq_real, L0_ffn_w_freq_imag, L0_ffn_bias):
    x = np.ascontiguousarray(np.asarray(L0_attn_hidden_states, np.float32))
    cmat, smat, ident, e2, eb, zsel = _host_constants()
    bqkT = np.ascontiguousarray(np.concatenate(
        [np.asarray(L0_attn_bq, np.float32).reshape(8, 128).T,
         np.asarray(L0_attn_bk, np.float32).reshape(8, 128).T], axis=1))
    shared = {
        "wqT": np.ascontiguousarray(np.asarray(L0_attn_Wq, np.float32).T),
        "wkT": np.ascontiguousarray(np.asarray(L0_attn_Wk, np.float32).T),
        "wvT": np.ascontiguousarray(np.asarray(L0_attn_Wv, np.float32).T),
        "woT": np.ascontiguousarray(np.asarray(L0_attn_Wo, np.float32).T),
        "bqkT": bqkT,
        "bo_row": np.asarray(L0_attn_bo, np.float32).reshape(1, D),
        "fb_row": np.asarray(L0_ffn_bias, np.float32).reshape(1, D),
        "bv_row": np.asarray(L0_attn_bv, np.float32).reshape(1, D),
        "wr": np.ascontiguousarray(
            np.asarray(L0_ffn_w_freq_real, np.float32)
            .reshape(8, 128, 8, 128).transpose(1, 2, 0, 3)),
        "wi": np.ascontiguousarray(
            np.asarray(L0_ffn_w_freq_imag, np.float32)
            .reshape(8, 128, 8, 128).transpose(1, 2, 0, 3)),
        "cmat": cmat, "smat": smat,
        "ident": ident, "e2": e2, "eb": eb, "zsel": zsel,
    }
    in_maps = []
    for b in range(8):
        m = dict(shared)
        m["x"] = np.ascontiguousarray(x[b])
        in_maps.append(m)
    nc = build_nc()
    res = run_bass_kernel_spmd(nc, in_maps, core_ids=list(range(8)))
    out = np.stack([res.results[b]["out"] for b in range(8)], axis=0)
    return out.astype(np.float32)


# version 15 (final)
# speedup vs baseline: 1.0605x; 1.0031x over previous
import os as _os
import contextlib
import numpy as np
import concourse.bass as bass
import concourse.mybir as mybir
from concourse.bass_utils import run_bass_kernel_spmd

F32 = mybir.dt.float32
F16 = mybir.dt.float16
AF = mybir.ActivationFunctionType
ALU = mybir.AluOpType
AX = mybir.AxisListType

T = 4096
D = 1024
NP = 8
QS = 512
NQS = T // QS
NKC = 32
NTT = T // 128
LN64 = float(np.log(64.0))


class Ctr:
    def __init__(self, sem):
        self.sem = sem
        self.v = 0

    def inc(self, inst, n=1):
        inst.then_inc(self.sem, n)
        self.v += n
        return self.v


def _final(nc, waits):
    with nc.Block() as blk:
        @blk.sync
        def _(s):
            for sem, v in waits:
                if v > 0:
                    s.wait_ge(sem, v)


def build_nc(phases=99):
    nc = bass.Bass("TRN2", target_bir_lowering=False)

    x_d = nc.dram_tensor("x", [T, D], F32, kind="ExternalInput").ap()
    wqT_d = nc.dram_tensor("wqT", [D, D], F32, kind="ExternalInput").ap()
    wkT_d = nc.dram_tensor("wkT", [D, D], F32, kind="ExternalInput").ap()
    wvT_d = nc.dram_tensor("wvT", [D, D], F32, kind="ExternalInput").ap()
    woT_d = nc.dram_tensor("woT", [D, D], F32, kind="ExternalInput").ap()
    bqk_d = nc.dram_tensor("bqkT", [128, 16], F32, kind="ExternalInput").ap()
    bo_d = nc.dram_tensor("bo_row", [1, D], F32, kind="ExternalInput").ap()
    fb_d = nc.dram_tensor("fb_row", [1, D], F32, kind="ExternalInput").ap()
    bv_d = nc.dram_tensor("bv_row", [1, D], F32, kind="ExternalInput").ap()
    wr_d = nc.dram_tensor("wr", [128, 8, 8, 128], F32, kind="ExternalInput").ap()
    wi_d = nc.dram_tensor("wi", [128, 8, 8, 128], F32, kind="ExternalInput").ap()
    c_d = nc.dram_tensor("cmat", [D, D], F16, kind="ExternalInput").ap()
    s_d = nc.dram_tensor("smat", [D, D], F16, kind="ExternalInput").ap()
    ident_d = nc.dram_tensor("ident", [128, 128], F16, kind="ExternalInput").ap()
    e2_d = nc.dram_tensor("e2", [128, 128], F16, kind="ExternalInput").ap()
    eb_d = nc.dram_tensor("eb", [128, 128], F16, kind="ExternalInput").ap()
    zsel_d = nc.dram_tensor("zsel", [128, 4], F16, kind="ExternalInput").ap()

    out_d = nc.dram_tensor("out", [T, D], F32, kind="ExternalOutput").ap()

    q_d = nc.dram_tensor("q_scr", [NP, 128, T], F16, kind="Internal").ap()
    v_d = nc.dram_tensor("v_scr", [T, D], F16, kind="Internal").ap()
    mt_d = nc.dram_tensor("mt_scr", [D, D], F16, kind="Internal").ap()
    gr_d = nc.dram_tensor("gr_scr", [D, D], F32, kind="Internal").ap()
    gip_d = nc.dram_tensor("gip_scr", [D, D], F32, kind="Internal").ap()
    part_d = nc.dram_tensor("part_scr", [T, D], F32, kind="Internal").ap()

    bigA_h = nc.alloc_sbuf_tensor("bigA", [128, NP * T], F16)
    bigB_h = nc.alloc_sbuf_tensor("bigB", [128, NP * T], F16)
    a16_h = nc.alloc_sbuf_tensor("a16", [128, 12288], F16)
    a32_h = nc.alloc_sbuf_tensor("a32", [128, 5120], F32)

    XNT = bigA_h.ap()
    VQ = XNT
    KT = bigB_h.ap()
    CSR = KT
    GrS = KT[:, 16384:24576]
    GipS = KT[:, 24576:32768]
    A16 = a16_h.ap()
    A32 = a32_h.ap()

    ident = nc.alloc_sbuf_tensor("identt", [128, 128], F16).ap()
    e2 = nc.alloc_sbuf_tensor("e2t", [128, 128], F16).ap()
    eb = nc.alloc_sbuf_tensor("ebt", [128, 128], F16).ap()
    zsel = nc.alloc_sbuf_tensor("zselt", [128, 4], F16).ap()
    bqk = nc.alloc_sbuf_tensor("bqkt", [128, 16], F32).ap()
    bvz = nc.alloc_sbuf_tensor("bvzt", [128, D], F16).ap()
    bz = nc.alloc_sbuf_tensor("bzt", [128, D], F16).ap()
    zpad = nc.alloc_sbuf_tensor("zpadt", [128, QS], F16).ap()
    zf32 = nc.alloc_sbuf_tensor("zf32t", [2, QS], F32).ap()
    rcpt = nc.alloc_sbuf_tensor("rcptt", [2, QS], F32).ap()
    sclt = nc.alloc_sbuf_tensor("scltt", [128, QS], F32).ap()
    wo_sb = nc.alloc_sbuf_tensor("wo_sbt", [128, 2048], F16).ap()
    qstage = nc.alloc_sbuf_tensor("qstaget", [128, 1024], F16).ap()
    vstage = nc.alloc_sbuf_tensor("vstaget", [128, 2048], F16).ap()
    xstage = nc.alloc_sbuf_tensor("xstaget", [128, 1024], F16).ap()
    brow0 = nc.alloc_sbuf_tensor("brow0t", [1, D], F32).ap()
    brow1 = nc.alloc_sbuf_tensor("brow1t", [1, D], F32).ap()
    lnbias = nc.alloc_sbuf_tensor("lnbiast", [128, 1], F32).ap()
    epsb = nc.alloc_sbuf_tensor("epsbt", [128, 1], F32).ap()
    st_s1 = nc.alloc_sbuf_tensor("st_s1t", [128, 1], F32).ap()
    st_s2 = nc.alloc_sbuf_tensor("st_s2t", [128, 1], F32).ap()
    st_mean = nc.alloc_sbuf_tensor("st_meant", [128, 1], F32).ap()
    st_var = nc.alloc_sbuf_tensor("st_vart", [128, 1], F32).ap()
    st_msq = nc.alloc_sbuf_tensor("st_msqt", [128, 1], F32).ap()
    st_lnv = nc.alloc_sbuf_tensor("st_lnvt", [128, 1], F32).ap()
    st_rstd = nc.alloc_sbuf_tensor("st_rstdt", [128, 1], F32).ap()

    x0 = A32[:, 0:1024]
    x1 = A32[:, 1024:2048]
    xsq = A32[:, 2048:3072]
    ev32 = [A32[:, i * 512:(i + 1) * 512] for i in range(4)]
    pstg = [A32[:, 3072 + i * 1024:3072 + (i + 1) * 1024] for i in range(2)]

    QTs = A16[:, 0:4096]
    pt2_h = nc.alloc_sbuf_tensor("pt2t", [128, 1024], F16).ap()
    PT = [A16[:, 4096:5120], A16[:, 5120:6144], pt2_h, xstage]
    PS = [A16[:, 6144:7168], A16[:, 7168:8192]]
    OTr = A16[:, 8192:12288]

    with contextlib.ExitStack() as _stack:
        semTE = _stack.enter_context(nc.semaphore("sTE"))
        semAC = _stack.enter_context(nc.semaphore("sAC"))
        semDV = _stack.enter_context(nc.semaphore("sDV"))
        semIN = _stack.enter_context(nc.semaphore("sIN"))
        semWT = _stack.enter_context(nc.semaphore("sWT"))
        semST = _stack.enter_context(nc.semaphore("sST"))
        semL0 = _stack.enter_context(nc.semaphore("sL0"))
        semL1 = _stack.enter_context(nc.semaphore("sL1"))
        semL2 = _stack.enter_context(nc.semaphore("sL2"))
        semL3 = _stack.enter_context(nc.semaphore("sL3"))
        semX0 = _stack.enter_context(nc.semaphore("sX0"))
        semX1 = _stack.enter_context(nc.semaphore("sX1"))
        semQS0 = _stack.enter_context(nc.semaphore("sQS0"))
        semQS1 = _stack.enter_context(nc.semaphore("sQS1"))
        semVS0 = _stack.enter_context(nc.semaphore("sVS0"))
        semVS1 = _stack.enter_context(nc.semaphore("sVS1"))
        semVC0 = _stack.enter_context(nc.semaphore("sVC0"))
        semVC1 = _stack.enter_context(nc.semaphore("sVC1"))
        semVC2 = _stack.enter_context(nc.semaphore("sVC2"))
        semVC3 = _stack.enter_context(nc.semaphore("sVC3"))
        semQT = _stack.enter_context(nc.semaphore("sQT"))
        semWO0 = _stack.enter_context(nc.semaphore("sWO0"))
        semWO1 = _stack.enter_context(nc.semaphore("sWO1"))
        semPA0 = _stack.enter_context(nc.semaphore("sPA0"))
        semPA1 = _stack.enter_context(nc.semaphore("sPA1"))
        semOU0 = _stack.enter_context(nc.semaphore("sOU0"))
        semOU1 = _stack.enter_context(nc.semaphore("sOU1"))
        TE, AC, DV = Ctr(semTE), Ctr(semAC), Ctr(semDV)
        IN = Ctr(semIN)
        WT = Ctr(semWT)
        ST = Ctr(semST)
        L = [Ctr(semL0), Ctr(semL1), Ctr(semL2), Ctr(semL3)]
        semL = [semL0, semL1, semL2, semL3]
        sX = [Ctr(semX0), Ctr(semX1)]
        semX = [semX0, semX1]
        QSm = [Ctr(semQS0), Ctr(semQS1)]
        semQS = [semQS0, semQS1]
        VS = [Ctr(semVS0), Ctr(semVS1)]
        semVS = [semVS0, semVS1]
        VC = [Ctr(semVC0), Ctr(semVC1), Ctr(semVC2), Ctr(semVC3)]
        semVC = [semVC0, semVC1, semVC2, semVC3]
        QT = Ctr(semQT)
        WO = [Ctr(semWO0), Ctr(semWO1)]
        semWO = [semWO0, semWO1]
        PA = [Ctr(semPA0), Ctr(semPA1)]
        semPA = [semPA0, semPA1]
        OU = [Ctr(semOU0), Ctr(semOU1)]
        semOU = [semOU0, semOU1]

        with nc.Block() as blk:
            @blk.vector
            def _(v):
                DV.inc(v.memset(bvz, 0.0))
                DV.inc(v.memset(bz, 0.0))
                DV.inc(v.memset(zpad, 0.0))
                DV.inc(v.memset(lnbias, -LN64))
                DV.inc(v.memset(epsb, 1e-5))

            @blk.gpsimd
            def _(g):
                for dst, src in [
                    (ident, ident_d), (e2, e2_d), (eb, eb_d), (zsel, zsel_d),
                    (bqk, bqk_d), (brow0, bo_d), (brow1, fb_d),
                ]:
                    IN.inc(g.dma_start(out=dst, in_=src), 16)
                for k in range(8):
                    IN.inc(g.dma_start(out=CSR[:, k * 1024:(k + 1) * 1024],
                                       in_=c_d[k * 128:(k + 1) * 128, :]), 16)
                    IN.inc(g.dma_start(
                        out=CSR[:, 8192 + k * 1024:8192 + (k + 1) * 1024],
                        in_=s_d[k * 128:(k + 1) * 128, :]), 16)
                g.wait_ge(semDV, 3)
                IN.inc(g.dma_start(out=bvz[0:1, :], in_=bv_d), 16)

        in_done = IN.v
        assert in_done == 24 * 16

        with nc.Block() as blk:
            @blk.vector
            def _(v):
                v.wait_ge(semIN, in_done)
                DV.inc(v.tensor_add(brow0, brow0, brow1))
                v.wait_ge(semDV, DV.v)
                DV.inc(v.tensor_copy(bz[0:1, :], brow0))
        dv_bz = DV.v

        NDV, NAC, NT1 = 15, 3, 8
        mx1a = {}
        _sxv = [c.v for c in sX]
        for t in range(NTT):
            _sxv[t % 2] += 16
            mx1a[t] = _sxv[t % 2]
        with (
            nc.psum_tensor("tp0", [128, 128], F16) as tp0,
            nc.psum_tensor("tp1", [128, 128], F16) as tp1,
            nc.psum_tensor("tp2", [128, 128], F16) as tp2,
            nc.psum_tensor("tp3", [128, 128], F16) as tp3,
            nc.Block() as blk,
        ):
            tps = [tp0, tp1, tp2, tp3]
            dvB, acB, teB = DV.v, AC.v, TE.v

            @blk.tensor
            def _(te):
                te.wait_ge(semIN, in_done)
                for t in range(NTT):
                    te.wait_ge(semDV, dvB + t * NDV + 7)
                    for j in range(8):
                        tj = t * 8 + j
                        if tj >= 4:
                            tp_, jp = (tj - 4) // 8, (tj - 4) % 8
                            te.wait_ge(semDV, dvB + tp_ * NDV + 7 + jp + 1)
                        TE.inc(te.transpose(
                            tps[j % 4][:],
                            xstage[:, j * 128:(j + 1) * 128], ident))

            @blk.scalar
            def _(sc):
                for t in range(NTT):
                    xs = x0 if t % 2 == 0 else x1
                    sc.wait_ge(semX[t % 2], mx1a[t])
                    if t >= 1:
                        sc.wait_ge(semDV, dvB + (t - 1) * NDV + 2)
                    AC.inc(sc.activation(xsq, xs, AF.Square))
                    sc.wait_ge(semDV, dvB + t * NDV + 6)
                    AC.inc(sc.activation(st_lnv, st_var, AF.Ln, bias=epsb))
                    sc.wait_ge(semAC, AC.v)
                    AC.inc(sc.activation(st_rstd, st_lnv, AF.Exp, scale=-0.5))

            @blk.vector
            def _(v):
                for t in range(NTT):
                    xs = x0 if t % 2 == 0 else x1
                    v.wait_ge(semX[t % 2], mx1a[t])
                    DV.inc(v.reduce_sum(st_s1, xs, axis=AX.X))
                    v.wait_ge(semAC, acB + t * NAC + 1)
                    DV.inc(v.reduce_sum(st_s2, xsq, axis=AX.X))
                    v.wait_ge(semDV, DV.v)
                    DV.inc(v.tensor_scalar_mul(st_mean, st_s1, 1.0 / D))
                    v.wait_ge(semDV, DV.v)
                    DV.inc(v.tensor_scalar_mul(st_var, st_s2, 1.0 / D))
                    v.wait_ge(semDV, DV.v)
                    DV.inc(v.tensor_mul(st_msq, st_mean, st_mean))
                    v.wait_ge(semDV, DV.v)
                    DV.inc(v.tensor_sub(st_var, st_var, st_msq))
                    v.wait_ge(semAC, acB + (t + 1) * NAC)
                    if t >= 1:
                        v.wait_ge(semTE, teB + t * NT1)
                    DV.inc(v.tensor_scalar(xstage, xs, st_mean, st_rstd,
                                           ALU.subtract, ALU.mult))
                    for j in range(8):
                        v.wait_ge(semTE, teB + t * NT1 + j + 1)
                        DV.inc(v.tensor_copy(
                            XNT[:, j * T + t * 128:j * T + (t + 1) * 128],
                            tps[j % 4][:]))

            @blk.sync
            def _(s):
                for t in range(NTT):
                    if t >= 2:
                        s.wait_ge(semDV, dvB + (t - 2) * NDV + 7)
                    sX[t % 2].inc(s.dma_start(
                        out=(x0 if t % 2 == 0 else x1),
                        in_=x_d[t * 128:(t + 1) * 128, :]), 16)
                    assert sX[t % 2].v == mx1a[t]

        p1a = dict(dv=DV.v, te=TE.v, ac=AC.v)

        if phases < 2:
            _final(nc, [(semIN, IN.v), (semX0, sX[0].v), (semX1, sX[1].v)])
            return nc

        mL1 = {}
        _lv = [c.v for c in L]
        for a in range(8):
            _lv[a % 2] += 32
            mL1[a] = _lv[a % 2]

        with (
            nc.psum_tensor("pgr", [128, 1024], F32) as pgr,
            nc.psum_tensor("pga", [128, 1024], F32) as pga,
            nc.psum_tensor("pgb", [128, 1024], F32) as pgb,
            nc.Block() as blk,
        ):
            teB, dvB = TE.v, DV.v

            @blk.tensor
            def _(te):
                te.wait_ge(semDV, p1a["dv"])
                for a in range(8):
                    if a >= 1:
                        te.wait_ge(semDV, dvB + a * 5)
                    te.wait_ge(semL[a % 2], mL1[a])
                    for k in range(8):
                        base = (a % 2) * 2048
                        wrs = A16[:, base + k * 128:base + (k + 1) * 128]
                        wis = A16[:, base + 1024 + k * 128:
                                  base + 1024 + (k + 1) * 128]
                        C = [CSR[:, k * 1024 + h * 512:k * 1024 + (h + 1) * 512]
                             for h in range(2)]
                        S = [CSR[:, 8192 + k * 1024 + h * 512:
                                 8192 + k * 1024 + (h + 1) * 512]
                             for h in range(2)]
                        st, sp = (k == 0), (k == 7)
                        te.matmul(pgr[:, 0:512], wrs, C[0], start=st, stop=False)
                        te.matmul(pgr[:, 512:1024], wrs, C[1], start=st,
                                  stop=False)
                        te.matmul(pga[:, 0:512], wrs, S[0], start=st, stop=sp)
                        te.matmul(pga[:, 512:1024], wrs, S[1], start=st, stop=sp)
                        te.matmul(pgr[:, 0:512], wis, S[0], start=False, stop=sp)
                        te.matmul(pgr[:, 512:1024], wis, S[1], start=False,
                                  stop=sp)
                        te.matmul(pgb[:, 0:512], wis, C[0], start=st, stop=sp)
                        TE.inc(te.matmul(pgb[:, 512:1024], wis, C[1],
                                         start=st, stop=sp))

            @blk.vector
            def _(v):
                for a in range(8):
                    v.wait_ge(semTE, teB + (a + 1) * 8)
                    DV.inc(v.tensor_copy(GrS[:, a * 1024:a * 1024 + 512],
                                         pgr[:, 0:512]))
                    DV.inc(v.tensor_copy(GrS[:, a * 1024 + 512:(a + 1) * 1024],
                                         pgr[:, 512:1024]))
                    DV.inc(v.tensor_copy(pstg[0], pga[:]))
                    v.wait_ge(semDV, DV.v)
                    DV.inc(v.tensor_sub(GipS[:, a * 1024:a * 1024 + 512],
                                        pstg[0][:, 0:512], pgb[:, 0:512]))
                    DV.inc(v.tensor_sub(GipS[:, a * 1024 + 512:(a + 1) * 1024],
                                        pstg[0][:, 512:1024],
                                        pgb[:, 512:1024]))

            @blk.gpsimd
            def _(g):
                g.wait_ge(semIN, in_done)
                g.wait_ge(semDV, p1a["dv"])
                g.wait_ge(semTE, p1a["te"])
                for a in range(8):
                    base = (a % 2) * 2048
                    if a >= 2:
                        g.wait_ge(semTE, teB + (a - 1) * 8)
                    L[a % 2].inc(g.dma_start(
                        out=A16[:, base:base + 1024],
                        in_=wr_d[:, a, :, :].rearrange("p k m -> p (k m)")),
                        16)
                    L[a % 2].inc(g.dma_start(
                        out=A16[:, base + 1024:base + 2048],
                        in_=wi_d[:, a, :, :].rearrange("p k m -> p (k m)")),
                        16)
                    assert L[a % 2].v == mL1[a]

        p0s1 = dict(dv=DV.v, te=TE.v)

        mtst = [qstage[:, 0:512], qstage[:, 512:1024]]
        mST2 = {}
        _stv = ST.v
        for a in range(8):
            _stv += 32
            mST2[a] = _stv
        with (
            nc.psum_tensor("pmt0", [128, 1024], F32) as pmt0,
            nc.psum_tensor("pmt1", [128, 1024], F32) as pmt1,
            nc.Block() as blk,
        ):
            teB, dvB = TE.v, DV.v

            @blk.tensor
            def _(te):
                te.wait_ge(semDV, p0s1["dv"])
                for a in range(8):
                    pm = pmt0 if a % 2 == 0 else pmt1
                    if a >= 2:
                        te.wait_ge(semDV, dvB + (a - 1) * 2)
                    for k in range(8):
                        grs = GrS[:, k * 1024 + a * 128:
                                  k * 1024 + (a + 1) * 128]
                        gis = GipS[:, k * 1024 + a * 128:
                                   k * 1024 + (a + 1) * 128]
                        C = [CSR[:, k * 1024 + h * 512:k * 1024 + (h + 1) * 512]
                             for h in range(2)]
                        S = [CSR[:, 8192 + k * 1024 + h * 512:
                                 8192 + k * 1024 + (h + 1) * 512]
                             for h in range(2)]
                        st, sp = (k == 0), (k == 7)
                        te.matmul(pm[:, 0:512], grs, C[0], start=st, stop=False)
                        te.matmul(pm[:, 512:1024], grs, C[1], start=st,
                                  stop=False)
                        te.matmul(pm[:, 0:512], gis, S[0], start=False, stop=sp)
                        TE.inc(te.matmul(pm[:, 512:1024], gis, S[1],
                                         start=False, stop=sp))

            @blk.vector
            def _(v):
                for a in range(8):
                    pm = pmt0 if a % 2 == 0 else pmt1
                    v.wait_ge(semTE, teB + (a + 1) * 8)
                    if a >= 1:
                        v.wait_ge(semST, mST2[a - 1])
                    DV.inc(v.tensor_scalar_mul(mtst[0], pm[:, 0:512], 1.0 / D))
                    v.wait_ge(semDV, DV.v)
                    DV.inc(v.tensor_scalar_mul(mtst[1], pm[:, 512:1024],
                                               1.0 / D))

            @blk.gpsimd
            def _(g):
                for a in range(8):
                    g.wait_ge(semDV, dvB + (a + 1) * 2)
                    ST.inc(g.dma_start(out=mt_d[a * 128:(a + 1) * 128, 0:512],
                                       in_=mtst[0]), 16)
                    ST.inc(g.dma_start(
                        out=mt_d[a * 128:(a + 1) * 128, 512:1024],
                        in_=mtst[1]), 16)
                    assert ST.v == mST2[a]

        p0 = dict(dv=DV.v, te=TE.v, st=ST.v)

        if phases < 3:
            _final(nc, [(semST, ST.v)])
            return nc

        mPA1b = {}
        _pav = [c.v for c in PA]
        for t in range(NTT):
            _pav[t % 2] += 16
            mPA1b[t] = _pav[t % 2]
        with (
            nc.psum_tensor("pf0", [128, 1024], F32) as pf0,
            nc.psum_tensor("pf1", [128, 1024], F32) as pf1,
            nc.Block() as blk,
        ):
            teB, dvB = TE.v, DV.v
            wt_mt = WT.v + 8 * 16

            @blk.tensor
            def _(te):
                te.wait_ge(semWT, wt_mt)
                te.wait_ge(semDV, dv_bz)
                for t in range(NTT):
                    pf = pf0 if t % 2 == 0 else pf1
                    if t >= 2:
                        te.wait_ge(semDV, dvB + (t - 1))
                    for k in range(8):
                        lh = XNT[:, k * T + t * 128:k * T + (t + 1) * 128]
                        st = (k == 0)
                        te.matmul(pf[:, 0:512], lh,
                                  A16[:, k * 1024:k * 1024 + 512],
                                  start=st, stop=False)
                        te.matmul(pf[:, 512:1024], lh,
                                  A16[:, k * 1024 + 512:(k + 1) * 1024],
                                  start=st, stop=False)
                    te.matmul(pf[:, 0:512], eb, bz[:, 0:512],
                              start=False, stop=True)
                    TE.inc(te.matmul(pf[:, 512:1024], eb, bz[:, 512:1024],
                                     start=False, stop=True))

            @blk.vector
            def _(v):
                for t in range(NTT):
                    pf = pf0 if t % 2 == 0 else pf1
                    v.wait_ge(semTE, teB + t + 1)
                    if t >= 2:
                        v.wait_ge(semPA[t % 2], mPA1b[t - 2])
                    DV.inc(v.tensor_copy(pstg[t % 2], pf[:]))

            @blk.gpsimd
            def _(g):
                g.wait_ge(semTE, p0["te"])
                g.wait_ge(semST, p0["st"])
                for k in range(8):
                    WT.inc(g.dma_start(
                        out=A16[:, k * 1024:(k + 1) * 1024],
                        in_=mt_d[k * 128:(k + 1) * 128, :]), 16)
                assert WT.v == wt_mt
                for t in range(NTT):
                    g.wait_ge(semDV, dvB + t + 1)
                    PA[t % 2].inc(g.dma_start(
                        out=part_d[t * 128:(t + 1) * 128, :],
                        in_=pstg[t % 2]), 16)
                    assert PA[t % 2].v == mPA1b[t]

        p1b = dict(dv=DV.v, te=TE.v)
        pa_1b_final = [PA[0].v, PA[1].v]

        if phases < 4:
            _final(nc, [(semPA0, PA[0].v), (semPA1, PA[1].v)])
            return nc

        mQS = {}
        _qv = [c.v for c in QSm]
        for n in range(64):
            _qv[n % 2] += 16
            mQS[n] = _qv[n % 2]
        mVS = {}
        _vv = [c.v for c in VS]
        for t in range(NTT):
            _vv[t % 2] += 16
            mVS[t] = _vv[t % 2]
        with (
            nc.psum_tensor("pq0", [128, 512], F32) as pq0,
            nc.psum_tensor("pq1", [128, 512], F32) as pq1,
            nc.psum_tensor("pv0", [128, 1024], F32) as pv0,
            nc.psum_tensor("pv1", [128, 1024], F32) as pv1,
            nc.Block() as blk,
        ):
            teB, dvB = TE.v, DV.v
            wt_wq = WT.v + 8 * 16
            wt_wk = wt_wq + 8 * 16
            wt_wv = wt_wk + 8 * 16

            @blk.tensor
            def _(te):
                te.wait_ge(semDV, p1b["dv"])
                te.wait_ge(semWT, wt_wq)
                for c in range(NP):
                    for t8 in range(8):
                        n = c * 8 + t8
                        pq = pq0 if n % 2 == 0 else pq1
                        if n >= 2:
                            te.wait_ge(semDV, dvB + n - 1)
                        last = None
                        for k in range(8):
                            last = te.matmul(
                                pq[:],
                                A16[:, k * 1024 + c * 128:
                                    k * 1024 + (c + 1) * 128],
                                XNT[:, k * T + t8 * 512:k * T + (t8 + 1) * 512],
                                start=(k == 0), stop=(k == 7))
                        TE.inc(last)
                te.wait_ge(semWT, wt_wk)
                for c in range(NP):
                    for t8 in range(8):
                        n = c * 8 + t8
                        pq = pq0 if n % 2 == 0 else pq1
                        if n >= 2:
                            te.wait_ge(semDV, dvB + 64 + n - 1)
                        last = None
                        for k in range(8):
                            last = te.matmul(
                                pq[:],
                                A16[:, k * 1024 + c * 128:
                                    k * 1024 + (c + 1) * 128],
                                XNT[:, k * T + t8 * 512:k * T + (t8 + 1) * 512],
                                start=(k == 0), stop=(k == 7))
                        TE.inc(last)
                te.wait_ge(semWT, wt_wv)
                te.wait_ge(semIN, in_done)
                for t in range(NTT):
                    pv = pv0 if t % 2 == 0 else pv1
                    if t >= 2:
                        te.wait_ge(semDV, dvB + 128 + t - 1)
                    for k in range(8):
                        lh = XNT[:, k * T + t * 128:k * T + (t + 1) * 128]
                        st = (k == 0)
                        te.matmul(pv[:, 0:512], lh,
                                  A16[:, k * 1024:k * 1024 + 512],
                                  start=st, stop=False)
                        te.matmul(pv[:, 512:1024], lh,
                                  A16[:, k * 1024 + 512:(k + 1) * 1024],
                                  start=st, stop=False)
                    te.matmul(pv[:, 0:512], eb, bvz[:, 0:512],
                              start=False, stop=True)
                    TE.inc(te.matmul(pv[:, 512:1024], eb, bvz[:, 512:1024],
                                     start=False, stop=True))

            @blk.vector
            def _(v):
                for c in range(NP):
                    for t8 in range(8):
                        n = c * 8 + t8
                        pq = pq0 if n % 2 == 0 else pq1
                        v.wait_ge(semTE, teB + n + 1)
                        if n >= 2:
                            v.wait_ge(semQS[n % 2], mQS[n - 2])
                        DV.inc(v.tensor_scalar(
                            qstage[:, (n % 2) * 512:(n % 2) * 512 + 512],
                            pq[:], bqk[:, c:c + 1], None, ALU.add))
                for c in range(NP):
                    for t8 in range(8):
                        n = c * 8 + t8
                        pq = pq0 if n % 2 == 0 else pq1
                        v.wait_ge(semTE, teB + 64 + n + 1)
                        DV.inc(v.tensor_scalar(
                            KT[:, c * T + t8 * 512:c * T + (t8 + 1) * 512],
                            pq[:], bqk[:, 8 + c:9 + c], None, ALU.add))
                for t in range(NTT):
                    pv = pv0 if t % 2 == 0 else pv1
                    v.wait_ge(semTE, teB + 128 + t + 1)
                    if t >= 2:
                        v.wait_ge(semVS[t % 2], mVS[t - 2])
                    DV.inc(v.tensor_copy(
                        vstage[:, (t % 2) * 1024:(t % 2) * 1024 + 1024],
                        pv[:]))

            @blk.gpsimd
            def _(g):
                g.wait_ge(semTE, p1b["te"])
                for k in range(8):
                    WT.inc(g.dma_start(
                        out=A16[:, k * 1024:(k + 1) * 1024],
                        in_=wqT_d[k * 128:(k + 1) * 128, :]), 16)
                assert WT.v == wt_wq
                for n in range(64):
                    c, t8 = n // 8, n % 8
                    g.wait_ge(semDV, dvB + n + 1)
                    QSm[n % 2].inc(g.dma_start(
                        out=q_d[c, :, t8 * 512:(t8 + 1) * 512],
                        in_=qstage[:, (n % 2) * 512:(n % 2) * 512 + 512]), 16)
                    assert QSm[n % 2].v == mQS[n]
                g.wait_ge(semTE, teB + 64)
                for k in range(8):
                    WT.inc(g.dma_start(
                        out=A16[:, k * 1024:(k + 1) * 1024],
                        in_=wkT_d[k * 128:(k + 1) * 128, :]), 16)
                assert WT.v == wt_wk
                g.wait_ge(semTE, teB + 128)
                for k in range(8):
                    WT.inc(g.dma_start(
                        out=A16[:, k * 1024:(k + 1) * 1024],
                        in_=wvT_d[k * 128:(k + 1) * 128, :]), 16)
                assert WT.v == wt_wv
                for t in range(NTT):
                    g.wait_ge(semDV, dvB + 128 + t + 1)
                    VS[t % 2].inc(g.dma_start(
                        out=v_d[t * 128:(t + 1) * 128, :],
                        in_=vstage[:, (t % 2) * 1024:(t % 2) * 1024 + 1024]),
                        16)
                    assert VS[t % 2].v == mVS[t]

        p2 = dict(dv=DV.v, te=TE.v)
        vs_final = [VS[0].v, VS[1].v]
        qs_final = [QSm[0].v, QSm[1].v]

        if phases < 5:
            _final(nc, [(semVS0, VS[0].v), (semVS1, VS[1].v),
                        (semQS0, QSm[0].v), (semQS1, QSm[1].v)])
            return nc

        with nc.Block() as blk:
            @blk.gpsimd
            def _(g):
                g.wait_ge(semQS0, qs_final[0])
                g.wait_ge(semQS1, qs_final[1])
                g.wait_ge(semTE, p2["te"])
                for pr in range(NP):
                    QT.inc(g.dma_start(
                        out=QTs[:, pr * 512:(pr + 1) * 512],
                        in_=q_d[pr, :, 0:512]), 16)
        qt_marks = {0: QT.v}

        def plan_dv_p3(base):
            m = {}
            v = base
            for pr in range(NP):
                for kc in range(NKC):
                    v += 1
                    m[("ps", pr, kc)] = v
                if pr >= 1:
                    v += 1
                    m[("scl", pr - 1)] = v
                    v += 1
                    m[("ot", pr - 1)] = v
                v += 3
                m[("zch", pr)] = v
            v += 1
            m[("scl", 7)] = v
            v += 1
            m[("ot", 7)] = v
            for tt in range(4):
                v += 2
                m[("ev", tt)] = v
            m["end"] = v
            return m

        P3 = {}
        _dvc = DV.v
        _acc = AC.v
        _vcc = [c.v for c in VC]
        _sxc = [c.v for c in sX]
        _woc = [c.v for c in WO]
        _pac = [c.v for c in PA]
        _ouc = [c.v for c in OU]
        _qtc = QT.v
        for q in range(NQS):
            pl = {}
            pl["dvM"] = plan_dv_p3(_dvc)
            _dvc = pl["dvM"]["end"]
            pl["acB"] = _acc
            _acc += NP * NKC
            pl["mVC"] = {}
            for kc in range(NKC):
                _vcc[kc % 4] += 16
                pl["mVC"][kc] = _vcc[kc % 4]
            pl["mXq"] = {}
            for tt in range(4):
                _sxc[tt % 2] += 16
                pl["mXq"][tt] = _sxc[tt % 2]
            pl["mWOq"] = {}
            for j in range(2 * NP):
                _woc[j % 2] += 16
                pl["mWOq"][j] = _woc[j % 2]
            pl["mPAq"] = {}
            for tt in range(4):
                _pac[tt % 2] += 16
                pl["mPAq"][tt] = _pac[tt % 2]
            pl["mOUq"] = {}
            for tt in range(4):
                _ouc[tt % 2] += 16
                pl["mOUq"][tt] = _ouc[tt % 2]
            if q + 1 < NQS:
                _qtc += NP * 16
                pl["qt_next"] = _qtc
            P3[q] = pl
        P3[0]["qt_ready"] = qt_marks[0]
        for q in range(1, NQS):
            P3[q]["qt_ready"] = P3[q - 1]["qt_next"]

        teM = {}

        with (
            nc.psum_tensor("pSA", [128, 1024], F32) as pSA,
            nc.psum_tensor("pSB", [128, 1024], F32) as pSB,
            nc.psum_tensor("pOu0", [128, 512], F32) as pOu0,
            nc.psum_tensor("pOu1", [128, 512], F32) as pOu1,
            nc.psum_tensor("pOu2", [128, 512], F32) as pOu2,
            nc.psum_tensor("pzb", [128, 512], F32) as pzb,
            nc.Block() as blk,
        ):
            SA = [pSA, pSB]
            pOu = [pOu0, pOu1, pOu2]
            pz = pzb[0:2, :]
            pbc = pzb
            pes = [pSA, pSB]

            @blk.tensor
            def _(te):
                for q in range(NQS):
                    pl = P3[q]
                    dvM = pl["dvM"]
                    acB = pl["acB"]
                    dvM_prev = P3[q - 1]["dvM"] if q > 0 else None

                    if q == 0:
                        te.wait_ge(semDV, p2["dv"])
                    else:
                        te.wait_ge(semDV, dvM_prev[("ev", 3)])
                    te.wait_ge(semQT, pl["qt_ready"])

                    def emit_sc(u):
                        pr, kc = u // NKC, u % NKC
                        if u >= 2:
                            te.wait_ge(semAC, acB + u - 1)
                        sset = SA[u % 2]
                        te.matmul(
                            sset[:, 0:512],
                            KT[0:64, pr * T + kc * 128:pr * T + (kc + 1) * 128],
                            QTs[0:64, pr * 512:(pr + 1) * 512],
                            start=True, stop=True)
                        last = te.matmul(
                            sset[:, 512:1024],
                            KT[64:128, pr * T + kc * 128:
                               pr * T + (kc + 1) * 128],
                            QTs[64:128, pr * 512:(pr + 1) * 512],
                            start=True, stop=True, tile_position=(64, 0))
                        teM[(q, "sc", u)] = TE.inc(last)

                    def emit_pv(u):
                        pr, kc = u // NKC, u % NKC
                        te.wait_ge(semAC, acB + u + 1)
                        if pr == 0:
                            te.wait_ge(semVC[kc % 4], pl["mVC"][kc])
                        elif kc == 0:
                            for _j in range(4):
                                te.wait_ge(semVC[_j], pl["mVC"][28 + _j])
                        if kc == 0:
                            if pr >= 3:
                                te.wait_ge(semDV, dvM[("ot", pr - 3)])
                            elif q > 0:
                                te.wait_ge(semDV,
                                           dvM_prev[("ot", {0: 6, 1: 7,
                                                            2: 5}[pr])])
                        pt = PT[u % 4]
                        po = pOu[pr % 3]
                        st, sp = (kc == 0), (kc == NKC - 1)
                        te.matmul(po[0:64, :],
                                  VQ[:, kc * 1024 + pr * 128:
                                     kc * 1024 + pr * 128 + 64],
                                  pt[:, 0:512], start=st, stop=sp,
                                  skip_group_check=True)
                        last = te.matmul(
                            po[64:128, :],
                            VQ[:, kc * 1024 + pr * 128 + 64:
                               kc * 1024 + (pr + 1) * 128],
                            pt[:, 512:1024], start=st, stop=sp,
                            tile_position=(0, 64), skip_group_check=True)
                        teM[(q, "pv", u)] = TE.inc(last)

                    def emit_zmm(pr):
                        te.wait_ge(semDV, dvM[("ps", pr, NKC - 1)])
                        if pr >= 1:
                            te.wait_ge(semDV, dvM[("scl", pr - 1)])
                        elif q > 0:
                            te.wait_ge(semDV, dvM_prev[("scl", 7)])
                        te.matmul(pz, zsel[:, 0:2], PS[0][:, 0:512],
                                  start=True, stop=False)
                        te.matmul(pz, zsel[:, 1:3], PS[0][:, 512:1024],
                                  start=False, stop=False)
                        te.matmul(pz, zsel[:, 0:2], PS[1][:, 0:512],
                                  start=False, stop=False)
                        last = te.matmul(pz, zsel[:, 1:3],
                                         PS[1][:, 512:1024],
                                         start=False, stop=True)
                        teM[(q, "zmm", pr)] = TE.inc(last)

                    def emit_bc(pr):
                        te.wait_ge(semDV, dvM[("zch", pr)])
                        if pr >= 1:
                            te.wait_ge(semDV, dvM[("scl", pr - 1)])
                        elif q > 0:
                            te.wait_ge(semDV, dvM_prev[("scl", 7)])
                        teM[(q, "bc", pr)] = TE.inc(
                            te.matmul(pbc[:], e2, zpad, start=True, stop=True,
                                      skip_group_check=True))

                    for u in range(NP * NKC):
                        emit_sc(u)
                        if u >= 2:
                            emit_pv(u - 2)
                            if (u - 2) % NKC == NKC - 1:
                                prd = (u - 2) // NKC
                                if prd >= 1:
                                    emit_bc(prd - 1)
                                emit_zmm(prd)
                    emit_pv(NP * NKC - 2)
                    emit_pv(NP * NKC - 1)
                    emit_bc(NP - 2)
                    emit_zmm(NP - 1)
                    emit_bc(NP - 1)

                    for i in range(2 * NP):
                        wave, pc = i // NP, i % NP
                        if i == 0:
                            te.wait_ge(semAC, acB + 256)
                            te.wait_ge(semDV, dvM[("ot", 7)])
                        if i == NP:
                            te.wait_ge(semDV, dvM[("ev", 1)])
                        te.wait_ge(semWO[i % 2], pl["mWOq"][i])
                        ws = wo_sb[:, (i % 2) * 1024:(i % 2) * 1024 + 1024]
                        last = None
                        for w2 in range(2):
                            tt = wave * 2 + w2
                            lh = OTr[:, pc * 512 + tt * 128:
                                     pc * 512 + (tt + 1) * 128]
                            st, sp = (pc == 0), (pc == NP - 1)
                            te.matmul(pes[w2][:, 0:512], lh, ws[:, 0:512],
                                      start=st, stop=sp)
                            last = te.matmul(pes[w2][:, 512:1024], lh,
                                             ws[:, 512:1024], start=st, stop=sp)
                        teM[(q, "p4g", i)] = TE.inc(last)

            @blk.scalar
            def _(sc):
                for q in range(NQS):
                    pl = P3[q]
                    acB = pl["acB"]
                    dvM = pl["dvM"]
                    dvM_prev = P3[q - 1]["dvM"] if q > 0 else None
                    for u in range(NP * NKC):
                        sc.wait_ge(semTE, teM[(q, "sc", u)])
                        if u >= 4:
                            sc.wait_ge(semTE, teM[(q, "pv", u - 4)])
                            sc.wait_ge(semDV,
                                       dvM[("ps", (u - 4) // NKC,
                                            (u - 4) % NKC)])
                        elif q > 0:
                            sc.wait_ge(semTE, teM[(q - 1, "pv", 252 + u)])
                            sc.wait_ge(semDV, dvM_prev[("ps", 7, 28 + u)])
                        AC.inc(sc.activation(PT[u % 4], SA[u % 2][:],
                                             AF.Exp, bias=lnbias, scale=0.125))
                    assert AC.v == acB + 256

            @blk.vector
            def _(v):
                for q in range(NQS):
                    pl = P3[q]
                    acB = pl["acB"]
                    dvM = pl["dvM"]
                    for pr in range(NP):
                        for kc in range(NKC):
                            u = pr * NKC + kc
                            v.wait_ge(semAC, acB + u + 1)
                            if kc <= 1:
                                if pr >= 1:
                                    v.wait_ge(semTE, teM[(q, "zmm", pr - 1)])
                                elif q > 0:
                                    v.wait_ge(semTE, teM[(q - 1, "zmm", 7)])
                            v.wait_ge(semDV, DV.v)
                            if kc <= 1:
                                DV.inc(v.tensor_copy(PS[kc % 2], PT[u % 4]))
                            else:
                                DV.inc(v.tensor_add(PS[kc % 2], PS[kc % 2],
                                                    PT[u % 4]))
                        if pr >= 1:
                            v.wait_ge(semTE, teM[(q, "bc", pr - 1)])
                            v.wait_ge(semDV, DV.v)
                            DV.inc(v.tensor_copy(sclt, pbc[:]))
                            v.wait_ge(semDV, DV.v)
                            v.wait_ge(semTE, teM[(q, "pv", (pr - 1) * NKC + 31)])
                            DV.inc(v.tensor_mul(
                                OTr[:, (pr - 1) * 512:pr * 512],
                                pOu[(pr - 1) % 3][:], sclt))
                        v.wait_ge(semTE, teM[(q, "zmm", pr)])
                        v.wait_ge(semDV, DV.v)
                        DV.inc(v.tensor_copy(zf32, pz))
                        v.wait_ge(semDV, DV.v)
                        DV.inc(v.reciprocal(rcpt, zf32))
                        v.wait_ge(semDV, DV.v)
                        if pr >= 1:
                            v.wait_ge(semTE, teM[(q, "bc", pr - 1)])
                        elif q > 0:
                            v.wait_ge(semTE, teM[(q - 1, "bc", 7)])
                        DV.inc(v.tensor_copy(zpad[0:2, :], rcpt))
                        assert DV.v == dvM[("zch", pr)]
                    v.wait_ge(semTE, teM[(q, "bc", 7)])
                    v.wait_ge(semDV, DV.v)
                    DV.inc(v.tensor_copy(sclt, pbc[:]))
                    v.wait_ge(semDV, DV.v)
                    v.wait_ge(semTE, teM[(q, "pv", 7 * NKC + 31)])
                    DV.inc(v.tensor_mul(OTr[:, 7 * 512:8 * 512],
                                        pOu[7 % 3][:], sclt))
                    for tt in range(4):
                        v.wait_ge(semTE, teM[(q, "p4g", 7 if tt < 2 else 15)])
                        v.wait_ge(semPA[tt % 2], pl["mPAq"][tt])
                        v.wait_ge(semDV, DV.v)
                        DV.inc(v.tensor_add(xsq, pes[tt % 2][:], pstg[tt % 2]))
                        v.wait_ge(semX[tt % 2], pl["mXq"][tt])
                        v.wait_ge(semDV, DV.v)
                        DV.inc(v.tensor_add(pstg[tt % 2], xsq,
                                            x0 if tt % 2 == 0 else x1))
                    assert DV.v == dvM[("ev", 3)]

            @blk.sync
            def _(s):
                for q in range(NQS):
                    pl = P3[q]
                    dvM = pl["dvM"]
                    dvM_prev = P3[q - 1]["dvM"] if q > 0 else None
                    for kc in range(NKC):
                        if kc == 0:
                            if q == 0:
                                s.wait_ge(semTE, p2["te"])
                                s.wait_ge(semVS0, vs_final[0])
                                s.wait_ge(semVS1, vs_final[1])
                            else:
                                s.wait_ge(semTE, teM[(q - 1, "pv", 255)])
                        if kc >= 4:
                            s.wait_ge(semTE, teM[(q, "pv", kc - 4)])
                        VC[kc % 4].inc(s.dma_start(
                            out=VQ[:, kc * 1024:(kc + 1) * 1024],
                            in_=v_d[kc * 128:(kc + 1) * 128, :]), 16)
                        assert VC[kc % 4].v == pl["mVC"][kc]
                    for tt in range(4):
                        if tt >= 2:
                            s.wait_ge(semDV, dvM[("ev", tt - 2)])
                        elif q > 0:
                            s.wait_ge(semDV, dvM_prev[("ev", tt + 2)])
                        sX[tt % 2].inc(s.dma_start(
                            out=(x0 if tt % 2 == 0 else x1),
                            in_=x_d[q * QS + tt * 128:
                                    q * QS + (tt + 1) * 128, :]), 16)
                        assert sX[tt % 2].v == pl["mXq"][tt]
                    for tt in range(4):
                        s.wait_ge(semDV, dvM[("ev", tt)])
                        OU[tt % 2].inc(s.dma_start(
                            out=out_d[q * QS + tt * 128:
                                      q * QS + (tt + 1) * 128, :],
                            in_=pstg[tt % 2]), 16)
                        assert OU[tt % 2].v == pl["mOUq"][tt]

            @blk.gpsimd
            def _(g):
                for q in range(NQS):
                    pl = P3[q]
                    for tt in range(2):
                        if q == 0:
                            g.wait_ge(semPA[tt % 2], pa_1b_final[tt % 2])
                        else:
                            g.wait_ge(semOU[tt % 2], P3[q - 1]["mOUq"][tt + 2])
                        PA[tt % 2].inc(g.dma_start(
                            out=pstg[tt % 2],
                            in_=part_d[q * QS + tt * 128:
                                       q * QS + (tt + 1) * 128, :]), 16)
                        assert PA[tt % 2].v == pl["mPAq"][tt]
                    if q + 1 < NQS:
                        g.wait_ge(semTE, teM[(q, "sc", 255)])
                        for pr in range(NP):
                            QT.inc(g.dma_start(
                                out=QTs[:, pr * 512:(pr + 1) * 512],
                                in_=q_d[pr, :, (q + 1) * 512:(q + 2) * 512]),
                                16)
                        assert QT.v == pl["qt_next"]
                    for i in range(2 * NP):
                        if i >= 2:
                            g.wait_ge(semTE, teM[(q, "p4g", i - 2)])
                        elif q > 0:
                            g.wait_ge(semTE, teM[(q - 1, "p4g", 14 + i)])
                        WO[i % 2].inc(g.dma_start(
                            out=wo_sb[:, (i % 2) * 1024:(i % 2) * 1024 + 1024],
                            in_=woT_d[(i % NP) * 128:(i % NP + 1) * 128, :]),
                            16)
                        assert WO[i % 2].v == pl["mWOq"][i]
                    for tt in range(2, 4):
                        g.wait_ge(semOU[tt % 2], pl["mOUq"][tt - 2])
                        PA[tt % 2].inc(g.dma_start(
                            out=pstg[tt % 2],
                            in_=part_d[q * QS + tt * 128:
                                       q * QS + (tt + 1) * 128, :]), 16)
                        assert PA[tt % 2].v == pl["mPAq"][tt]

        assert DV.v == P3[NQS - 1]["dvM"]["end"]

        _final(nc, [(semOU0, OU[0].v), (semOU1, OU[1].v),
                    (semVC0, VC[0].v), (semVC1, VC[1].v),
                    (semVC2, VC[2].v), (semVC3, VC[3].v),
                    (semPA0, PA[0].v), (semPA1, PA[1].v),
                    (semQT, QT.v)])
    return nc


def _host_constants():
    j = np.arange(D, dtype=np.float64)
    ang = 2.0 * np.pi * np.outer(j, j) / D
    cmat = np.cos(ang).astype(np.float32).astype(np.float16)
    smat = np.sin(ang).astype(np.float32).astype(np.float16)
    ident = np.eye(128, dtype=np.float16)
    e2 = np.zeros((128, 128), np.float16)
    e2[0, 0:64] = 1.0
    e2[1, 64:128] = 1.0
    eb = np.zeros((128, 128), np.float16)
    eb[0, :] = 1.0
    zsel = np.zeros((128, 4), np.float16)
    zsel[:, 0] = 1.0
    zsel[:, 2] = 1.0
    return cmat, smat, ident, e2, eb, zsel


def kernel(L0_attn_hidden_states, L0_attn_Wq, L0_attn_bq, L0_attn_Wk,
           L0_attn_bk, L0_attn_Wv, L0_attn_bv, L0_attn_Wo, L0_attn_bo,
           L0_ffn_w_freq_real, L0_ffn_w_freq_imag, L0_ffn_bias):
    x = np.ascontiguousarray(np.asarray(L0_attn_hidden_states, np.float32))
    cmat, smat, ident, e2, eb, zsel = _host_constants()
    bqkT = np.ascontiguousarray(np.concatenate(
        [np.asarray(L0_attn_bq, np.float32).reshape(8, 128).T,
         np.asarray(L0_attn_bk, np.float32).reshape(8, 128).T], axis=1))
    shared = {
        "wqT": np.ascontiguousarray(np.asarray(L0_attn_Wq, np.float32).T),
        "wkT": np.ascontiguousarray(np.asarray(L0_attn_Wk, np.float32).T),
        "wvT": np.ascontiguousarray(np.asarray(L0_attn_Wv, np.float32).T),
        "woT": np.ascontiguousarray(np.asarray(L0_attn_Wo, np.float32).T),
        "bqkT": bqkT,
        "bo_row": np.asarray(L0_attn_bo, np.float32).reshape(1, D),
        "fb_row": np.asarray(L0_ffn_bias, np.float32).reshape(1, D),
        "bv_row": np.asarray(L0_attn_bv, np.float32).reshape(1, D),
        "wr": np.ascontiguousarray(
            np.asarray(L0_ffn_w_freq_real, np.float32)
            .reshape(8, 128, 8, 128).transpose(1, 2, 0, 3)),
        "wi": np.ascontiguousarray(
            np.asarray(L0_ffn_w_freq_imag, np.float32)
            .reshape(8, 128, 8, 128).transpose(1, 2, 0, 3)),
        "cmat": cmat, "smat": smat,
        "ident": ident, "e2": e2, "eb": eb, "zsel": zsel,
    }
    in_maps = []
    for b in range(8):
        m = dict(shared)
        m["x"] = np.ascontiguousarray(x[b])
        in_maps.append(m)
    nc = build_nc()
    res = run_bass_kernel_spmd(nc, in_maps, core_ids=list(range(8)))
    out = np.stack([res.results[b]["out"] for b in range(8)], axis=0)
    return out.astype(np.float32)
